# version 37
# baseline (speedup 1.0000x reference)
import contextlib
import numpy as np
import concourse.bass as bass
import concourse.mybir as mybir
from concourse.bass_utils import run_bass_kernel_spmd
from concourse.alu_op_type import AluOpType as OP

F32 = mybir.dt.float32
BF16 = mybir.dt.bfloat16
AF = mybir.ActivationFunctionType
AX = mybir.AxisListType

T = 2048
C = 1024
NCH = 8
NJ = 4
NT = 512
EPS = 1e-5
INV_C = 1.0 / C

ENGINES = ("sync", "gpsimd", "vector", "scalar", "tensor")


class Builder:
    def __init__(self, nc, sems):
        self.nc = nc
        self.sems = sems
        self.cnt = {k: 0 for k in sems}
        self.ops = {e: [] for e in ENGINES}

    def on(self, eng, fn, inc=None):
        sem = self.sems[inc] if inc else None
        if inc:
            self.cnt[inc] += 1

        def run(e, fn=fn, sem=sem):
            inst = fn(e)
            if sem is not None:
                inst.then_inc(sem, 1)
        self.ops[eng].append(run)
        return self.cnt[inc] if inc else None

    def dma(self, eng, out, in_, inc, slow=False):
        self.cnt[inc] += 16
        sem = self.sems[inc]

        def run(e, out=out, in_=in_, sem=sem, slow=slow):
            kw = dict(allow_slow_non_contiguous=True) if slow else {}
            e.dma_start(out=out, in_=in_, **kw).then_inc(sem, 16)
        self.ops[eng].append(run)
        return self.cnt[inc]

    def mark(self, sem):
        return self.cnt[sem]

    def wait(self, eng, sem, val=None):
        if val is None:
            val = self.cnt[sem]
        s = self.sems[sem]

        def run(e, s=s, val=val):
            e.wait_ge(s, val)
        self.ops[eng].append(run)


def build_nc(debug=False):
    nc = bass.Bass("TRN2", target_bir_lowering=False)

    xT = nc.dram_tensor("xT", [C, T], F32, kind="ExternalInput").ap()
    w_qkv = nc.dram_tensor("w_qkv", [C, 384], F32, kind="ExternalInput").ap()
    w_out = nc.dram_tensor("w_out", [128, C], F32, kind="ExternalInput").ap()
    WrT = nc.dram_tensor("WrT", [C, C], F32, kind="ExternalInput").ap()
    WkT = nc.dram_tensor("WkT", [C, C], F32, kind="ExternalInput").ap()
    WvT = nc.dram_tensor("WvT", [C, C], F32, kind="ExternalInput").ap()
    vgamma = nc.dram_tensor("vgamma", [C], F32, kind="ExternalInput").ap()
    vbeta = nc.dram_tensor("vbeta", [C], F32, kind="ExternalInput").ap()
    vbout = nc.dram_tensor("vbout", [C], F32, kind="ExternalInput").ap()
    vmur = nc.dram_tensor("vmur", [C], F32, kind="ExternalInput").ap()
    vmuk = nc.dram_tensor("vmuk", [C], F32, kind="ExternalInput").ap()
    yT = nc.dram_tensor("yT", [C, T], F32, kind="ExternalOutput").ap()

    dbg = {}
    if debug:
        for nm, shp, dt in [("d_x2", [C, T], BF16), ("d_w", [C, T], BF16),
                            ("d_or", [128, NCH * NT], BF16),
                            ("d_xr", [128, T], BF16), ("d_xk", [128, T], BF16),
                            ("d_xk2", [128, T], BF16), ("d_rr", [128, T], BF16),
                            ("d_ka", [128, T], BF16)]:
            dbg[nm] = nc.dram_tensor(nm, shp, dt, kind="ExternalOutput").ap()

    with contextlib.ExitStack() as ST:
        E_ = ST.enter_context

        x_sb = [E_(nc.sbuf_tensor(f"x{c}", [128, T], BF16)) for c in range(NCH)]
        x2_sb = [E_(nc.sbuf_tensor(f"x2_{c}", [128, 2 * NT], BF16)) for c in range(NCH)]
        q_sb = E_(nc.sbuf_tensor("q", [128, T], BF16))
        k_sb = E_(nc.sbuf_tensor("k", [128, T], BF16))
        v_sb = E_(nc.sbuf_tensor("v", [128, T], BF16))
        Eb = E_(nc.sbuf_tensor("Eb", [128, T], BF16))
        Etm = E_(nc.sbuf_tensor("Etm", [128, T], BF16))
        vtm = E_(nc.sbuf_tensor("vtm", [128, T], BF16))
        outh = E_(nc.sbuf_tensor("outh", [128, T], BF16))
        t5r = E_(nc.sbuf_tensor("t5r", [128, T], BF16))
        Wr_sb = E_(nc.sbuf_tensor("Wr", [128, NCH * C], BF16))
        Wk_sb = E_(nc.sbuf_tensor("Wk", [128, NCH * C], BF16))
        Wv_sb = E_(nc.sbuf_tensor("Wv", [128, NCH * C], BF16))
        woutb = E_(nc.sbuf_tensor("woutb", [128, C], BF16))
        attr = E_(nc.sbuf_tensor("attr", [128, NCH * NT], BF16))
        oring = E_(nc.sbuf_tensor("oring", [128, NCH * NT], BF16))
        sqr = E_(nc.sbuf_tensor("sqr", [128, 3 * NT], BF16))
        t3s = E_(nc.sbuf_tensor("t3s", [128, 2 * NT], BF16))
        rrB = E_(nc.sbuf_tensor("rrB", [128, 4 * NT], BF16))

        def _alias(name, backing):
            t_ = nc.virtual_sbuf_tensor(name, [128, T], BF16)
            nc.back_virtual_tensor_with(t_, backing)
            return t_
        w_sb = [_alias(f"w{c}", x_sb[c]) for c in range(NCH)]
        xrA = _alias("xrA", k_sb)
        xrB = _alias("xrB", Eb)
        xkA = _alias("xkA", v_sb)
        xkB = _alias("xkB", vtm)
        rrA = _alias("rrA", Etm)
        kaA = _alias("kaA", outh)
        kaB = _alias("kaB", t5r)

        def _seg(tA, tB, c):
            t_ = tA if c < 4 else tB
            return t_[:, NT * (c % 4):NT * (c % 4 + 1)]

        def x2_(c, j):
            return x2_sb[c][:, NT * (j % 2):NT * (j % 2 + 1)]

        def xr_(c):
            return _seg(xrA, xrB, c)

        def xk_(c):
            return _seg(xkA, xkB, c)

        def rr_(c):
            if c < 4:
                return rrA[:, NT * c:NT * (c + 1)]
            return rrB[:, NT * (c - 4):NT * (c - 3)]

        def ka_(c):
            return _seg(kaA, kaB, c)

        rsrow = E_(nc.sbuf_tensor("rsrow", [1, T], BF16))
        rowM4 = E_(nc.sbuf_tensor("rowM4", [1, T], F32))
        rowQ4 = E_(nc.sbuf_tensor("rowQ4", [1, T], F32))
        wT0 = E_(nc.sbuf_tensor("wT0", [1, T], F32))
        rowRw = E_(nc.sbuf_tensor("rowRw", [1, T], F32))
        mrs1row = E_(nc.sbuf_tensor("mrs1row", [1, T], BF16))
        p2row = E_(nc.sbuf_tensor("p2row", [1, T], BF16))
        s4hi = p2row

        outr = E_(nc.sbuf_tensor("outr", [128, 2 * NT], F32))
        TRT = E_(nc.sbuf_tensor("TRT", [128, 2 * NT], BF16))
        trtb4 = E_(nc.sbuf_tensor("trtb4", [128, NT], BF16))

        def trt(i):
            return TRT[:, NT * i:NT * (i + 1)]

        gcol = E_(nc.sbuf_tensor("gcol", [128, 8], F32))
        bcol = E_(nc.sbuf_tensor("bcol", [128, 8], F32))
        bocol = E_(nc.sbuf_tensor("bocol", [128, 8], F32))
        murcol = E_(nc.sbuf_tensor("murcol", [128, 8], F32))
        mukcol = E_(nc.sbuf_tensor("mukcol", [128, 8], F32))
        bcolb2 = E_(nc.sbuf_tensor("bcolb2", [128, 16], BF16, align_bytes=4))
        onescol = E_(nc.sbuf_tensor("onescol", [128, 1], BF16, align_bytes=4))
        invccol = E_(nc.sbuf_tensor("invccol", [128, 1], BF16, align_bytes=4))
        ones1 = E_(nc.sbuf_tensor("ones1", [1, 128], BF16, align_bytes=4))
        bocolb = E_(nc.sbuf_tensor("bocolb", [128, 8], BF16, align_bytes=4))
        cb_sb = E_(nc.sbuf_tensor("cb_sb", [128, 128], BF16))
        idb = nc.virtual_sbuf_tensor("idb", [128, 128], BF16)
        nc.back_virtual_tensor_with(idb, cb_sb)
        io_c = E_(nc.sbuf_tensor("io_c", [128, 128], BF16))
        maskb = nc.virtual_sbuf_tensor("maskb", [128, 128], BF16)
        nc.back_virtual_tensor_with(maskb, io_c)
        io_r = E_(nc.sbuf_tensor("io_r", [128, 1], F32, align_bytes=4))
        rspr = E_(nc.sbuf_tensor("rspr", [128, 1], F32, align_bytes=4))
        kmax = E_(nc.sbuf_tensor("kmax", [128, 1], F32, align_bytes=4))
        ssum = E_(nc.sbuf_tensor("ssum", [128, 1], F32, align_bytes=4))
        rwcol = E_(nc.sbuf_tensor("rwcol", [128, 1], BF16, align_bytes=4))
        rwf = E_(nc.sbuf_tensor("rwf", [128, 1], F32, align_bytes=4))
        bbar = E_(nc.sbuf_tensor("bbar", [1, 1], F32, align_bytes=4))
        epsc = E_(nc.sbuf_tensor("epsc", [1, 1], F32, align_bytes=4))
        pb8 = E_(nc.sbuf_tensor("pb8", [1, 8], F32, align_bytes=4))

        pQ = E_(nc.psum_tensor("pQ", [128, NT], F32))
        pK = E_(nc.psum_tensor("pK", [128, NT], F32))
        pV = E_(nc.psum_tensor("pV", [128, NT], F32))
        pSm = E_(nc.psum_tensor("pSm", [1, NT], F32))
        pSq = E_(nc.psum_tensor("pSq", [1, NT], F32))
        pB = E_(nc.psum_tensor("pB", [128, NT], F32))
        pAux = E_(nc.psum_tensor("pAux", [128, NT], F32))
        pT = E_(nc.psum_tensor("pT", [128, 1024], BF16))

        sems = {}
        for s in ("dX", "dW", "dM", "dO", "sv", "ss", "st", "sg"):
            sems[s] = E_(nc.semaphore(s))
        B = Builder(nc, sems)

        for c in range(NCH):
            B.dma("gpsimd", x_sb[c][:], xT[128 * c:128 * c + 128, :], "dX")
        mark_x = B.mark("dX")
        for dst, src in ((gcol, vgamma), (bcol, vbeta), (bocol, vbout),
                         (murcol, vmur), (mukcol, vmuk)):
            B.dma("sync", dst[:], src.rearrange("(a p) -> p a", p=128), "dM",
                  slow=True)
        mark_vecs = B.mark("dM")
        B.dma("gpsimd", Wr_sb[:, 0:NCH * 384].rearrange("p (a k) -> p a k", k=384),
              w_qkv.rearrange("(a p) k -> p a k", p=128), "dW")
        B.dma("gpsimd", woutb[:], w_out, "dW")
        mark_wqkv = B.mark("dW")
        B.dma("gpsimd", Wv_sb[:].rearrange("p (a k) -> p a k", k=C),
              WvT.rearrange("(a p) k -> p a k", p=128), "dW")

        B.on("gpsimd", lambda g: g.iota(io_c[:], pattern=[[1, 128]], base=0,
                                        channel_multiplier=0,
                                        allow_small_or_imprecise_dtypes=True))
        mark_iota = B.on("gpsimd", lambda g: g.iota(
            io_r[:], pattern=[[0, 1]], base=0, channel_multiplier=1,
            allow_small_or_imprecise_dtypes=True), inc="sg")

        B.on("vector", lambda v: v.memset(onescol[:], 1.0))
        B.on("vector", lambda v: v.memset(invccol[:], INV_C))
        B.on("vector", lambda v: v.memset(ones1[:], 1.0))
        B.on("vector", lambda v: v.memset(epsc[:], EPS))
        B.on("vector", lambda v: v.memset(bcolb2[:], 0.0))
        B.wait("vector", "sg", mark_iota)
        mark_const = B.on("vector", lambda v: v.tensor_scalar(
            idb[:], io_c[:], io_r[:], None, OP.is_equal), inc="sv")
        B.on("vector", lambda v: v.memset(maskb[:], 0.0))
        for h in range(4):
            B.on("vector", lambda v, h=h: v.memset(
                maskb[32 * h:32 * h + 32, 32 * h:32 * h + 32], 32.0 ** -0.5))

        B.wait("vector", "dW", mark_wqkv)
        B.wait("vector", "dM", mark_vecs)
        for c in range(NCH):
            B.on("vector", lambda v, c=c: v.tensor_scalar(
                Wk_sb[:, 384 * c:384 * (c + 1)], Wr_sb[:, 384 * c:384 * (c + 1)],
                gcol[:, c:c + 1], None, OP.mult), inc="sv")
        B.on("vector", lambda v: v.tensor_copy(
            bcolb2[:].rearrange("p (a two) -> p a two", two=2)[:, :, 0:1],
            bcol[:].rearrange("p (a one) -> p a one", one=1)), inc="sv")
        mk_rwf = B.on("vector", lambda v: v.reduce_sum(rwf[:], woutb[:],
                                                       axis=AX.X), inc="sv")
        B.on("vector", lambda v: v.tensor_copy(bocolb[:], bocol[:]), inc="sv")
        B.wait("scalar", "sv", mk_rwf)
        B.on("scalar", lambda s: s.mul(rwcol[:], rwf[:], INV_C), inc="ss")
        mark_prep = B.mark("sv")
        mark_prep_s = B.mark("ss")

        B.wait("tensor", "sv", mark_prep)
        mark_bqcw = None
        for c in range(NCH):
            B.on("tensor", lambda t, c=c: t.matmul(
                pSm[0:1, 0:384], bcolb2[:, 2 * c:2 * c + 1],
                Wr_sb[:, 384 * c:384 * (c + 1)], start=(c == 0),
                stop=(c == NCH - 1)))
            mark_bqcw = B.on("tensor", lambda t, c=c: t.matmul(
                pSq[0:1, 0:384], onescol[:], Wk_sb[:, 384 * c:384 * (c + 1)],
                start=(c == 0), stop=(c == NCH - 1)),
                inc="st" if c == NCH - 1 else None) or mark_bqcw
        B.wait("vector", "st", mark_bqcw)
        B.on("vector", lambda v: v.tensor_copy(attr[0:1, 0:384], pSm[0:1, 0:384]),
             inc="sv")
        B.on("vector", lambda v: v.tensor_copy(attr[0:1, 384:768], pSq[0:1, 0:384]),
             inc="sv")
        mark_cwbq = B.mark("sv")
        B.wait("tensor", "sv", mark_cwbq)
        mark_bo = B.on("tensor", lambda t: t.matmul(
            pSm[0:1, 0:8], onescol[:], bocolb[:], start=True, stop=True),
            inc="st")
        B.wait("vector", "st", mark_bo)
        mk_pb8 = B.on("vector", lambda v: v.reduce_sum(pb8[:, 0:1],
                                                       pSm[0:1, 0:8],
                                                       axis=AX.X), inc="sv")
        B.wait("scalar", "sv", mk_pb8)
        B.on("scalar", lambda s: s.mul(bbar[:], pb8[:, 0:1], INV_C), inc="ss")
        B.wait("gpsimd", "st", mark_bo)
        B.wait("gpsimd", "sv", mark_prep)
        B.dma("gpsimd", Wr_sb[:].rearrange("p (a k) -> p a k", k=C),
              WrT.rearrange("(a p) k -> p a k", p=128), "dW")

        sq_used = {}
        ob_used = sq_used

        def sq_slot(c):
            return sqr[:, NT * (c % 3):NT * (c % 3 + 1)]

        def ob_slot(c):
            return sq_slot(c)

        def rmj(j):
            return rowM4[0:1, 0:NT]

        def rqj(j):
            return rowQ4[0:1, 0:NT]

        def chain_core(j=0):
            B.wait("scalar", "sv", B.mark("sv"))
            B.on("scalar", lambda s, j=j: s.activation(wT0[0:1, 0:NT], rmj(j),
                                                       AF.Square), inc="ss")
            B.wait("vector", "ss", B.mark("ss"))
            mkv = B.on("vector", lambda v, j=j: v.tensor_tensor(
                rowM4[0:1, NT:2 * NT], rqj(j), wT0[0:1, 0:NT], OP.subtract), inc="sv")
            B.wait("scalar", "sv", mkv)
            B.on("scalar", lambda s: s.activation(
                wT0[0:1, 0:NT], rowM4[0:1, NT:2 * NT], AF.Ln, bias=epsc[:], scale=1.0), inc="ss")
            mk = B.on("scalar", lambda s: s.activation(
                rowRw[0:1, 0:NT], wT0[0:1, 0:NT], AF.Exp, scale=-0.5), inc="ss")
            B.wait("vector", "ss", mk)
            return mk

        def stats_evac_s(j=0):
            B.wait("scalar", "sv", B.mark("sv"))
            B.on("scalar", lambda s, j=j: s.copy(rmj(j), pSm[:]))
            return B.on("scalar", lambda s, j=j: s.copy(rqj(j), pSq[:]),
                        inc="ss")

        def stats_evac(j=0):
            B.wait("vector", "ss", B.mark("ss"))
            B.on("vector", lambda v, j=j: v.tensor_copy(rmj(j), pSm[:]),
                 inc="sv")
            return B.on("vector", lambda v, j=j: v.tensor_copy(rqj(j), pSq[:]),
                        inc="sv")

        def bcast(src_row, dst_tile, negate=False):
            mk = B.on("vector", lambda v: v.tensor_copy(rsrow[0:1, 0:NT], src_row),
                      inc="sv")
            B.wait("tensor", "sv", mk)
            mk2 = B.on("tensor", lambda t: t.matmul(pB[:], ones1[:],
                                                    rsrow[0:1, 0:NT],
                                                    start=True, stop=True),
                       inc="st")
            B.wait("vector", "st", mk2)
            return B.on("vector", lambda v, dst=dst_tile: v.tensor_copy(
                dst, pB[:]), inc="sv")

        mark_setup_ss = B.mark("ss")
        B.wait("vector", "dX", mark_x)
        B.wait("scalar", "dX", mark_x)
        B.wait("tensor", "dX", mark_x)
        B.wait("tensor", "sv", mark_cwbq)
        B.wait("tensor", "ss", mark_setup_ss)
        B.wait("vector", "ss", mark_setup_ss)
        qkv_out = (q_sb, k_sb, v_sb)
        evac_marks = {}
        stA_evac = {}

        def stats_j(j):
            sl = slice(NT * j, NT * (j + 1))
            if j == 0:
                B.wait("tensor", "sv", mk_pb8)
            else:
                B.wait("tensor", "ss", stA_evac[j - 1])
            mkm = None
            for c in range(NCH):
                mkm = B.on("tensor", lambda t, c=c, sl=sl: t.matmul(
                    pSm[:], invccol[:], x_sb[c][:, sl],
                    start=(c == 0), stop=(c == NCH - 1)),
                    inc="st" if c == NCH - 1 else None) or mkm
            mks = None
            for c in range(NCH):
                if (c % 3) in sq_used:
                    B.wait("scalar", *sq_used[c % 3])
                mkv = B.on("scalar", lambda s, c=c, sl=sl: s.activation(
                    sq_slot(c), x_sb[c][:, sl], AF.Square), inc="ss")
                B.wait("tensor", "ss", mkv)
                mks = B.on("tensor", lambda t, c=c: t.matmul(
                    pSq[:], invccol[:], sq_slot(c),
                    start=(c == 0), stop=(c == NCH - 1)), inc="st")
                sq_used[c % 3] = ("st", mks)
            B.wait("scalar", "st", mks)
            B.on("scalar", lambda s, sl=sl: s.copy(rowM4[0:1, sl], pSm[:]))
            stA_evac[j] = B.on("scalar", lambda s, sl=sl: s.copy(
                rowQ4[0:1, sl], pSq[:]), inc="ss")

        def chain_wide():
            mk0 = B.on("scalar", lambda s: s.activation(
                wT0[:], rowM4[:], AF.Square), inc="ss")
            B.wait("vector", "ss", mk0)
            mkv = B.on("vector", lambda v: v.tensor_tensor(
                rowQ4[:], rowQ4[:], wT0[:], OP.subtract), inc="sv")
            B.wait("scalar", "sv", mkv)
            B.on("scalar", lambda s: s.activation(
                wT0[:], rowQ4[:], AF.Ln, bias=epsc[:], scale=1.0), inc="ss")
            mke = B.on("scalar", lambda s: s.activation(
                rowRw[:], wT0[:], AF.Exp, scale=-0.5), inc="ss")
            B.wait("vector", "ss", mke)
            B.on("vector", lambda v: v.tensor_tensor(
                mrs1row[0:1, :], rowM4[:], rowRw[:], OP.mult), inc="sv")
            B.on("vector", lambda v: v.tensor_scalar(
                oring[0:1, 0:T], rowM4[:], -1.0, None, OP.mult), inc="sv")
            mku = B.on("vector", lambda v: v.tensor_scalar(
                rowM4[:], rowQ4[:], 1.0 + EPS, EPS * EPS, OP.mult, OP.add),
                inc="sv")
            B.wait("scalar", "sv", mku)
            B.on("scalar", lambda s: s.activation(
                wT0[:], rowM4[:], AF.Ln), inc="ss")
            mk_s4 = B.on("scalar", lambda s: s.activation(
                s4hi[0:1, :], wT0[:], AF.Exp, scale=-0.5), inc="ss")
            mk_rows = B.on("scalar", lambda s: s.activation(
                rsrow[0:1, :], wT0[:], AF.Exp, scale=0.5), inc="ss")
            return mk_s4, mk_rows

        def Gbig(j):
            sl = slice(NT * j, NT * (j + 1))
            for oi, psum in enumerate((pQ, pK, pV)):
                osl = slice(128 * oi, 128 * (oi + 1))
                if (j - 1, oi) in evac_marks:
                    B.wait("tensor", "sv", evac_marks[(j - 1, oi)])
                for c in range(NCH):
                    B.on("tensor", lambda t, c=c, osl=osl, sl=sl, psum=psum:
                         t.matmul(psum[:],
                                  Wk_sb[:, 384 * c + osl.start:384 * c + osl.stop],
                                  x_sb[c][:, sl], start=(c == 0), stop=False))

        def sbcast(j, mk_s4=None):
            if mk_s4 is not None:
                B.wait("tensor", "ss", mk_s4)
            if last_bcastA[0] is not None:
                B.wait("tensor", "sv", last_bcastA[0])
            mkb = B.on("tensor", lambda t, j=j: t.matmul(
                pB[:], ones1[:], s4hi[0:1, NT * j:NT * (j + 1)],
                start=True, stop=True), inc="st")
            B.wait("vector", "st", mkb)
            last_bcastA[0] = B.on("vector", lambda v, j=j: v.tensor_copy(
                trt(j % 2), pB[:]), inc="sv")

        def Gtail(j, mk_rows=None):
            sl = slice(NT * j, NT * (j + 1))
            jj4 = NT * j
            if mk_rows is not None:
                B.wait("tensor", "ss", mk_rows)
            for oi, (psum, sbout) in enumerate(zip((pQ, pK, pV), qkv_out)):
                osl = slice(128 * oi, 128 * (oi + 1))
                B.on("tensor", lambda t, osl=osl, jj4=jj4, psum=psum: t.matmul(
                    psum[:], attr[0:1, 384 + osl.start:384 + osl.stop], oring[0:1, jj4:jj4 + NT],
                    start=False, stop=False))
                mkq = B.on("tensor", lambda t, osl=osl, jj4=jj4, psum=psum:
                           t.matmul(psum[:], attr[0:1, osl.start:osl.stop],
                                    rsrow[0:1, jj4:jj4 + NT],
                                    start=False, stop=True), inc="st")
                B.wait("vector", "st", mkq)
                evac_marks[(j, oi)] = B.on(
                    "vector", lambda v, sbout=sbout, sl=sl, psum=psum, j=j:
                    v.tensor_tensor(sbout[:, sl], psum[:], trt(j % 2), OP.mult),
                    inc="sv")

        last_bcastA = [None]
        stats_j(0)
        Gbig(0)
        stats_j(1)
        stats_j(2)
        stats_j(3)
        mk_s4, mk_rows = chain_wide()
        sbcast(0, mk_s4)
        Gtail(0, mk_rows)
        sbcast(1)
        Gbig(1)
        Gtail(1)
        sbcast(2)
        Gbig(2)
        Gtail(2)
        sbcast(3)
        Gbig(3)
        Gtail(3)
        mark_qkv = B.mark("sv")
        B.wait("gpsimd", "sv", mark_qkv)
        B.dma("gpsimd", Wk_sb[:].rearrange("p (a k) -> p a k", k=C),
              WkT.rearrange("(a p) k -> p a k", p=128), "dW")
        mark_wffn = B.mark("dW")

        B.wait("vector", "sv", mark_qkv)
        B.on("vector", lambda v: v.reduce_max(kmax[:], k_sb[:], axis=AX.X,
                                              negate=True), inc="sv")
        B.wait("scalar", "sv", B.mark("sv"))
        mark_E = B.on("scalar", lambda s: s.activation(
            Eb[:], k_sb[:], AF.Exp, bias=kmax[:], scale=1.0),
            inc="ss")
        B.wait("vector", "ss", mark_E)
        B.on("vector", lambda v: v.reduce_sum(ssum[:], Eb[:], axis=AX.X),
             inc="sv")
        B.wait("tensor", "ss", mark_E)
        B.wait("tensor", "sv", mark_const)
        for src, dst in ((Eb, Etm), (v_sb, vtm)):
            for g in range(2):
                B.wait("tensor", "sv", B.mark("sv"))
                mkt = None
                for b_ in range(8):
                    blk = 8 * g + b_
                    mkt = B.on("tensor", lambda t, src=src, b_=b_, blk=blk:
                               t.transpose(pT[:, 128 * b_:128 * (b_ + 1)],
                                           src[:, 128 * blk:128 * (blk + 1)],
                                           idb[:]),
                               inc="st" if b_ == 7 else None) or mkt
                B.wait("vector", "st", mkt)
                B.on("vector", lambda v, dst=dst, g=g: v.tensor_copy(
                    dst[:, 1024 * g:1024 * (g + 1)], pT[:]), inc="sv")
        mark_tr = B.mark("sv")
        B.wait("tensor", "sv", mark_tr)
        mark_ctx = None
        for blk in range(16):
            mark_ctx = B.on("tensor", lambda t, blk=blk: t.matmul(
                pAux[:, 0:128], Etm[:, 128 * blk:128 * (blk + 1)],
                vtm[:, 128 * blk:128 * (blk + 1)],
                start=(blk == 0), stop=(blk == 15)),
                inc="st" if blk == 15 else None) or mark_ctx
        B.on("vector", lambda v: v.reciprocal(rspr[:], ssum[:]), inc="sv")
        B.wait("vector", "st", mark_ctx)
        mark_cb = B.on("vector", lambda v: v.scalar_tensor_tensor(
            cb_sb[:], pAux[:, 0:128], rspr[:], maskb[:], OP.mult, OP.mult),
            inc="sv")

        B.wait("tensor", "dW", mark_wffn)
        last_bcast = {"evac": None}

        def bcastp(src_row, dst_tile, negate=False):
            if last_bcast["evac"] is not None:
                B.wait("tensor", *last_bcast["evac"])
            mkv = bcast(src_row, dst_tile, negate)
            last_bcast["evac"] = ("sv", mkv)
            return mkv

        att_all = {}
        outh_evac = {}
        stat_free = ["ss", stA_evac[3]]
        x2_marks = {}
        w_marks = {}
        mix_marks = {}
        o_all = {}
        out_dma_marks = {}
        r1_ev = {}
        m3_marks = {}
        last_fin = {"sg": 0}

        def CC1p(j):
            sl = slice(NT * j, NT * (j + 1))
            if j == 0:
                B.wait("tensor", "sv", mark_cb)
            if j - 1 in outh_evac:
                B.wait("tensor", "sv", outh_evac[j - 1])
            if (j - 2, NCH - 1) in o_all:
                B.wait("tensor", "sv", o_all[(j - 2, NCH - 1)])
            mko = B.on("tensor", lambda t, sl=sl: t.matmul(
                pV[:], cb_sb[:], q_sb[:, sl], start=True, stop=True), inc="st")
            B.wait("vector", "st", mko)
            outh_evac[j] = B.on("vector", lambda v, sl=sl: v.tensor_copy(
                outh[:, sl], pV[:]), inc="sv")
            B.wait("tensor", "sv", outh_evac[j])
            B.wait("vector", "sv", last_fin["sg"])
            mkc = B.on("vector", lambda v, sl=sl: v.tensor_copy(
                rsrow[0:1, 0:NT], rowRw[0:1, sl]), inc="sv")
            B.wait("tensor", "sv", mkc)
            mkm = B.on("tensor", lambda t: t.matmul(
                pV[:], ones1[:], rsrow[0:1, 0:NT], start=True, stop=True),
                inc="st")
            B.wait("vector", "st", mkm)
            r1_ev[j] = B.on("vector", lambda v: v.tensor_copy(trt(0), pV[:]),
                            inc="sv")

        def CC1a(j):
            sl = slice(NT * j, NT * (j + 1))
            if j - 1 in ffn_m and 7 in ffn_m[j - 1][0]:
                B.wait("tensor", "ss", ffn_m[j - 1][0][7])
            for c in range(NCH):
                psum = (pQ, pK, pV)[c % 3]
                if c == 2:
                    B.wait("tensor", "sv", r1_ev[j])
                prev_key = (j, c - 3) if c >= 3 else (j - 1, (6, 7, 5)[c])
                if prev_key in att_all:
                    B.wait("tensor", "sv", att_all[prev_key])
                mkp = B.on("tensor", lambda t, c=c, sl=sl, psum=psum: t.matmul(
                    psum[:], woutb[:, 128 * c:128 * (c + 1)], outh[:, sl],
                    start=True, stop=True), inc="st")
                tsl = NT * (c % 2)
                B.on("vector", lambda v, c=c, sl=sl, tsl=tsl: v.tensor_tensor(
                    t5r[:, tsl:tsl + NT], x_sb[c][:, sl], trt(0), OP.mult),
                    inc="sv")
                B.wait("vector", "st", mkp)
                att_all[(j, c)] = B.on(
                    "vector", lambda v, c=c, psum=psum, tsl=tsl:
                    v.scalar_tensor_tensor(
                        attr[:, NT * c:NT * (c + 1)], psum[:],
                        bocol[:, c:c + 1], t5r[:, tsl:tsl + NT],
                        OP.add, OP.add), inc="sv")
            B.wait("tensor", *stat_free)
            m3_marks[j] = B.on("tensor", lambda t, sl=sl: t.matmul(
                pSm[:], rwcol[:], outh[:, sl], start=True, stop=True),
                inc="st")

        def CC1c(j):
            sl = slice(NT * j, NT * (j + 1))
            B.wait("tensor", *stat_free)
            for c in range(NCH):
                if (c % 3) in sq_used:
                    B.wait("scalar", *sq_used[c % 3])
                B.wait("scalar", "sv", att_all[(j, c)])
                mkaq = B.on("scalar", lambda s, c=c: s.activation(
                    sq_slot(c), attr[:, NT * c:NT * (c + 1)], AF.Square),
                    inc="ss")
                B.wait("tensor", "ss", mkaq)
                sq_used[c % 3] = ("st", B.on("tensor", lambda t, c=c: t.matmul(
                    pSq[:], invccol[:], sq_slot(c),
                    start=(c == 0), stop=(c == NCH - 1)), inc="st"))
            B.wait("vector", "st", m3_marks[j])
            B.wait("vector", "ss", B.mark("ss"))
            B.on("vector", lambda v, sl=sl: v.scalar_tensor_tensor(
                rmj(0), pSm[:], bbar[:], mrs1row[0:1, sl], OP.add, OP.add),
                inc="sv")
            stat_free[:] = ["sv", B.on("vector", lambda v: v.tensor_copy(
                rqj(0), pSq[:]), inc="sv")]
            chain_core(0)
            B.on("vector", lambda v: v.tensor_tensor(rowM4[0:1, NT:2 * NT], rmj(0), rowRw[0:1, 0:NT],
                                                     OP.mult), inc="sv")
            B.on("vector", lambda v, sl=sl: v.tensor_copy(p2row[0:1, sl],
                                                          rowM4[0:1, NT:2 * NT]), inc="sv")

        def CC1b(j):
            sl = slice(NT * j, NT * (j + 1))
            mk_r3 = bcastp(rowRw[0:1, 0:NT], trt(0))
            B.wait("vector", "sv", mk_r3)
            for c in range(NCH):
                tsl = NT * (c % 2)
                B.on("vector", lambda v, c=c, tsl=tsl: v.tensor_tensor(
                    t3s[:, tsl:tsl + NT], attr[:, NT * c:NT * (c + 1)],
                    trt(0), OP.mult), inc="sv")
                x2_marks[(j, c)] = B.on(
                    "vector", lambda v, c=c, sl=sl, tsl=tsl, j=j: v.tensor_tensor(
                        x2_(c, j), t3s[:, tsl:tsl + NT],
                        x_sb[c][:, sl], OP.add), inc="sv")

        def CC2a(j):
            sl = slice(NT * j, NT * (j + 1))
            B.wait("tensor", *stat_free)
            for c in range(NCH):
                B.wait("tensor", "sv", x2_marks[(j, c)])
                B.on("tensor", lambda t, c=c, j=j: t.matmul(
                    pSm[:], invccol[:], x2_(c, j),
                    start=(c == 0), stop=(c == NCH - 1)))
            mks4 = None
            for c in range(NCH):
                if (c % 3) in sq_used:
                    B.wait("scalar", *sq_used[c % 3])
                B.wait("scalar", "sv", x2_marks[(j, c)])
                mkv = B.on("scalar", lambda s, c=c, j=j: s.activation(
                    sq_slot(c), x2_(c, j), AF.Square), inc="ss")
                B.wait("tensor", "ss", mkv)
                mks4 = B.on("tensor", lambda t, c=c: t.matmul(
                    pSq[:], invccol[:], sq_slot(c),
                    start=(c == 0), stop=(c == NCH - 1)), inc="st")
                sq_used[c % 3] = ("st", mks4)
            B.wait("scalar", "st", mks4)
            stat_free[:] = ["ss", stats_evac_s(0)]
            chain_core(0)
            B.on("vector", lambda v: v.tensor_tensor(rowM4[0:1, NT:2 * NT], rmj(0), rowRw[0:1, 0:NT],
                                                     OP.mult), inc="sv")

        def CC2b(j):
            sl = slice(NT * j, NT * (j + 1))
            B.wait("vector", "sv", last_fin["sg"])
            bcastp(rowRw[0:1, 0:NT], trt(1))
            mk_m4 = bcastp(rowM4[0:1, NT:2 * NT], trtb4[:], negate=True)
            B.wait("vector", "sv", mk_m4)
            for c in range(NCH):
                tsl = NT * (c % 2)
                B.on("vector", lambda v, c=c, tsl=tsl, j=j: v.tensor_tensor(
                    t3s[:, tsl:tsl + NT], x2_(c, j), trt(1), OP.mult),
                    inc="sv")
                w_marks[(j, c)] = B.on(
                    "vector", lambda v, c=c, sl=sl, tsl=tsl: v.tensor_tensor(
                        w_sb[c][:, sl], t3s[:, tsl:tsl + NT], trtb4[:],
                        OP.subtract), inc="sv")
            if j - 1 in rk_last:
                B.wait("vector", "st", rk_last[j - 1])
            for c in range(NCH):
                if j == 0:
                    B.on("vector", lambda v, c=c: v.tensor_copy(
                        attr[:, NT * c:NT * c + 1], w_sb[c][:, 0:1]))
                    B.on("vector", lambda v, c=c: v.tensor_tensor(
                        attr[:, NT * c + 1:NT * (c + 1)], w_sb[c][:, 1:NT],
                        w_sb[c][:, 0:NT - 1], OP.subtract), inc="sv")
                    B.on("vector", lambda v, c=c: v.tensor_scalar(
                        xr_(c)[:, 0:1], attr[:, NT * c:NT * c + 1],
                        murcol[:, c:c + 1], None, OP.mult))
                    B.on("vector", lambda v, c=c: v.scalar_tensor_tensor(
                        xr_(c)[:, 1:NT], attr[:, NT * c + 1:NT * (c + 1)],
                        murcol[:, c:c + 1], w_sb[c][:, 0:NT - 1],
                        OP.mult, OP.add), inc="sv")
                    B.on("vector", lambda v, c=c: v.tensor_scalar(
                        xk_(c)[:, 0:1], attr[:, NT * c:NT * c + 1],
                        mukcol[:, c:c + 1], None, OP.mult))
                    mix_marks[(j, c)] = B.on(
                        "vector", lambda v, c=c: v.scalar_tensor_tensor(
                            xk_(c)[:, 1:NT], attr[:, NT * c + 1:NT * (c + 1)],
                            mukcol[:, c:c + 1], w_sb[c][:, 0:NT - 1],
                            OP.mult, OP.add), inc="sv")
                else:
                    B.on("vector", lambda v, c=c, j=j: v.tensor_tensor(
                        attr[:, NT * c:NT * (c + 1)],
                        w_sb[c][:, NT * j:NT * (j + 1)],
                        w_sb[c][:, NT * j - 1:NT * (j + 1) - 1], OP.subtract),
                        inc="sv")
                    B.on("vector", lambda v, c=c, j=j: v.scalar_tensor_tensor(
                        xr_(c), attr[:, NT * c:NT * (c + 1)],
                        murcol[:, c:c + 1],
                        w_sb[c][:, NT * j - 1:NT * (j + 1) - 1],
                        OP.mult, OP.add), inc="sv")
                    mix_marks[(j, c)] = B.on(
                        "vector", lambda v, c=c, j=j: v.scalar_tensor_tensor(
                            xk_(c), attr[:, NT * c:NT * (c + 1)],
                            mukcol[:, c:c + 1],
                            w_sb[c][:, NT * j - 1:NT * (j + 1) - 1],
                            OP.mult, OP.add), inc="sv")

        rk_last = {}
        ffn_m = {}

        def _rk_groups(j, g0, g1, sig_m, ka_m):
            groups = ([("R", c) for c in range(NCH)]
                      + [("K", c) for c in range(NCH)])
            for g in range(g0, g1):
                kind, c = groups[g]
                bank = (pAux, pQ, pK)[g % 3]
                if j == 0:
                    if g == 1 and (0, 6) in att_all:
                        B.wait("tensor", "sv", att_all[(0, 6)])
                    if g == 2 and (0, 7) in att_all:
                        B.wait("tensor", "sv", att_all[(0, 7)])
                if g == 8 and (j + 1, 7) in att_all:
                    B.wait("tensor", "sv", att_all[(j + 1, 7)])
                if g == 10 and (j + 1, 6) in att_all:
                    B.wait("tensor", "sv", att_all[(j + 1, 6)])
                if g >= 3:
                    pk, pc = groups[g - 3]
                    if pk == "R":
                        B.wait("tensor", "ss", sig_m[pc])
                    else:
                        B.wait("tensor", "ss", ka_m[pc])
                W = Wr_sb if kind == "R" else Wk_sb
                xin = xr_ if kind == "R" else xk_
                mk = None
                for cc in range(NCH):
                    if g == 0:
                        B.wait("tensor", "sv", mix_marks[(j, cc)])
                    elif g == 1 and cc == 0:
                        B.wait("tensor", "sv", mix_marks[(j, NCH - 1)])
                    mk = B.on("tensor", lambda t, c=c, cc=cc, bank=bank, W=W,
                              xin=xin: t.matmul(
                                  bank[:],
                                  W[:, C * cc + 128 * c:C * cc + 128 * (c + 1)],
                                  xin(cc), start=(cc == 0),
                                  stop=(cc == NCH - 1)),
                              inc="st" if cc == NCH - 1 else None) or mk
                if kind == "R":
                    B.wait("scalar", "st", mk)
                    sig_m[c] = B.on("scalar", lambda s, c=c, bank=bank:
                                    s.activation(rr_(c), bank[:], AF.Sigmoid),
                                    inc="ss")
                else:
                    B.wait("scalar", "st", mk)
                    if c == 4:
                        B.wait("scalar", "sv",
                               att_all[(min(j + 1, NJ - 1), NCH - 1)])
                    if c == 6:
                        B.wait("scalar", "sv", last_fin["sg"])
                    if (c % 3) in sq_used:
                        B.wait("scalar", *sq_used[c % 3])
                    B.on("scalar", lambda s, c=c, bank=bank:
                         s.activation(sq_slot(c), bank[:], AF.Relu),
                         inc="ss")
                    ka_m[c] = B.on("scalar", lambda s, c=c: s.activation(
                        ka_(c), sq_slot(c), AF.Square), inc="ss")
                    sq_used[c % 3] = ("ss", ka_m[c])
                if g == 2 * NCH - 1:
                    rk_last[j] = mk

        def EEa1A(j):
            sig_m = {}
            ka_m = {}
            ffn_m[j] = (sig_m, ka_m)
            _rk_groups(j, 0, 8, sig_m, ka_m)

        def EEa1B(j):
            sig_m, ka_m = ffn_m[j]
            _rk_groups(j, 8, 16, sig_m, ka_m)

        def EEa2a(j):
            sl = slice(NT * j, NT * (j + 1))
            sig_m, ka_m = ffn_m[j]
            B.wait("tensor", "ss", ka_m[NCH - 1])
            B.wait("tensor", "ss", sig_m[NCH - 1])
            B.wait("vector", "ss", B.mark("ss"))
            for c in range(NCH):
                bV = (pV, pQ, pK)[c % 3]
                if c == 0 and (j + 1, 5) in att_all:
                    B.wait("tensor", "sv", att_all[(j + 1, 5)])
                if c >= 3:
                    B.wait("tensor", "sv", o_all[(j, c - 3)])
                mkV = None
                for cc in range(NCH):
                    mkV = B.on("tensor", lambda t, c=c, cc=cc, bV=bV: t.matmul(
                        bV[:], Wv_sb[:, C * cc + 128 * c:C * cc + 128 * (c + 1)],
                        ka_(cc), start=(cc == 0), stop=(cc == NCH - 1)),
                        inc="st" if cc == NCH - 1 else None) or mkV
                B.wait("vector", "st", mkV)
                o_all[(j, c)] = B.on("vector", lambda v, c=c, bV=bV:
                                     v.tensor_tensor(
                                         oring[:, NT * c:NT * (c + 1)], bV[:],
                                         rr_(c), OP.mult), inc="sv")
        def EEa2b(j):
            sl = slice(NT * j, NT * (j + 1))
            B.wait("tensor", *stat_free)
            for c in range(NCH):
                B.wait("tensor", "sv", o_all[(j, c)])
                B.on("tensor", lambda t, c=c: t.matmul(
                    pSm[:], invccol[:], oring[:, NT * c:NT * (c + 1)],
                    start=(c == 0), stop=(c == NCH - 1)))
            mks5 = None
            for c in range(NCH):
                if (c % 3) in sq_used:
                    B.wait("scalar", *sq_used[c % 3])
                B.wait("scalar", "sv", o_all[(j, c)])
                mkv = B.on("scalar", lambda s, c=c: s.activation(
                    sq_slot(c), oring[:, NT * c:NT * (c + 1)], AF.Square),
                    inc="ss")
                B.wait("tensor", "ss", mkv)
                mks5 = B.on("tensor", lambda t, c=c: t.matmul(
                    pSq[:], invccol[:], sq_slot(c),
                    start=(c == 0), stop=(c == NCH - 1)), inc="st")
                sq_used[c % 3] = ("st", mks5)
            B.wait("scalar", "st", mks5)
            stat_free[:] = ["ss", stats_evac_s(0)]
            chain_core(0)
            B.on("vector", lambda v: v.tensor_tensor(rowM4[0:1, NT:2 * NT], rmj(0), rowRw[0:1, 0:NT],
                                                     OP.mult), inc="sv")
            B.on("vector", lambda v, sl=sl: v.tensor_tensor(
                rowM4[0:1, NT:2 * NT], rowM4[0:1, NT:2 * NT], p2row[0:1, sl], OP.add), inc="sv")

        def EEa2c(j):
            bcastp(rowRw[0:1, 0:NT], trt(1))
            return bcastp(rowM4[0:1, NT:2 * NT], trtb4[:], negate=True)

        def EEb(j, mk_cl):
            sl = slice(NT * j, NT * (j + 1))
            B.wait("vector", "sv", mk_cl)
            B.wait("gpsimd", "sv", mk_cl)
            last = None
            fin_slot = {}
            for c in range(NCH):
                slot = c % 2
                tA = NT * (2 + c % 2)
                if slot in fin_slot:
                    B.wait("gpsimd", "sv", fin_slot[slot])
                B.on("gpsimd", lambda g, c=c, tA=tA: g.tensor_tensor(
                    t5r[:, tA:tA + NT], oring[:, NT * c:NT * (c + 1)],
                    trt(1), OP.mult), inc="sg")
                mg = B.on("gpsimd", lambda g, tA=tA: g.tensor_tensor(
                    t5r[:, tA:tA + NT], t5r[:, tA:tA + NT],
                    trtb4[:], OP.subtract), inc="sg")
                B.wait("vector", "sg", mg)
                if slot in out_dma_marks:
                    B.wait("vector", "dO", out_dma_marks[slot])
                last = B.on("vector", lambda v, c=c, slot=slot, tA=tA, j=j:
                            v.tensor_tensor(
                                outr[:, NT * slot:NT * (slot + 1)],
                                t5r[:, tA:tA + NT],
                                x2_(c, j), OP.add), inc="sv")
                fin_slot[slot] = last
                B.wait("sync", "sv", last)
                out_dma_marks[slot] = B.dma(
                    "sync", yT[128 * c:128 * (c + 1), sl],
                    outr[:, NT * slot:NT * (slot + 1)], "dO")
            last_fin["sg"] = last

        CC1p(0)
        CC1a(0)
        CC1c(0)
        CC1b(0)
        CC2a(0)
        CC1p(1)
        CC2b(0)
        EEa1A(0)
        CC1a(1)
        CC1c(1)
        EEa1B(0)
        CC1b(1)
        CC2a(1)
        EEa2a(0)
        CC1p(2)
        CC2b(1)
        EEa2b(0)
        EEa1A(1)
        CC1a(2)
        cl0 = EEa2c(0)
        CC1c(2)
        EEb(0, cl0)
        EEa1B(1)
        CC1b(2)
        CC2a(2)
        EEa2a(1)
        CC1p(3)
        CC2b(2)
        EEa2b(1)
        EEa1A(2)
        CC1a(3)
        cl1 = EEa2c(1)
        CC1c(3)
        EEb(1, cl1)
        EEa1B(2)
        CC1b(3)
        CC2a(3)
        EEa2a(2)
        CC2b(3)
        EEa2b(2)
        EEa1A(3)
        cl2 = EEa2c(2)
        EEb(2, cl2)
        EEa1B(3)
        EEa2a(3)
        EEa2b(3)
        cl3 = EEa2c(3)
        EEb(3, cl3)

        if debug:
            B.wait("sync", "sv", B.mark("sv"))
            for c in range(NCH):
                B.dma("sync", dbg["d_x2"][128 * c:128 * (c + 1), 0:2 * NT],
                      x2_sb[c][:], "dO")
                B.dma("sync", dbg["d_w"][128 * c:128 * (c + 1), :], w_sb[c][:],
                      "dO")
            B.dma("sync", dbg["d_or"], oring[:], "dO")
            B.dma("sync", dbg["d_xr"], k_sb[:], "dO")
            B.dma("sync", dbg["d_xk"], v_sb[:], "dO")
            B.dma("sync", dbg["d_xk2"], vtm[:], "dO")
            B.dma("sync", dbg["d_rr"], Etm[:], "dO")
            B.dma("sync", dbg["d_ka"], outh[:], "dO")
        B.wait("sync", "dO", B.mark("dO"))

        block = E_(nc.Block())

        @block.sync
        def _(e):
            for fn in B.ops["sync"]:
                fn(e)

        @block.gpsimd
        def _(e):
            for fn in B.ops["gpsimd"]:
                fn(e)

        @block.vector
        def _(e):
            for fn in B.ops["vector"]:
                fn(e)

        @block.scalar
        def _(e):
            for fn in B.ops["scalar"]:
                fn(e)

        @block.tensor
        def _(e):
            for fn in B.ops["tensor"]:
                fn(e)

    return nc


_NC_CACHE = {}


def _get_nc(debug=False):
    if debug not in _NC_CACHE:
        _NC_CACHE[debug] = build_nc(debug)
    return _NC_CACHE[debug]


def kernel(L0_attn_x, L0_attn_gamma, L0_attn_beta, L0_attn_w_qkv, L0_attn_w_out,
           L0_attn_b_out, L0_ffn_W_r, L0_ffn_W_k, L0_ffn_W_v, L0_ffn_mu_r,
           L0_ffn_mu_k, _debug=False):
    x = np.ascontiguousarray(L0_attn_x, dtype=np.float32)
    shared = {
        "w_qkv": np.ascontiguousarray(L0_attn_w_qkv, np.float32),
        "w_out": np.ascontiguousarray(L0_attn_w_out, np.float32),
        "WrT": np.ascontiguousarray(np.asarray(L0_ffn_W_r, np.float32).T),
        "WkT": np.ascontiguousarray(np.asarray(L0_ffn_W_k, np.float32).T),
        "WvT": np.ascontiguousarray(np.asarray(L0_ffn_W_v, np.float32).T),
        "vgamma": np.ascontiguousarray(np.asarray(L0_attn_gamma, np.float32).ravel()),
        "vbeta": np.ascontiguousarray(np.asarray(L0_attn_beta, np.float32).ravel()),
        "vbout": np.ascontiguousarray(np.asarray(L0_attn_b_out, np.float32).ravel()),
        "vmur": np.ascontiguousarray(L0_ffn_mu_r, np.float32),
        "vmuk": np.ascontiguousarray(L0_ffn_mu_k, np.float32),
    }
    in_maps = []
    for b in range(8):
        m = dict(shared)
        m["xT"] = np.ascontiguousarray(x[b].T)
        in_maps.append(m)
    nc = _get_nc(_debug)
    res = run_bass_kernel_spmd(nc, in_maps, core_ids=list(range(8)))
    out = np.stack([np.asarray(r["yT"]).T for r in res.results])
    if _debug:
        return out, res.results
    return out


# version 38
# speedup vs baseline: 1.0076x; 1.0076x over previous
import contextlib
import numpy as np
import concourse.bass as bass
import concourse.mybir as mybir
from concourse.bass_utils import run_bass_kernel_spmd
from concourse.alu_op_type import AluOpType as OP

F32 = mybir.dt.float32
BF16 = mybir.dt.bfloat16
AF = mybir.ActivationFunctionType
AX = mybir.AxisListType

T = 2048
C = 1024
NCH = 8
NJ = 4
NT = 512
EPS = 1e-5
INV_C = 1.0 / C

ENGINES = ("sync", "gpsimd", "vector", "scalar", "tensor")


class Builder:
    def __init__(self, nc, sems):
        self.nc = nc
        self.sems = sems
        self.cnt = {k: 0 for k in sems}
        self.ops = {e: [] for e in ENGINES}

    def on(self, eng, fn, inc=None):
        sem = self.sems[inc] if inc else None
        if inc:
            self.cnt[inc] += 1

        def run(e, fn=fn, sem=sem):
            inst = fn(e)
            if sem is not None:
                inst.then_inc(sem, 1)
        self.ops[eng].append(run)
        return self.cnt[inc] if inc else None

    def dma(self, eng, out, in_, inc, slow=False):
        self.cnt[inc] += 16
        sem = self.sems[inc]

        def run(e, out=out, in_=in_, sem=sem, slow=slow):
            kw = dict(allow_slow_non_contiguous=True) if slow else {}
            e.dma_start(out=out, in_=in_, **kw).then_inc(sem, 16)
        self.ops[eng].append(run)
        return self.cnt[inc]

    def mark(self, sem):
        return self.cnt[sem]

    def wait(self, eng, sem, val=None):
        if val is None:
            val = self.cnt[sem]
        s = self.sems[sem]

        def run(e, s=s, val=val):
            e.wait_ge(s, val)
        self.ops[eng].append(run)


def build_nc(debug=False):
    nc = bass.Bass("TRN2", target_bir_lowering=False)

    xT = nc.dram_tensor("xT", [C, T], F32, kind="ExternalInput").ap()
    w_qkv = nc.dram_tensor("w_qkv", [C, 384], F32, kind="ExternalInput").ap()
    w_out = nc.dram_tensor("w_out", [128, C], F32, kind="ExternalInput").ap()
    WrT = nc.dram_tensor("WrT", [C, C], F32, kind="ExternalInput").ap()
    WkT = nc.dram_tensor("WkT", [C, C], F32, kind="ExternalInput").ap()
    WvT = nc.dram_tensor("WvT", [C, C], F32, kind="ExternalInput").ap()
    vgamma = nc.dram_tensor("vgamma", [C], F32, kind="ExternalInput").ap()
    vbeta = nc.dram_tensor("vbeta", [C], F32, kind="ExternalInput").ap()
    vbout = nc.dram_tensor("vbout", [C], F32, kind="ExternalInput").ap()
    vmur = nc.dram_tensor("vmur", [C], F32, kind="ExternalInput").ap()
    vmuk = nc.dram_tensor("vmuk", [C], F32, kind="ExternalInput").ap()
    yT = nc.dram_tensor("yT", [C, T], F32, kind="ExternalOutput").ap()

    dbg = {}
    if debug:
        for nm, shp, dt in [("d_x2", [C, T], BF16), ("d_w", [C, T], BF16),
                            ("d_or", [128, NCH * NT], BF16),
                            ("d_xr", [128, T], BF16), ("d_xk", [128, T], BF16),
                            ("d_xk2", [128, T], BF16), ("d_rr", [128, T], BF16),
                            ("d_ka", [128, T], BF16)]:
            dbg[nm] = nc.dram_tensor(nm, shp, dt, kind="ExternalOutput").ap()

    with contextlib.ExitStack() as ST:
        E_ = ST.enter_context

        x_sb = [E_(nc.sbuf_tensor(f"x{c}", [128, T], BF16)) for c in range(NCH)]
        x2_sb = [E_(nc.sbuf_tensor(f"x2_{c}", [128, 2 * NT], BF16)) for c in range(NCH)]
        q_sb = E_(nc.sbuf_tensor("q", [128, T], BF16))
        k_sb = E_(nc.sbuf_tensor("k", [128, T], BF16))
        v_sb = E_(nc.sbuf_tensor("v", [128, T], BF16))
        Eb = E_(nc.sbuf_tensor("Eb", [128, T], BF16))
        Etm = E_(nc.sbuf_tensor("Etm", [128, T], BF16))
        vtm = E_(nc.sbuf_tensor("vtm", [128, T], BF16))
        outh = E_(nc.sbuf_tensor("outh", [128, T], BF16))
        t5r = E_(nc.sbuf_tensor("t5r", [128, T], BF16))
        Wr_sb = E_(nc.sbuf_tensor("Wr", [128, NCH * C], BF16))
        Wk_sb = E_(nc.sbuf_tensor("Wk", [128, NCH * C], BF16))
        Wv_sb = E_(nc.sbuf_tensor("Wv", [128, NCH * C], BF16))
        woutb = E_(nc.sbuf_tensor("woutb", [128, C], BF16))
        attr = E_(nc.sbuf_tensor("attr", [128, NCH * NT], BF16))
        oring = E_(nc.sbuf_tensor("oring", [128, NCH * NT], BF16))
        sqr = E_(nc.sbuf_tensor("sqr", [128, 3 * NT], BF16))
        t3s = E_(nc.sbuf_tensor("t3s", [128, 2 * NT], BF16))
        rrB = E_(nc.sbuf_tensor("rrB", [128, 4 * NT], BF16))

        def _alias(name, backing):
            t_ = nc.virtual_sbuf_tensor(name, [128, T], BF16)
            nc.back_virtual_tensor_with(t_, backing)
            return t_
        w_sb = [_alias(f"w{c}", x_sb[c]) for c in range(NCH)]
        xrA = _alias("xrA", k_sb)
        xrB = _alias("xrB", Eb)
        xkA = _alias("xkA", v_sb)
        xkB = _alias("xkB", vtm)
        rrA = _alias("rrA", Etm)
        kaA = _alias("kaA", outh)
        kaB = _alias("kaB", t5r)

        def _seg(tA, tB, c):
            t_ = tA if c < 4 else tB
            return t_[:, NT * (c % 4):NT * (c % 4 + 1)]

        def x2_(c, j):
            return x2_sb[c][:, NT * (j % 2):NT * (j % 2 + 1)]

        def xr_(c):
            return _seg(xrA, xrB, c)

        def xk_(c):
            return _seg(xkA, xkB, c)

        def rr_(c):
            if c < 4:
                return rrA[:, NT * c:NT * (c + 1)]
            return rrB[:, NT * (c - 4):NT * (c - 3)]

        def ka_(c):
            return _seg(kaA, kaB, c)

        rsrow = E_(nc.sbuf_tensor("rsrow", [1, T], BF16))
        rowM4 = E_(nc.sbuf_tensor("rowM4", [1, T], F32))
        rowQ4 = E_(nc.sbuf_tensor("rowQ4", [1, T], F32))
        wT0 = E_(nc.sbuf_tensor("wT0", [1, T], F32))
        rowRw = E_(nc.sbuf_tensor("rowRw", [1, T], F32))
        mrs1row = E_(nc.sbuf_tensor("mrs1row", [1, T], BF16))
        p2row = E_(nc.sbuf_tensor("p2row", [1, T], BF16))
        s4hi = p2row

        outr = E_(nc.sbuf_tensor("outr", [128, 2 * NT], F32))
        TRT = E_(nc.sbuf_tensor("TRT", [128, 2 * NT], BF16))
        trtb4 = E_(nc.sbuf_tensor("trtb4", [128, NT], BF16))

        def trt(i):
            return TRT[:, NT * i:NT * (i + 1)]

        gcol = E_(nc.sbuf_tensor("gcol", [128, 8], F32))
        bcol = E_(nc.sbuf_tensor("bcol", [128, 8], F32))
        bocol = E_(nc.sbuf_tensor("bocol", [128, 8], F32))
        murcol = E_(nc.sbuf_tensor("murcol", [128, 8], F32))
        mukcol = E_(nc.sbuf_tensor("mukcol", [128, 8], F32))
        bcolb2 = E_(nc.sbuf_tensor("bcolb2", [128, 16], BF16, align_bytes=4))
        onescol = E_(nc.sbuf_tensor("onescol", [128, 1], BF16, align_bytes=4))
        invccol = E_(nc.sbuf_tensor("invccol", [128, 1], BF16, align_bytes=4))
        ones1 = E_(nc.sbuf_tensor("ones1", [1, 128], BF16, align_bytes=4))
        bocolb = E_(nc.sbuf_tensor("bocolb", [128, 8], BF16, align_bytes=4))
        cb_sb = E_(nc.sbuf_tensor("cb_sb", [128, 128], BF16))
        idb = nc.virtual_sbuf_tensor("idb", [128, 128], BF16)
        nc.back_virtual_tensor_with(idb, cb_sb)
        io_c = E_(nc.sbuf_tensor("io_c", [128, 128], BF16))
        maskb = nc.virtual_sbuf_tensor("maskb", [128, 128], BF16)
        nc.back_virtual_tensor_with(maskb, io_c)
        io_r = E_(nc.sbuf_tensor("io_r", [128, 1], F32, align_bytes=4))
        rspr = E_(nc.sbuf_tensor("rspr", [128, 1], F32, align_bytes=4))
        kmax = E_(nc.sbuf_tensor("kmax", [128, 1], F32, align_bytes=4))
        ssum = E_(nc.sbuf_tensor("ssum", [128, 1], F32, align_bytes=4))
        rwcol = E_(nc.sbuf_tensor("rwcol", [128, 1], BF16, align_bytes=4))
        rwf = E_(nc.sbuf_tensor("rwf", [128, 1], F32, align_bytes=4))
        bbar = E_(nc.sbuf_tensor("bbar", [1, 1], F32, align_bytes=4))
        epsc = E_(nc.sbuf_tensor("epsc", [1, 1], F32, align_bytes=4))
        pb8 = E_(nc.sbuf_tensor("pb8", [1, 8], F32, align_bytes=4))

        pQ = E_(nc.psum_tensor("pQ", [128, NT], F32))
        pK = E_(nc.psum_tensor("pK", [128, NT], F32))
        pV = E_(nc.psum_tensor("pV", [128, NT], F32))
        pSm = E_(nc.psum_tensor("pSm", [1, NT], F32))
        pSq = E_(nc.psum_tensor("pSq", [1, NT], F32))
        pB = E_(nc.psum_tensor("pB", [128, NT], F32))
        pAux = E_(nc.psum_tensor("pAux", [128, NT], F32))
        pT = E_(nc.psum_tensor("pT", [128, 1024], BF16))

        sems = {}
        for s in ("dX", "dW", "dM", "dO", "sv", "ss", "st", "sg"):
            sems[s] = E_(nc.semaphore(s))
        B = Builder(nc, sems)

        for c in range(NCH):
            B.dma("gpsimd", x_sb[c][:], xT[128 * c:128 * c + 128, :], "dX")
        mark_x = B.mark("dX")
        for dst, src in ((gcol, vgamma), (bcol, vbeta), (bocol, vbout),
                         (murcol, vmur), (mukcol, vmuk)):
            B.dma("sync", dst[:], src.rearrange("(a p) -> p a", p=128), "dM",
                  slow=True)
        mark_vecs = B.mark("dM")
        B.dma("gpsimd", Wr_sb[:, 0:NCH * 384].rearrange("p (a k) -> p a k", k=384),
              w_qkv.rearrange("(a p) k -> p a k", p=128), "dW")
        B.dma("gpsimd", woutb[:], w_out, "dW")
        mark_wqkv = B.mark("dW")
        B.dma("gpsimd", Wv_sb[:].rearrange("p (a k) -> p a k", k=C),
              WvT.rearrange("(a p) k -> p a k", p=128), "dW")

        B.on("gpsimd", lambda g: g.iota(io_c[:], pattern=[[1, 128]], base=0,
                                        channel_multiplier=0,
                                        allow_small_or_imprecise_dtypes=True))
        mark_iota = B.on("gpsimd", lambda g: g.iota(
            io_r[:], pattern=[[0, 1]], base=0, channel_multiplier=1,
            allow_small_or_imprecise_dtypes=True), inc="sg")

        B.on("vector", lambda v: v.memset(onescol[:], 1.0))
        B.on("vector", lambda v: v.memset(invccol[:], INV_C))
        B.on("vector", lambda v: v.memset(ones1[:], 1.0))
        B.on("vector", lambda v: v.memset(epsc[:], EPS))
        B.on("vector", lambda v: v.memset(bcolb2[:], 0.0))
        B.wait("vector", "sg", mark_iota)
        mark_const = B.on("vector", lambda v: v.tensor_scalar(
            idb[:], io_c[:], io_r[:], None, OP.is_equal), inc="sv")
        B.on("vector", lambda v: v.memset(maskb[:], 0.0))
        for h in range(4):
            B.on("vector", lambda v, h=h: v.memset(
                maskb[32 * h:32 * h + 32, 32 * h:32 * h + 32], 32.0 ** -0.5))

        B.wait("vector", "dW", mark_wqkv)
        B.wait("vector", "dM", mark_vecs)
        for c in range(NCH):
            B.on("vector", lambda v, c=c: v.tensor_scalar(
                Wk_sb[:, 384 * c:384 * (c + 1)], Wr_sb[:, 384 * c:384 * (c + 1)],
                gcol[:, c:c + 1], None, OP.mult), inc="sv")
        B.on("vector", lambda v: v.tensor_copy(
            bcolb2[:].rearrange("p (a two) -> p a two", two=2)[:, :, 0:1],
            bcol[:].rearrange("p (a one) -> p a one", one=1)), inc="sv")
        mk_rwf = B.on("vector", lambda v: v.reduce_sum(rwf[:], woutb[:],
                                                       axis=AX.X), inc="sv")
        B.on("vector", lambda v: v.tensor_copy(bocolb[:], bocol[:]), inc="sv")
        B.wait("scalar", "sv", mk_rwf)
        B.on("scalar", lambda s: s.mul(rwcol[:], rwf[:], INV_C), inc="ss")
        mark_prep = B.mark("sv")
        mark_prep_s = B.mark("ss")

        B.wait("tensor", "sv", mark_prep)
        mark_bqcw = None
        for c in range(NCH):
            B.on("tensor", lambda t, c=c: t.matmul(
                pSm[0:1, 0:384], bcolb2[:, 2 * c:2 * c + 1],
                Wr_sb[:, 384 * c:384 * (c + 1)], start=(c == 0),
                stop=(c == NCH - 1)))
            mark_bqcw = B.on("tensor", lambda t, c=c: t.matmul(
                pSq[0:1, 0:384], onescol[:], Wk_sb[:, 384 * c:384 * (c + 1)],
                start=(c == 0), stop=(c == NCH - 1)),
                inc="st" if c == NCH - 1 else None) or mark_bqcw
        B.wait("vector", "st", mark_bqcw)
        B.on("vector", lambda v: v.tensor_copy(attr[0:1, 0:384], pSm[0:1, 0:384]),
             inc="sv")
        B.on("vector", lambda v: v.tensor_copy(attr[0:1, 384:768], pSq[0:1, 0:384]),
             inc="sv")
        mark_cwbq = B.mark("sv")
        B.wait("tensor", "sv", mark_cwbq)
        mark_bo = B.on("tensor", lambda t: t.matmul(
            pSm[0:1, 0:8], onescol[:], bocolb[:], start=True, stop=True),
            inc="st")
        B.wait("vector", "st", mark_bo)
        mk_pb8 = B.on("vector", lambda v: v.reduce_sum(pb8[:, 0:1],
                                                       pSm[0:1, 0:8],
                                                       axis=AX.X), inc="sv")
        B.wait("scalar", "sv", mk_pb8)
        B.on("scalar", lambda s: s.mul(bbar[:], pb8[:, 0:1], INV_C), inc="ss")
        B.wait("gpsimd", "st", mark_bo)
        B.wait("gpsimd", "sv", mark_prep)
        B.dma("gpsimd", Wr_sb[:].rearrange("p (a k) -> p a k", k=C),
              WrT.rearrange("(a p) k -> p a k", p=128), "dW")

        sq_used = {}
        ob_used = sq_used

        def sq_slot(c):
            return sqr[:, NT * (c % 3):NT * (c % 3 + 1)]

        def ob_slot(c):
            return sq_slot(c)

        def rmj(j):
            return rowM4[0:1, 0:NT]

        def rqj(j):
            return rowQ4[0:1, 0:NT]

        def chain_core(j=0):
            B.wait("scalar", "sv", B.mark("sv"))
            B.on("scalar", lambda s, j=j: s.activation(wT0[0:1, 0:NT], rmj(j),
                                                       AF.Square), inc="ss")
            B.wait("vector", "ss", B.mark("ss"))
            mkv = B.on("vector", lambda v, j=j: v.tensor_tensor(
                rowM4[0:1, NT:2 * NT], rqj(j), wT0[0:1, 0:NT], OP.subtract), inc="sv")
            B.wait("scalar", "sv", mkv)
            B.on("scalar", lambda s: s.activation(
                wT0[0:1, 0:NT], rowM4[0:1, NT:2 * NT], AF.Ln, bias=epsc[:], scale=1.0), inc="ss")
            mk = B.on("scalar", lambda s: s.activation(
                rowRw[0:1, 0:NT], wT0[0:1, 0:NT], AF.Exp, scale=-0.5), inc="ss")
            B.wait("vector", "ss", mk)
            return mk

        def stats_evac_s(j=0):
            B.wait("scalar", "sv", B.mark("sv"))
            B.on("scalar", lambda s, j=j: s.copy(rmj(j), pSm[:]))
            return B.on("scalar", lambda s, j=j: s.copy(rqj(j), pSq[:]),
                        inc="ss")

        def stats_evac(j=0):
            B.wait("vector", "ss", B.mark("ss"))
            B.on("vector", lambda v, j=j: v.tensor_copy(rmj(j), pSm[:]),
                 inc="sv")
            return B.on("vector", lambda v, j=j: v.tensor_copy(rqj(j), pSq[:]),
                        inc="sv")

        def bcast(src_row, dst_tile, negate=False):
            mk = B.on("vector", lambda v: v.tensor_copy(rsrow[0:1, 0:NT], src_row),
                      inc="sv")
            B.wait("tensor", "sv", mk)
            mk2 = B.on("tensor", lambda t: t.matmul(pB[:], ones1[:],
                                                    rsrow[0:1, 0:NT],
                                                    start=True, stop=True),
                       inc="st")
            B.wait("vector", "st", mk2)
            return B.on("vector", lambda v, dst=dst_tile: v.tensor_copy(
                dst, pB[:]), inc="sv")

        mark_setup_ss = B.mark("ss")
        B.wait("vector", "dX", mark_x)
        B.wait("scalar", "dX", mark_x)
        B.wait("tensor", "dX", mark_x)
        B.wait("tensor", "sv", mark_cwbq)
        B.wait("tensor", "ss", mark_setup_ss)
        B.wait("vector", "ss", mark_setup_ss)
        qkv_out = (q_sb, k_sb, v_sb)
        evac_marks = {}
        stA_evac = {}

        def stats_j(j):
            sl = slice(NT * j, NT * (j + 1))
            if j == 0:
                B.wait("tensor", "sv", mk_pb8)
            else:
                B.wait("tensor", "ss", stA_evac[j - 1])
            mkm = None
            for c in range(NCH):
                mkm = B.on("tensor", lambda t, c=c, sl=sl: t.matmul(
                    pSm[:], invccol[:], x_sb[c][:, sl],
                    start=(c == 0), stop=(c == NCH - 1)),
                    inc="st" if c == NCH - 1 else None) or mkm
            mks = None
            for c in range(NCH):
                if (c % 3) in sq_used:
                    B.wait("scalar", *sq_used[c % 3])
                mkv = B.on("scalar", lambda s, c=c, sl=sl: s.activation(
                    sq_slot(c), x_sb[c][:, sl], AF.Square), inc="ss")
                B.wait("tensor", "ss", mkv)
                mks = B.on("tensor", lambda t, c=c: t.matmul(
                    pSq[:], invccol[:], sq_slot(c),
                    start=(c == 0), stop=(c == NCH - 1)), inc="st")
                sq_used[c % 3] = ("st", mks)
            B.wait("scalar", "st", mks)
            B.on("scalar", lambda s, sl=sl: s.copy(rowM4[0:1, sl], pSm[:]))
            stA_evac[j] = B.on("scalar", lambda s, sl=sl: s.copy(
                rowQ4[0:1, sl], pSq[:]), inc="ss")

        def chain_wide():
            mk0 = B.on("scalar", lambda s: s.activation(
                wT0[:], rowM4[:], AF.Square), inc="ss")
            B.wait("vector", "ss", mk0)
            mkv = B.on("vector", lambda v: v.tensor_tensor(
                rowQ4[:], rowQ4[:], wT0[:], OP.subtract), inc="sv")
            B.wait("scalar", "sv", mkv)
            B.on("scalar", lambda s: s.activation(
                wT0[:], rowQ4[:], AF.Ln, bias=epsc[:], scale=1.0), inc="ss")
            mke = B.on("scalar", lambda s: s.activation(
                rowRw[:], wT0[:], AF.Exp, scale=-0.5), inc="ss")
            B.wait("vector", "ss", mke)
            B.on("vector", lambda v: v.tensor_tensor(
                mrs1row[0:1, :], rowM4[:], rowRw[:], OP.mult), inc="sv")
            B.on("vector", lambda v: v.tensor_scalar(
                oring[0:1, 0:T], rowM4[:], -1.0, None, OP.mult), inc="sv")
            mku = B.on("vector", lambda v: v.tensor_scalar(
                rowM4[:], rowQ4[:], 1.0 + EPS, EPS * EPS, OP.mult, OP.add),
                inc="sv")
            B.wait("scalar", "sv", mku)
            B.on("scalar", lambda s: s.activation(
                wT0[:], rowM4[:], AF.Ln), inc="ss")
            mk_s4 = B.on("scalar", lambda s: s.activation(
                s4hi[0:1, :], wT0[:], AF.Exp, scale=-0.5), inc="ss")
            mk_rows = B.on("scalar", lambda s: s.activation(
                rsrow[0:1, :], wT0[:], AF.Exp, scale=0.5), inc="ss")
            return mk_s4, mk_rows

        def Gbig(j):
            sl = slice(NT * j, NT * (j + 1))
            for oi, psum in enumerate((pQ, pK, pV)):
                osl = slice(128 * oi, 128 * (oi + 1))
                if (j - 1, oi) in evac_marks:
                    B.wait("tensor", "sv", evac_marks[(j - 1, oi)])
                for c in range(NCH):
                    B.on("tensor", lambda t, c=c, osl=osl, sl=sl, psum=psum:
                         t.matmul(psum[:],
                                  Wk_sb[:, 384 * c + osl.start:384 * c + osl.stop],
                                  x_sb[c][:, sl], start=(c == 0), stop=False))

        def sbcast(j, mk_s4=None):
            if mk_s4 is not None:
                B.wait("tensor", "ss", mk_s4)
            if last_bcastA[0] is not None:
                B.wait("tensor", "sv", last_bcastA[0])
            mkb = B.on("tensor", lambda t, j=j: t.matmul(
                pB[:], ones1[:], s4hi[0:1, NT * j:NT * (j + 1)],
                start=True, stop=True), inc="st")
            B.wait("vector", "st", mkb)
            last_bcastA[0] = B.on("vector", lambda v, j=j: v.tensor_copy(
                trt(j % 2), pB[:]), inc="sv")

        def Gtail(j, mk_rows=None):
            sl = slice(NT * j, NT * (j + 1))
            jj4 = NT * j
            if mk_rows is not None:
                B.wait("tensor", "ss", mk_rows)
            for oi, (psum, sbout) in enumerate(zip((pQ, pK, pV), qkv_out)):
                osl = slice(128 * oi, 128 * (oi + 1))
                B.on("tensor", lambda t, osl=osl, jj4=jj4, psum=psum: t.matmul(
                    psum[:], attr[0:1, 384 + osl.start:384 + osl.stop], oring[0:1, jj4:jj4 + NT],
                    start=False, stop=False))
                mkq = B.on("tensor", lambda t, osl=osl, jj4=jj4, psum=psum:
                           t.matmul(psum[:], attr[0:1, osl.start:osl.stop],
                                    rsrow[0:1, jj4:jj4 + NT],
                                    start=False, stop=True), inc="st")
                B.wait("vector", "st", mkq)
                evac_marks[(j, oi)] = B.on(
                    "vector", lambda v, sbout=sbout, sl=sl, psum=psum, j=j:
                    v.tensor_tensor(sbout[:, sl], psum[:], trt(j % 2), OP.mult),
                    inc="sv")

        last_bcastA = [None]
        stats_j(0)
        Gbig(0)
        stats_j(1)
        stats_j(2)
        stats_j(3)
        mk_s4, mk_rows = chain_wide()
        sbcast(0, mk_s4)
        Gtail(0, mk_rows)
        sbcast(1)
        Gbig(1)
        Gtail(1)
        sbcast(2)
        Gbig(2)
        Gtail(2)
        sbcast(3)
        Gbig(3)
        Gtail(3)
        mark_qkv = B.mark("sv")
        B.wait("gpsimd", "sv", mark_qkv)
        B.dma("gpsimd", Wk_sb[:].rearrange("p (a k) -> p a k", k=C),
              WkT.rearrange("(a p) k -> p a k", p=128), "dW")
        mark_wffn = B.mark("dW")

        B.wait("vector", "sv", mark_qkv)
        B.on("vector", lambda v: v.reduce_max(kmax[:], k_sb[:], axis=AX.X,
                                              negate=True), inc="sv")
        B.wait("scalar", "sv", B.mark("sv"))
        mark_E = B.on("scalar", lambda s: s.activation(
            Eb[:], k_sb[:], AF.Exp, bias=kmax[:], scale=1.0),
            inc="ss")
        B.wait("vector", "ss", mark_E)
        B.on("vector", lambda v: v.reduce_sum(ssum[:], Eb[:], axis=AX.X),
             inc="sv")
        B.wait("tensor", "ss", mark_E)
        B.wait("tensor", "sv", mark_const)
        for src, dst in ((Eb, Etm), (v_sb, vtm)):
            for g in range(2):
                B.wait("tensor", "sv", B.mark("sv"))
                mkt = None
                for b_ in range(8):
                    blk = 8 * g + b_
                    mkt = B.on("tensor", lambda t, src=src, b_=b_, blk=blk:
                               t.transpose(pT[:, 128 * b_:128 * (b_ + 1)],
                                           src[:, 128 * blk:128 * (blk + 1)],
                                           idb[:]),
                               inc="st" if b_ == 7 else None) or mkt
                B.wait("vector", "st", mkt)
                B.on("vector", lambda v, dst=dst, g=g: v.tensor_copy(
                    dst[:, 1024 * g:1024 * (g + 1)], pT[:]), inc="sv")
        mark_tr = B.mark("sv")
        B.wait("tensor", "sv", mark_tr)
        mark_ctx = None
        for blk in range(16):
            mark_ctx = B.on("tensor", lambda t, blk=blk: t.matmul(
                pAux[:, 0:128], Etm[:, 128 * blk:128 * (blk + 1)],
                vtm[:, 128 * blk:128 * (blk + 1)],
                start=(blk == 0), stop=(blk == 15)),
                inc="st" if blk == 15 else None) or mark_ctx
        B.on("vector", lambda v: v.reciprocal(rspr[:], ssum[:]), inc="sv")
        B.wait("vector", "st", mark_ctx)
        mark_cb = B.on("vector", lambda v: v.scalar_tensor_tensor(
            cb_sb[:], pAux[:, 0:128], rspr[:], maskb[:], OP.mult, OP.mult),
            inc="sv")

        B.wait("tensor", "dW", mark_wffn)
        last_bcast = {"evac": None}

        def bcastp(src_row, dst_tile, negate=False):
            if last_bcast["evac"] is not None:
                B.wait("tensor", *last_bcast["evac"])
            mkv = bcast(src_row, dst_tile, negate)
            last_bcast["evac"] = ("sv", mkv)
            return mkv

        att_all = {}
        outh_evac = {}
        stat_free = ["ss", stA_evac[3]]
        x2_marks = {}
        w_marks = {}
        mix_marks = {}
        xk_marks = {}
        o_all = {}
        out_dma_marks = {}
        m3_marks = {}
        last_fin = {"sg": 0}

        def CC1a(j):
            sl = slice(NT * j, NT * (j + 1))
            if j == 0:
                B.wait("tensor", "sv", mark_cb)
            if j - 1 in outh_evac:
                B.wait("tensor", "sv", outh_evac[j - 1])
            if last_bcast["evac"] is not None:
                B.wait("tensor", *last_bcast["evac"])
            if (j - 2, NCH - 1) in o_all:
                B.wait("tensor", "sv", o_all[(j - 2, NCH - 1)])
            mko = B.on("tensor", lambda t, sl=sl: t.matmul(
                pB[:], cb_sb[:], q_sb[:, sl], start=True, stop=True), inc="st")
            B.wait("vector", "st", mko)
            outh_evac[j] = B.on("vector", lambda v, sl=sl: v.tensor_copy(
                outh[:, sl], pB[:]), inc="sv")
            B.wait("tensor", "sv", outh_evac[j])
            B.wait("vector", "sv", last_fin["sg"])
            bcastp(rowRw[0:1, sl], trt(0))
            if j - 1 in ffn_m and 7 in ffn_m[j - 1][0]:
                B.wait("tensor", "ss", ffn_m[j - 1][0][7])
            for c in range(NCH):
                psum = (pQ, pK, pV)[c % 3]
                prev_key = (j, c - 3) if c >= 3 else (j - 1, (6, 7, 5)[c])
                if prev_key in att_all:
                    B.wait("tensor", "sv", att_all[prev_key])
                mkp = B.on("tensor", lambda t, c=c, sl=sl, psum=psum: t.matmul(
                    psum[:], woutb[:, 128 * c:128 * (c + 1)], outh[:, sl],
                    start=True, stop=True), inc="st")
                tsl = NT * (c % 2)
                B.on("vector", lambda v, c=c, sl=sl, tsl=tsl: v.tensor_tensor(
                    t5r[:, tsl:tsl + NT], x_sb[c][:, sl], trt(0), OP.mult),
                    inc="sv")
                B.wait("vector", "st", mkp)
                att_all[(j, c)] = B.on(
                    "vector", lambda v, c=c, psum=psum, tsl=tsl:
                    v.scalar_tensor_tensor(
                        attr[:, NT * c:NT * (c + 1)], psum[:],
                        bocol[:, c:c + 1], t5r[:, tsl:tsl + NT],
                        OP.add, OP.add), inc="sv")
            B.wait("tensor", *stat_free)
            m3_marks[j] = B.on("tensor", lambda t, sl=sl: t.matmul(
                pSm[:], rwcol[:], outh[:, sl], start=True, stop=True),
                inc="st")

        def CC1c(j):
            sl = slice(NT * j, NT * (j + 1))
            B.wait("tensor", *stat_free)
            for c in range(NCH):
                if (c % 3) in sq_used:
                    B.wait("scalar", *sq_used[c % 3])
                B.wait("scalar", "sv", att_all[(j, c)])
                mkaq = B.on("scalar", lambda s, c=c: s.activation(
                    sq_slot(c), attr[:, NT * c:NT * (c + 1)], AF.Square),
                    inc="ss")
                B.wait("tensor", "ss", mkaq)
                sq_used[c % 3] = ("st", B.on("tensor", lambda t, c=c: t.matmul(
                    pSq[:], invccol[:], sq_slot(c),
                    start=(c == 0), stop=(c == NCH - 1)), inc="st"))
            B.wait("vector", "st", m3_marks[j])
            B.wait("vector", "ss", B.mark("ss"))
            B.on("vector", lambda v, sl=sl: v.scalar_tensor_tensor(
                rmj(0), pSm[:], bbar[:], mrs1row[0:1, sl], OP.add, OP.add),
                inc="sv")
            stat_free[:] = ["sv", B.on("vector", lambda v: v.tensor_copy(
                rqj(0), pSq[:]), inc="sv")]
            chain_core(0)
            B.on("vector", lambda v: v.tensor_tensor(rowM4[0:1, NT:2 * NT], rmj(0), rowRw[0:1, 0:NT],
                                                     OP.mult), inc="sv")
            B.on("vector", lambda v, sl=sl: v.tensor_copy(p2row[0:1, sl],
                                                          rowM4[0:1, NT:2 * NT]), inc="sv")

        def CC1b(j):
            sl = slice(NT * j, NT * (j + 1))
            mk_r3 = bcastp(rowRw[0:1, 0:NT], trt(0))
            B.wait("vector", "sv", mk_r3)
            for c in range(NCH):
                tsl = NT * (c % 2)
                B.on("vector", lambda v, c=c, tsl=tsl: v.tensor_tensor(
                    t3s[:, tsl:tsl + NT], attr[:, NT * c:NT * (c + 1)],
                    trt(0), OP.mult), inc="sv")
                x2_marks[(j, c)] = B.on(
                    "vector", lambda v, c=c, sl=sl, tsl=tsl, j=j: v.tensor_tensor(
                        x2_(c, j), t3s[:, tsl:tsl + NT],
                        x_sb[c][:, sl], OP.add), inc="sv")

        def CC2a(j):
            sl = slice(NT * j, NT * (j + 1))
            B.wait("tensor", *stat_free)
            for c in range(NCH):
                B.wait("tensor", "sv", x2_marks[(j, c)])
                B.on("tensor", lambda t, c=c, j=j: t.matmul(
                    pSm[:], invccol[:], x2_(c, j),
                    start=(c == 0), stop=(c == NCH - 1)))
            mks4 = None
            for c in range(NCH):
                if (c % 3) in sq_used:
                    B.wait("scalar", *sq_used[c % 3])
                B.wait("scalar", "sv", x2_marks[(j, c)])
                mkv = B.on("scalar", lambda s, c=c, j=j: s.activation(
                    sq_slot(c), x2_(c, j), AF.Square), inc="ss")
                B.wait("tensor", "ss", mkv)
                mks4 = B.on("tensor", lambda t, c=c: t.matmul(
                    pSq[:], invccol[:], sq_slot(c),
                    start=(c == 0), stop=(c == NCH - 1)), inc="st")
                sq_used[c % 3] = ("st", mks4)
            B.wait("scalar", "st", mks4)
            stat_free[:] = ["ss", stats_evac_s(0)]
            chain_core(0)
            B.on("vector", lambda v: v.tensor_tensor(rowM4[0:1, NT:2 * NT], rmj(0), rowRw[0:1, 0:NT],
                                                     OP.mult), inc="sv")

        def CC2b(j):
            sl = slice(NT * j, NT * (j + 1))
            B.wait("vector", "sv", last_fin["sg"])
            bcastp(rowRw[0:1, 0:NT], trt(1))
            mk_m4 = bcastp(rowM4[0:1, NT:2 * NT], trtb4[:], negate=True)
            B.wait("vector", "sv", mk_m4)
            for c in range(NCH):
                tsl = NT * (c % 2)
                B.on("vector", lambda v, c=c, tsl=tsl, j=j: v.tensor_tensor(
                    t3s[:, tsl:tsl + NT], x2_(c, j), trt(1), OP.mult),
                    inc="sv")
                w_marks[(j, c)] = B.on(
                    "vector", lambda v, c=c, sl=sl, tsl=tsl: v.tensor_tensor(
                        w_sb[c][:, sl], t3s[:, tsl:tsl + NT], trtb4[:],
                        OP.subtract), inc="sv")
            if j - 1 in rk_last:
                B.wait("vector", "st", rk_last[j - 1])
                B.wait("gpsimd", "st", rk_last[j - 1])
            d_marks = {}
            for c in range(NCH):
                B.wait("gpsimd", "sv", w_marks[(j, c)])
                if j == 0:
                    B.on("vector", lambda v, c=c: v.tensor_copy(
                        attr[:, NT * c:NT * c + 1], w_sb[c][:, 0:1]))
                    d_marks[c] = B.on("gpsimd", lambda g, c=c: g.tensor_tensor(
                        attr[:, NT * c + 1:NT * (c + 1)], w_sb[c][:, 1:NT],
                        w_sb[c][:, 0:NT - 1], OP.subtract), inc="sg")
                else:
                    d_marks[c] = B.on("gpsimd", lambda g, c=c, j=j:
                                      g.tensor_tensor(
                                          attr[:, NT * c:NT * (c + 1)],
                                          w_sb[c][:, NT * j:NT * (j + 1)],
                                          w_sb[c][:, NT * j - 1:NT * (j + 1) - 1],
                                          OP.subtract), inc="sg")
            for c in range(NCH):
                B.wait("vector", "sg", d_marks[c])
                if j == 0:
                    B.on("vector", lambda v, c=c: v.tensor_scalar(
                        xr_(c)[:, 0:1], attr[:, NT * c:NT * c + 1],
                        murcol[:, c:c + 1], None, OP.mult))
                    mix_marks[(j, c)] = B.on(
                        "vector", lambda v, c=c: v.scalar_tensor_tensor(
                            xr_(c)[:, 1:NT], attr[:, NT * c + 1:NT * (c + 1)],
                            murcol[:, c:c + 1], w_sb[c][:, 0:NT - 1],
                            OP.mult, OP.add), inc="sv")
                else:
                    mix_marks[(j, c)] = B.on(
                        "vector", lambda v, c=c, j=j: v.scalar_tensor_tensor(
                            xr_(c), attr[:, NT * c:NT * (c + 1)],
                            murcol[:, c:c + 1],
                            w_sb[c][:, NT * j - 1:NT * (j + 1) - 1],
                            OP.mult, OP.add), inc="sv")
            for c in range(NCH):
                if j == 0:
                    B.on("vector", lambda v, c=c: v.tensor_scalar(
                        xk_(c)[:, 0:1], attr[:, NT * c:NT * c + 1],
                        mukcol[:, c:c + 1], None, OP.mult))
                    xk_marks[(j, c)] = B.on(
                        "vector", lambda v, c=c: v.scalar_tensor_tensor(
                            xk_(c)[:, 1:NT], attr[:, NT * c + 1:NT * (c + 1)],
                            mukcol[:, c:c + 1], w_sb[c][:, 0:NT - 1],
                            OP.mult, OP.add), inc="sv")
                else:
                    xk_marks[(j, c)] = B.on(
                        "vector", lambda v, c=c, j=j: v.scalar_tensor_tensor(
                            xk_(c), attr[:, NT * c:NT * (c + 1)],
                            mukcol[:, c:c + 1],
                            w_sb[c][:, NT * j - 1:NT * (j + 1) - 1],
                            OP.mult, OP.add), inc="sv")

        rk_last = {}
        rk_last = {}
        ffn_m = {}

        def _rk_groups(j, g0, g1, sig_m, ka_m):
            groups = ([("R", c) for c in range(NCH)]
                      + [("K", c) for c in range(NCH)])
            for g in range(g0, g1):
                kind, c = groups[g]
                bank = (pAux, pQ, pK)[g % 3]
                if j == 0:
                    if g == 1 and (0, 6) in att_all:
                        B.wait("tensor", "sv", att_all[(0, 6)])
                    if g == 2 and (0, 7) in att_all:
                        B.wait("tensor", "sv", att_all[(0, 7)])
                if g == 8 and (j + 1, 7) in att_all:
                    B.wait("tensor", "sv", att_all[(j + 1, 7)])
                if g == 10 and (j + 1, 6) in att_all:
                    B.wait("tensor", "sv", att_all[(j + 1, 6)])
                if g >= 3:
                    pk, pc = groups[g - 3]
                    if pk == "R":
                        B.wait("tensor", "ss", sig_m[pc])
                    else:
                        B.wait("tensor", "ss", ka_m[pc])
                W = Wr_sb if kind == "R" else Wk_sb
                xin = xr_ if kind == "R" else xk_
                mk = None
                for cc in range(NCH):
                    if g == 0:
                        B.wait("tensor", "sv", mix_marks[(j, cc)])
                    elif g == 1 and cc == 0:
                        B.wait("tensor", "sv", mix_marks[(j, NCH - 1)])
                    elif g == 8:
                        B.wait("tensor", "sv", xk_marks[(j, cc)])
                    elif g == 9 and cc == 0:
                        B.wait("tensor", "sv", xk_marks[(j, NCH - 1)])
                    mk = B.on("tensor", lambda t, c=c, cc=cc, bank=bank, W=W,
                              xin=xin: t.matmul(
                                  bank[:],
                                  W[:, C * cc + 128 * c:C * cc + 128 * (c + 1)],
                                  xin(cc), start=(cc == 0),
                                  stop=(cc == NCH - 1)),
                              inc="st" if cc == NCH - 1 else None) or mk
                if kind == "R":
                    B.wait("scalar", "st", mk)
                    sig_m[c] = B.on("scalar", lambda s, c=c, bank=bank:
                                    s.activation(rr_(c), bank[:], AF.Sigmoid),
                                    inc="ss")
                else:
                    B.wait("scalar", "st", mk)
                    if c == 4:
                        B.wait("scalar", "sv",
                               att_all[(min(j + 1, NJ - 1), NCH - 1)])
                    if c == 6:
                        B.wait("scalar", "sv", last_fin["sg"])
                    if (c % 3) in sq_used:
                        B.wait("scalar", *sq_used[c % 3])
                    B.on("scalar", lambda s, c=c, bank=bank:
                         s.activation(sq_slot(c), bank[:], AF.Relu),
                         inc="ss")
                    ka_m[c] = B.on("scalar", lambda s, c=c: s.activation(
                        ka_(c), sq_slot(c), AF.Square), inc="ss")
                    sq_used[c % 3] = ("ss", ka_m[c])
                if g == 2 * NCH - 1:
                    rk_last[j] = mk

        def EEa1A(j):
            sig_m = {}
            ka_m = {}
            ffn_m[j] = (sig_m, ka_m)
            _rk_groups(j, 0, 8, sig_m, ka_m)

        def EEa1B(j):
            sig_m, ka_m = ffn_m[j]
            _rk_groups(j, 8, 16, sig_m, ka_m)

        def EEa2a(j):
            sl = slice(NT * j, NT * (j + 1))
            sig_m, ka_m = ffn_m[j]
            B.wait("tensor", "ss", ka_m[NCH - 1])
            B.wait("tensor", "ss", sig_m[NCH - 1])
            B.wait("vector", "ss", B.mark("ss"))
            for c in range(NCH):
                bV = (pV, pQ, pK)[c % 3]
                if c == 0 and (j + 1, 5) in att_all:
                    B.wait("tensor", "sv", att_all[(j + 1, 5)])
                if c >= 3:
                    B.wait("tensor", "sv", o_all[(j, c - 3)])
                mkV = None
                for cc in range(NCH):
                    mkV = B.on("tensor", lambda t, c=c, cc=cc, bV=bV: t.matmul(
                        bV[:], Wv_sb[:, C * cc + 128 * c:C * cc + 128 * (c + 1)],
                        ka_(cc), start=(cc == 0), stop=(cc == NCH - 1)),
                        inc="st" if cc == NCH - 1 else None) or mkV
                B.wait("vector", "st", mkV)
                o_all[(j, c)] = B.on("vector", lambda v, c=c, bV=bV:
                                     v.tensor_tensor(
                                         oring[:, NT * c:NT * (c + 1)], bV[:],
                                         rr_(c), OP.mult), inc="sv")
        def EEa2b(j):
            sl = slice(NT * j, NT * (j + 1))
            B.wait("tensor", *stat_free)
            for c in range(NCH):
                B.wait("tensor", "sv", o_all[(j, c)])
                B.on("tensor", lambda t, c=c: t.matmul(
                    pSm[:], invccol[:], oring[:, NT * c:NT * (c + 1)],
                    start=(c == 0), stop=(c == NCH - 1)))
            mks5 = None
            for c in range(NCH):
                if (c % 3) in sq_used:
                    B.wait("scalar", *sq_used[c % 3])
                B.wait("scalar", "sv", o_all[(j, c)])
                mkv = B.on("scalar", lambda s, c=c: s.activation(
                    sq_slot(c), oring[:, NT * c:NT * (c + 1)], AF.Square),
                    inc="ss")
                B.wait("tensor", "ss", mkv)
                mks5 = B.on("tensor", lambda t, c=c: t.matmul(
                    pSq[:], invccol[:], sq_slot(c),
                    start=(c == 0), stop=(c == NCH - 1)), inc="st")
                sq_used[c % 3] = ("st", mks5)
            B.wait("scalar", "st", mks5)
            stat_free[:] = ["ss", stats_evac_s(0)]
            chain_core(0)
            B.on("vector", lambda v: v.tensor_tensor(rowM4[0:1, NT:2 * NT], rmj(0), rowRw[0:1, 0:NT],
                                                     OP.mult), inc="sv")
            B.on("vector", lambda v, sl=sl: v.tensor_tensor(
                rowM4[0:1, NT:2 * NT], rowM4[0:1, NT:2 * NT], p2row[0:1, sl], OP.add), inc="sv")

        def EEa2c(j):
            bcastp(rowRw[0:1, 0:NT], trt(1))
            return bcastp(rowM4[0:1, NT:2 * NT], trtb4[:], negate=True)

        def EEb(j, mk_cl):
            sl = slice(NT * j, NT * (j + 1))
            B.wait("vector", "sv", mk_cl)
            B.wait("gpsimd", "sv", mk_cl)
            last = None
            fin_slot = {}
            for c in range(NCH):
                slot = c % 2
                tA = NT * (2 + c % 2)
                if c % 2 == 0:
                    if slot in fin_slot:
                        B.wait("gpsimd", "sv", fin_slot[slot])
                    B.on("gpsimd", lambda g, c=c, tA=tA: g.tensor_tensor(
                        t5r[:, tA:tA + NT], oring[:, NT * c:NT * (c + 1)],
                        trt(1), OP.mult), inc="sg")
                    mg = B.on("gpsimd", lambda g, tA=tA: g.tensor_tensor(
                        t5r[:, tA:tA + NT], t5r[:, tA:tA + NT],
                        trtb4[:], OP.subtract), inc="sg")
                    B.wait("vector", "sg", mg)
                else:
                    B.on("vector", lambda v, c=c, tA=tA: v.tensor_tensor(
                        t5r[:, tA:tA + NT], oring[:, NT * c:NT * (c + 1)],
                        trt(1), OP.mult), inc="sv")
                    B.on("vector", lambda v, tA=tA: v.tensor_tensor(
                        t5r[:, tA:tA + NT], t5r[:, tA:tA + NT],
                        trtb4[:], OP.subtract), inc="sv")
                if slot in out_dma_marks:
                    B.wait("vector", "dO", out_dma_marks[slot])
                last = B.on("vector", lambda v, c=c, slot=slot, tA=tA, j=j:
                            v.tensor_tensor(
                                outr[:, NT * slot:NT * (slot + 1)],
                                t5r[:, tA:tA + NT],
                                x2_(c, j), OP.add), inc="sv")
                fin_slot[slot] = last
                B.wait("sync", "sv", last)
                out_dma_marks[slot] = B.dma(
                    "sync", yT[128 * c:128 * (c + 1), sl],
                    outr[:, NT * slot:NT * (slot + 1)], "dO")
            last_fin["sg"] = last

        CC1a(0)
        CC1c(0)
        CC1b(0)
        CC2a(0)
        CC2b(0)
        EEa1A(0)
        CC1a(1)
        CC1c(1)
        EEa1B(0)
        CC1b(1)
        CC2a(1)
        EEa2a(0)
        CC2b(1)
        EEa2b(0)
        EEa1A(1)
        CC1a(2)
        cl0 = EEa2c(0)
        CC1c(2)
        EEb(0, cl0)
        EEa1B(1)
        CC1b(2)
        CC2a(2)
        EEa2a(1)
        CC2b(2)
        EEa2b(1)
        EEa1A(2)
        CC1a(3)
        cl1 = EEa2c(1)
        CC1c(3)
        EEb(1, cl1)
        EEa1B(2)
        CC1b(3)
        CC2a(3)
        EEa2a(2)
        CC2b(3)
        EEa2b(2)
        EEa1A(3)
        cl2 = EEa2c(2)
        EEb(2, cl2)
        EEa1B(3)
        EEa2a(3)
        EEa2b(3)
        cl3 = EEa2c(3)
        EEb(3, cl3)

        if debug:
            B.wait("sync", "sv", B.mark("sv"))
            for c in range(NCH):
                B.dma("sync", dbg["d_x2"][128 * c:128 * (c + 1), 0:2 * NT],
                      x2_sb[c][:], "dO")
                B.dma("sync", dbg["d_w"][128 * c:128 * (c + 1), :], w_sb[c][:],
                      "dO")
            B.dma("sync", dbg["d_or"], oring[:], "dO")
            B.dma("sync", dbg["d_xr"], k_sb[:], "dO")
            B.dma("sync", dbg["d_xk"], v_sb[:], "dO")
            B.dma("sync", dbg["d_xk2"], vtm[:], "dO")
            B.dma("sync", dbg["d_rr"], Etm[:], "dO")
            B.dma("sync", dbg["d_ka"], outh[:], "dO")
        B.wait("sync", "dO", B.mark("dO"))

        block = E_(nc.Block())

        @block.sync
        def _(e):
            for fn in B.ops["sync"]:
                fn(e)

        @block.gpsimd
        def _(e):
            for fn in B.ops["gpsimd"]:
                fn(e)

        @block.vector
        def _(e):
            for fn in B.ops["vector"]:
                fn(e)

        @block.scalar
        def _(e):
            for fn in B.ops["scalar"]:
                fn(e)

        @block.tensor
        def _(e):
            for fn in B.ops["tensor"]:
                fn(e)

    return nc


_NC_CACHE = {}


def _get_nc(debug=False):
    if debug not in _NC_CACHE:
        _NC_CACHE[debug] = build_nc(debug)
    return _NC_CACHE[debug]


def kernel(L0_attn_x, L0_attn_gamma, L0_attn_beta, L0_attn_w_qkv, L0_attn_w_out,
           L0_attn_b_out, L0_ffn_W_r, L0_ffn_W_k, L0_ffn_W_v, L0_ffn_mu_r,
           L0_ffn_mu_k, _debug=False):
    x = np.ascontiguousarray(L0_attn_x, dtype=np.float32)
    shared = {
        "w_qkv": np.ascontiguousarray(L0_attn_w_qkv, np.float32),
        "w_out": np.ascontiguousarray(L0_attn_w_out, np.float32),
        "WrT": np.ascontiguousarray(np.asarray(L0_ffn_W_r, np.float32).T),
        "WkT": np.ascontiguousarray(np.asarray(L0_ffn_W_k, np.float32).T),
        "WvT": np.ascontiguousarray(np.asarray(L0_ffn_W_v, np.float32).T),
        "vgamma": np.ascontiguousarray(np.asarray(L0_attn_gamma, np.float32).ravel()),
        "vbeta": np.ascontiguousarray(np.asarray(L0_attn_beta, np.float32).ravel()),
        "vbout": np.ascontiguousarray(np.asarray(L0_attn_b_out, np.float32).ravel()),
        "vmur": np.ascontiguousarray(L0_ffn_mu_r, np.float32),
        "vmuk": np.ascontiguousarray(L0_ffn_mu_k, np.float32),
    }
    in_maps = []
    for b in range(8):
        m = dict(shared)
        m["xT"] = np.ascontiguousarray(x[b].T)
        in_maps.append(m)
    nc = _get_nc(_debug)
    res = run_bass_kernel_spmd(nc, in_maps, core_ids=list(range(8)))
    out = np.stack([np.asarray(r["yT"]).T for r in res.results])
    if _debug:
        return out, res.results
    return out


# version 39
# speedup vs baseline: 1.0337x; 1.0259x over previous
import contextlib
import numpy as np
import concourse.bass as bass
import concourse.mybir as mybir
from concourse.bass_utils import run_bass_kernel_spmd
from concourse.alu_op_type import AluOpType as OP

F32 = mybir.dt.float32
BF16 = mybir.dt.bfloat16
AF = mybir.ActivationFunctionType
AX = mybir.AxisListType

T = 2048
C = 1024
NCH = 8
NJ = 4
NT = 512
EPS = 1e-5
INV_C = 1.0 / C

ENGINES = ("sync", "gpsimd", "vector", "scalar", "tensor")


class Builder:
    def __init__(self, nc, sems):
        self.nc = nc
        self.sems = sems
        self.cnt = {k: 0 for k in sems}
        self.ops = {e: [] for e in ENGINES}

    def on(self, eng, fn, inc=None):
        sem = self.sems[inc] if inc else None
        if inc:
            self.cnt[inc] += 1

        def run(e, fn=fn, sem=sem):
            inst = fn(e)
            if sem is not None:
                inst.then_inc(sem, 1)
        self.ops[eng].append(run)
        return self.cnt[inc] if inc else None

    def dma(self, eng, out, in_, inc, slow=False):
        self.cnt[inc] += 16
        sem = self.sems[inc]

        def run(e, out=out, in_=in_, sem=sem, slow=slow):
            kw = dict(allow_slow_non_contiguous=True) if slow else {}
            e.dma_start(out=out, in_=in_, **kw).then_inc(sem, 16)
        self.ops[eng].append(run)
        return self.cnt[inc]

    def mark(self, sem):
        return self.cnt[sem]

    def wait(self, eng, sem, val=None):
        if val is None:
            val = self.cnt[sem]
        s = self.sems[sem]

        def run(e, s=s, val=val):
            e.wait_ge(s, val)
        self.ops[eng].append(run)


def build_nc(debug=False):
    nc = bass.Bass("TRN2", target_bir_lowering=False)

    xT = nc.dram_tensor("xT", [C, T], F32, kind="ExternalInput").ap()
    w_qkv = nc.dram_tensor("w_qkv", [C, 384], F32, kind="ExternalInput").ap()
    w_out = nc.dram_tensor("w_out", [128, C], F32, kind="ExternalInput").ap()
    WrT = nc.dram_tensor("WrT", [C, C], F32, kind="ExternalInput").ap()
    WkT = nc.dram_tensor("WkT", [C, C], F32, kind="ExternalInput").ap()
    WvT = nc.dram_tensor("WvT", [C, C], F32, kind="ExternalInput").ap()
    vgamma = nc.dram_tensor("vgamma", [C], F32, kind="ExternalInput").ap()
    vbeta = nc.dram_tensor("vbeta", [C], F32, kind="ExternalInput").ap()
    vbout = nc.dram_tensor("vbout", [C], F32, kind="ExternalInput").ap()
    vmur = nc.dram_tensor("vmur", [C], F32, kind="ExternalInput").ap()
    vmuk = nc.dram_tensor("vmuk", [C], F32, kind="ExternalInput").ap()
    yT = nc.dram_tensor("yT", [C, T], F32, kind="ExternalOutput").ap()

    dbg = {}
    if debug:
        for nm, shp, dt in [("d_x2", [C, T], BF16), ("d_w", [C, T], BF16),
                            ("d_or", [128, NCH * NT], BF16),
                            ("d_xr", [128, T], BF16), ("d_xk", [128, T], BF16),
                            ("d_xk2", [128, T], BF16), ("d_rr", [128, T], BF16),
                            ("d_ka", [128, T], BF16)]:
            dbg[nm] = nc.dram_tensor(nm, shp, dt, kind="ExternalOutput").ap()

    with contextlib.ExitStack() as ST:
        E_ = ST.enter_context

        x_sb = [E_(nc.sbuf_tensor(f"x{c}", [128, T], BF16)) for c in range(NCH)]
        x2_sb = [E_(nc.sbuf_tensor(f"x2_{c}", [128, 2 * NT], BF16)) for c in range(NCH)]
        q_sb = E_(nc.sbuf_tensor("q", [128, T], BF16))
        k_sb = E_(nc.sbuf_tensor("k", [128, T], BF16))
        v_sb = E_(nc.sbuf_tensor("v", [128, T], BF16))
        Eb = E_(nc.sbuf_tensor("Eb", [128, T], BF16))
        Etm = E_(nc.sbuf_tensor("Etm", [128, T], BF16))
        vtm = E_(nc.sbuf_tensor("vtm", [128, T], BF16))
        outh = E_(nc.sbuf_tensor("outh", [128, T], BF16))
        t5r = E_(nc.sbuf_tensor("t5r", [128, T], BF16))
        Wr_sb = E_(nc.sbuf_tensor("Wr", [128, NCH * C], BF16))
        Wk_sb = E_(nc.sbuf_tensor("Wk", [128, NCH * C], BF16))
        Wv_sb = E_(nc.sbuf_tensor("Wv", [128, NCH * C], BF16))
        woutb = E_(nc.sbuf_tensor("woutb", [128, C], BF16))
        attr = E_(nc.sbuf_tensor("attr", [128, NCH * NT], BF16))
        oring = E_(nc.sbuf_tensor("oring", [128, NCH * NT], BF16))
        sqr = E_(nc.sbuf_tensor("sqr", [128, 3 * NT], BF16))
        t3s = E_(nc.sbuf_tensor("t3s", [128, 2 * NT], BF16))
        rrB = E_(nc.sbuf_tensor("rrB", [128, 4 * NT], BF16))

        def _alias(name, backing):
            t_ = nc.virtual_sbuf_tensor(name, [128, T], BF16)
            nc.back_virtual_tensor_with(t_, backing)
            return t_
        w_sb = [_alias(f"w{c}", x_sb[c]) for c in range(NCH)]
        xrA = _alias("xrA", k_sb)
        xrB = _alias("xrB", Eb)
        xkA = _alias("xkA", v_sb)
        xkB = _alias("xkB", vtm)
        rrA = _alias("rrA", Etm)
        kaA = _alias("kaA", outh)
        kaB = _alias("kaB", t5r)

        def _seg(tA, tB, c):
            t_ = tA if c < 4 else tB
            return t_[:, NT * (c % 4):NT * (c % 4 + 1)]

        def x2_(c, j):
            return x2_sb[c][:, NT * (j % 2):NT * (j % 2 + 1)]

        def xr_(c):
            return _seg(xrA, xrB, c)

        def xk_(c):
            return _seg(xkA, xkB, c)

        def rr_(c):
            if c < 4:
                return rrA[:, NT * c:NT * (c + 1)]
            return rrB[:, NT * (c - 4):NT * (c - 3)]

        def ka_(c):
            return _seg(kaA, kaB, c)

        rsrow = E_(nc.sbuf_tensor("rsrow", [1, T], BF16))
        rowM4 = E_(nc.sbuf_tensor("rowM4", [1, T], F32))
        rowQ4 = E_(nc.sbuf_tensor("rowQ4", [1, T], F32))
        wT0 = E_(nc.sbuf_tensor("wT0", [1, T], F32))
        rowRw = E_(nc.sbuf_tensor("rowRw", [1, T], F32))
        mrs1row = E_(nc.sbuf_tensor("mrs1row", [1, T], BF16))
        p2row = E_(nc.sbuf_tensor("p2row", [1, T], BF16))
        s4hi = p2row

        outr = E_(nc.sbuf_tensor("outr", [128, 2 * NT], F32))
        TRT = E_(nc.sbuf_tensor("TRT", [128, 2 * NT], BF16))
        trtb4 = E_(nc.sbuf_tensor("trtb4", [128, NT], BF16))

        def trt(i):
            return TRT[:, NT * i:NT * (i + 1)]

        gcol = E_(nc.sbuf_tensor("gcol", [128, 8], F32))
        bcol = E_(nc.sbuf_tensor("bcol", [128, 8], F32))
        bocol = E_(nc.sbuf_tensor("bocol", [128, 8], F32))
        murcol = E_(nc.sbuf_tensor("murcol", [128, 8], F32))
        mukcol = E_(nc.sbuf_tensor("mukcol", [128, 8], F32))
        bcolb2 = E_(nc.sbuf_tensor("bcolb2", [128, 16], BF16, align_bytes=4))
        onescol = E_(nc.sbuf_tensor("onescol", [128, 1], BF16, align_bytes=4))
        invccol = E_(nc.sbuf_tensor("invccol", [128, 1], BF16, align_bytes=4))
        ones1 = E_(nc.sbuf_tensor("ones1", [1, 128], BF16, align_bytes=4))
        bocolb = E_(nc.sbuf_tensor("bocolb", [128, 8], BF16, align_bytes=4))
        cb_sb = E_(nc.sbuf_tensor("cb_sb", [128, 128], BF16))
        idb = nc.virtual_sbuf_tensor("idb", [128, 128], BF16)
        nc.back_virtual_tensor_with(idb, cb_sb)
        io_c = E_(nc.sbuf_tensor("io_c", [128, 128], BF16))
        maskb = nc.virtual_sbuf_tensor("maskb", [128, 128], BF16)
        nc.back_virtual_tensor_with(maskb, io_c)
        io_r = E_(nc.sbuf_tensor("io_r", [128, 1], F32, align_bytes=4))
        rspr = E_(nc.sbuf_tensor("rspr", [128, 1], F32, align_bytes=4))
        kmax = E_(nc.sbuf_tensor("kmax", [128, 1], F32, align_bytes=4))
        ssum = E_(nc.sbuf_tensor("ssum", [128, 1], F32, align_bytes=4))
        rwcol = E_(nc.sbuf_tensor("rwcol", [128, 1], BF16, align_bytes=4))
        rwf = E_(nc.sbuf_tensor("rwf", [128, 1], F32, align_bytes=4))
        bbar = E_(nc.sbuf_tensor("bbar", [1, 1], F32, align_bytes=4))
        epsc = E_(nc.sbuf_tensor("epsc", [1, 1], F32, align_bytes=4))
        pb8 = E_(nc.sbuf_tensor("pb8", [1, 8], F32, align_bytes=4))

        pQ = E_(nc.psum_tensor("pQ", [128, NT], F32))
        pK = E_(nc.psum_tensor("pK", [128, NT], F32))
        pV = E_(nc.psum_tensor("pV", [128, NT], F32))
        pSm = E_(nc.psum_tensor("pSm", [1, NT], F32))
        pSq = E_(nc.psum_tensor("pSq", [1, NT], F32))
        pB = E_(nc.psum_tensor("pB", [128, NT], F32))
        pAux = E_(nc.psum_tensor("pAux", [128, NT], F32))
        pT = E_(nc.psum_tensor("pT", [128, 1024], BF16))

        sems = {}
        for s in ("dX", "dW", "dM", "dO", "sv", "ss", "st", "sg"):
            sems[s] = E_(nc.semaphore(s))
        B = Builder(nc, sems)

        for c in range(NCH):
            B.dma("gpsimd", x_sb[c][:], xT[128 * c:128 * c + 128, :], "dX")
        mark_x = B.mark("dX")
        for dst, src in ((gcol, vgamma), (bcol, vbeta), (bocol, vbout),
                         (murcol, vmur), (mukcol, vmuk)):
            B.dma("sync", dst[:], src.rearrange("(a p) -> p a", p=128), "dM",
                  slow=True)
        mark_vecs = B.mark("dM")
        B.dma("gpsimd", Wr_sb[:, 0:NCH * 384].rearrange("p (a k) -> p a k", k=384),
              w_qkv.rearrange("(a p) k -> p a k", p=128), "dW")
        B.dma("gpsimd", woutb[:], w_out, "dW")
        mark_wqkv = B.mark("dW")
        B.dma("gpsimd", Wv_sb[:].rearrange("p (a k) -> p a k", k=C),
              WvT.rearrange("(a p) k -> p a k", p=128), "dW")

        B.on("gpsimd", lambda g: g.iota(io_c[:], pattern=[[1, 128]], base=0,
                                        channel_multiplier=0,
                                        allow_small_or_imprecise_dtypes=True))
        mark_iota = B.on("gpsimd", lambda g: g.iota(
            io_r[:], pattern=[[0, 1]], base=0, channel_multiplier=1,
            allow_small_or_imprecise_dtypes=True), inc="sg")

        B.on("vector", lambda v: v.memset(onescol[:], 1.0))
        B.on("vector", lambda v: v.memset(invccol[:], INV_C))
        B.on("vector", lambda v: v.memset(ones1[:], 1.0))
        B.on("vector", lambda v: v.memset(epsc[:], EPS))
        B.on("vector", lambda v: v.memset(bcolb2[:], 0.0))
        B.wait("vector", "sg", mark_iota)
        mark_const = B.on("vector", lambda v: v.tensor_scalar(
            idb[:], io_c[:], io_r[:], None, OP.is_equal), inc="sv")
        B.on("vector", lambda v: v.memset(maskb[:], 0.0))
        for h in range(4):
            B.on("vector", lambda v, h=h: v.memset(
                maskb[32 * h:32 * h + 32, 32 * h:32 * h + 32], 32.0 ** -0.5))

        B.wait("vector", "dW", mark_wqkv)
        B.wait("vector", "dM", mark_vecs)
        for c in range(NCH):
            B.on("vector", lambda v, c=c: v.tensor_scalar(
                Wk_sb[:, 384 * c:384 * (c + 1)], Wr_sb[:, 384 * c:384 * (c + 1)],
                gcol[:, c:c + 1], None, OP.mult), inc="sv")
        B.on("vector", lambda v: v.tensor_copy(
            bcolb2[:].rearrange("p (a two) -> p a two", two=2)[:, :, 0:1],
            bcol[:].rearrange("p (a one) -> p a one", one=1)), inc="sv")
        mk_rwf = B.on("vector", lambda v: v.reduce_sum(rwf[:], woutb[:],
                                                       axis=AX.X), inc="sv")
        B.on("vector", lambda v: v.tensor_copy(bocolb[:], bocol[:]), inc="sv")
        B.wait("scalar", "sv", mk_rwf)
        B.on("scalar", lambda s: s.mul(rwcol[:], rwf[:], INV_C), inc="ss")
        mark_prep = B.mark("sv")
        mark_prep_s = B.mark("ss")

        B.wait("tensor", "sv", mark_prep)
        mark_bqcw = None
        for c in range(NCH):
            B.on("tensor", lambda t, c=c: t.matmul(
                pSm[0:1, 0:384], bcolb2[:, 2 * c:2 * c + 1],
                Wr_sb[:, 384 * c:384 * (c + 1)], start=(c == 0),
                stop=(c == NCH - 1)))
            mark_bqcw = B.on("tensor", lambda t, c=c: t.matmul(
                pSq[0:1, 0:384], onescol[:], Wk_sb[:, 384 * c:384 * (c + 1)],
                start=(c == 0), stop=(c == NCH - 1)),
                inc="st" if c == NCH - 1 else None) or mark_bqcw
        B.wait("vector", "st", mark_bqcw)
        B.on("vector", lambda v: v.tensor_copy(attr[0:1, 0:384], pSm[0:1, 0:384]),
             inc="sv")
        B.on("vector", lambda v: v.tensor_copy(attr[0:1, 384:768], pSq[0:1, 0:384]),
             inc="sv")
        mark_cwbq = B.mark("sv")
        B.wait("tensor", "sv", mark_cwbq)
        mark_bo = B.on("tensor", lambda t: t.matmul(
            pSm[0:1, 0:8], onescol[:], bocolb[:], start=True, stop=True),
            inc="st")
        B.wait("vector", "st", mark_bo)
        mk_pb8 = B.on("vector", lambda v: v.reduce_sum(pb8[:, 0:1],
                                                       pSm[0:1, 0:8],
                                                       axis=AX.X), inc="sv")
        B.wait("scalar", "sv", mk_pb8)
        B.on("scalar", lambda s: s.mul(bbar[:], pb8[:, 0:1], INV_C), inc="ss")
        B.wait("gpsimd", "st", mark_bo)
        B.wait("gpsimd", "sv", mark_prep)
        B.dma("gpsimd", Wr_sb[:].rearrange("p (a k) -> p a k", k=C),
              WrT.rearrange("(a p) k -> p a k", p=128), "dW")

        sq_used = {}
        ob_used = sq_used

        def sq_slot(c):
            return sqr[:, NT * (c % 3):NT * (c % 3 + 1)]

        def ob_slot(c):
            return sq_slot(c)

        def rmj(j):
            return rowM4[0:1, 0:NT]

        def rqj(j):
            return rowQ4[0:1, 0:NT]

        def chain_core(j=0):
            B.wait("scalar", "sv", B.mark("sv"))
            B.on("scalar", lambda s, j=j: s.activation(wT0[0:1, 0:NT], rmj(j),
                                                       AF.Square), inc="ss")
            B.wait("vector", "ss", B.mark("ss"))
            mkv = B.on("vector", lambda v, j=j: v.tensor_tensor(
                rowM4[0:1, NT:2 * NT], rqj(j), wT0[0:1, 0:NT], OP.subtract), inc="sv")
            B.wait("scalar", "sv", mkv)
            B.on("scalar", lambda s: s.activation(
                wT0[0:1, 0:NT], rowM4[0:1, NT:2 * NT], AF.Ln, bias=epsc[:], scale=1.0), inc="ss")
            mk = B.on("scalar", lambda s: s.activation(
                rowRw[0:1, 0:NT], wT0[0:1, 0:NT], AF.Exp, scale=-0.5), inc="ss")
            B.wait("vector", "ss", mk)
            return mk

        def stats_evac_s(j=0):
            B.wait("scalar", "sv", B.mark("sv"))
            B.on("scalar", lambda s, j=j: s.copy(rmj(j), pSm[:]))
            return B.on("scalar", lambda s, j=j: s.copy(rqj(j), pSq[:]),
                        inc="ss")

        def stats_evac(j=0):
            B.wait("vector", "ss", B.mark("ss"))
            B.on("vector", lambda v, j=j: v.tensor_copy(rmj(j), pSm[:]),
                 inc="sv")
            return B.on("vector", lambda v, j=j: v.tensor_copy(rqj(j), pSq[:]),
                        inc="sv")

        def bcast(src_row, dst_tile, negate=False):
            mk = B.on("vector", lambda v: v.tensor_copy(rsrow[0:1, 0:NT], src_row),
                      inc="sv")
            B.wait("tensor", "sv", mk)
            mk2 = B.on("tensor", lambda t: t.matmul(pB[:], ones1[:],
                                                    rsrow[0:1, 0:NT],
                                                    start=True, stop=True),
                       inc="st")
            B.wait("vector", "st", mk2)
            return B.on("vector", lambda v, dst=dst_tile: v.tensor_copy(
                dst, pB[:]), inc="sv")

        mark_setup_ss = B.mark("ss")
        B.wait("vector", "dX", mark_x)
        B.wait("scalar", "dX", mark_x)
        B.wait("tensor", "dX", mark_x)
        B.wait("tensor", "sv", mark_cwbq)
        B.wait("tensor", "ss", mark_setup_ss)
        B.wait("vector", "ss", mark_setup_ss)
        qkv_out = (q_sb, k_sb, v_sb)
        evac_marks = {}
        stA_evac = {}

        def stats_j(j):
            sl = slice(NT * j, NT * (j + 1))
            if j == 0:
                B.wait("tensor", "sv", mk_pb8)
            else:
                B.wait("tensor", "ss", stA_evac[j - 1])
            mkm = None
            for c in range(NCH):
                mkm = B.on("tensor", lambda t, c=c, sl=sl: t.matmul(
                    pSm[:], invccol[:], x_sb[c][:, sl],
                    start=(c == 0), stop=(c == NCH - 1)),
                    inc="st" if c == NCH - 1 else None) or mkm
            mks = None
            for c in range(NCH):
                if (c % 3) in sq_used:
                    B.wait("scalar", *sq_used[c % 3])
                mkv = B.on("scalar", lambda s, c=c, sl=sl: s.activation(
                    sq_slot(c), x_sb[c][:, sl], AF.Square), inc="ss")
                B.wait("tensor", "ss", mkv)
                mks = B.on("tensor", lambda t, c=c: t.matmul(
                    pSq[:], invccol[:], sq_slot(c),
                    start=(c == 0), stop=(c == NCH - 1)), inc="st")
                sq_used[c % 3] = ("st", mks)
            B.wait("scalar", "st", mks)
            B.on("scalar", lambda s, sl=sl: s.copy(rowM4[0:1, sl], pSm[:]))
            stA_evac[j] = B.on("scalar", lambda s, sl=sl: s.copy(
                rowQ4[0:1, sl], pSq[:]), inc="ss")

        def chain_wide():
            mk0 = B.on("scalar", lambda s: s.activation(
                wT0[:], rowM4[:], AF.Square), inc="ss")
            B.wait("vector", "ss", mk0)
            mkv = B.on("vector", lambda v: v.tensor_tensor(
                rowQ4[:], rowQ4[:], wT0[:], OP.subtract), inc="sv")
            B.wait("scalar", "sv", mkv)
            B.on("scalar", lambda s: s.activation(
                wT0[:], rowQ4[:], AF.Ln, bias=epsc[:], scale=1.0), inc="ss")
            mke = B.on("scalar", lambda s: s.activation(
                rowRw[:], wT0[:], AF.Exp, scale=-0.5), inc="ss")
            B.wait("vector", "ss", mke)
            B.on("vector", lambda v: v.tensor_tensor(
                mrs1row[0:1, :], rowM4[:], rowRw[:], OP.mult), inc="sv")
            B.on("vector", lambda v: v.tensor_scalar(
                oring[0:1, 0:T], rowM4[:], -1.0, None, OP.mult), inc="sv")
            mku = B.on("vector", lambda v: v.tensor_scalar(
                rowM4[:], rowQ4[:], 1.0 + EPS, EPS * EPS, OP.mult, OP.add),
                inc="sv")
            B.wait("scalar", "sv", mku)
            B.on("scalar", lambda s: s.activation(
                wT0[:], rowM4[:], AF.Ln), inc="ss")
            mk_s4 = B.on("scalar", lambda s: s.activation(
                s4hi[0:1, :], wT0[:], AF.Exp, scale=-0.5), inc="ss")
            mk_rows = B.on("scalar", lambda s: s.activation(
                rsrow[0:1, :], wT0[:], AF.Exp, scale=0.5), inc="ss")
            return mk_s4, mk_rows

        def Gbig(j):
            sl = slice(NT * j, NT * (j + 1))
            for oi, psum in enumerate((pQ, pK, pV)):
                osl = slice(128 * oi, 128 * (oi + 1))
                if (j - 1, oi) in evac_marks:
                    B.wait("tensor", "sv", evac_marks[(j - 1, oi)])
                for c in range(NCH):
                    B.on("tensor", lambda t, c=c, osl=osl, sl=sl, psum=psum:
                         t.matmul(psum[:],
                                  Wk_sb[:, 384 * c + osl.start:384 * c + osl.stop],
                                  x_sb[c][:, sl], start=(c == 0), stop=False))

        def sbcast(j, mk_s4=None):
            if mk_s4 is not None:
                B.wait("tensor", "ss", mk_s4)
            if last_bcastA[0] is not None:
                B.wait("tensor", "sv", last_bcastA[0])
            mkb = B.on("tensor", lambda t, j=j: t.matmul(
                pB[:], ones1[:], s4hi[0:1, NT * j:NT * (j + 1)],
                start=True, stop=True), inc="st")
            B.wait("vector", "st", mkb)
            last_bcastA[0] = B.on("vector", lambda v, j=j: v.tensor_copy(
                trt(j % 2), pB[:]), inc="sv")

        def Gtail(j, mk_rows=None):
            sl = slice(NT * j, NT * (j + 1))
            jj4 = NT * j
            if mk_rows is not None:
                B.wait("tensor", "ss", mk_rows)
            for oi, (psum, sbout) in enumerate(zip((pQ, pK, pV), qkv_out)):
                osl = slice(128 * oi, 128 * (oi + 1))
                B.on("tensor", lambda t, osl=osl, jj4=jj4, psum=psum: t.matmul(
                    psum[:], attr[0:1, 384 + osl.start:384 + osl.stop], oring[0:1, jj4:jj4 + NT],
                    start=False, stop=False))
                mkq = B.on("tensor", lambda t, osl=osl, jj4=jj4, psum=psum:
                           t.matmul(psum[:], attr[0:1, osl.start:osl.stop],
                                    rsrow[0:1, jj4:jj4 + NT],
                                    start=False, stop=True), inc="st")
                B.wait("vector", "st", mkq)
                evac_marks[(j, oi)] = B.on(
                    "vector", lambda v, sbout=sbout, sl=sl, psum=psum, j=j:
                    v.tensor_tensor(sbout[:, sl], psum[:], trt(j % 2), OP.mult),
                    inc="sv")

        last_bcastA = [None]
        stats_j(0)
        Gbig(0)
        stats_j(1)
        stats_j(2)
        stats_j(3)
        mk_s4, mk_rows = chain_wide()
        sbcast(0, mk_s4)
        Gtail(0, mk_rows)
        sbcast(1)
        Gbig(1)
        Gtail(1)
        sbcast(2)
        Gbig(2)
        Gtail(2)
        sbcast(3)
        Gbig(3)
        Gtail(3)
        mark_qkv = B.mark("sv")
        B.wait("gpsimd", "sv", mark_qkv)
        B.dma("gpsimd", Wk_sb[:].rearrange("p (a k) -> p a k", k=C),
              WkT.rearrange("(a p) k -> p a k", p=128), "dW")
        mark_wffn = B.mark("dW")

        B.wait("vector", "sv", mark_qkv)
        B.on("vector", lambda v: v.reduce_max(kmax[:], k_sb[:], axis=AX.X,
                                              negate=True), inc="sv")
        B.wait("scalar", "sv", B.mark("sv"))
        mark_E = B.on("scalar", lambda s: s.activation(
            Eb[:], k_sb[:], AF.Exp, bias=kmax[:], scale=1.0),
            inc="ss")
        B.wait("vector", "ss", mark_E)
        B.on("vector", lambda v: v.reduce_sum(ssum[:], Eb[:], axis=AX.X),
             inc="sv")
        B.wait("tensor", "ss", mark_E)
        B.wait("tensor", "sv", mark_const)
        for src, dst in ((Eb, Etm), (v_sb, vtm)):
            for g in range(2):
                B.wait("tensor", "sv", B.mark("sv"))
                mkt = None
                for b_ in range(8):
                    blk = 8 * g + b_
                    mkt = B.on("tensor", lambda t, src=src, b_=b_, blk=blk:
                               t.transpose(pT[:, 128 * b_:128 * (b_ + 1)],
                                           src[:, 128 * blk:128 * (blk + 1)],
                                           idb[:]),
                               inc="st" if b_ == 7 else None) or mkt
                B.wait("vector", "st", mkt)
                B.on("vector", lambda v, dst=dst, g=g: v.tensor_copy(
                    dst[:, 1024 * g:1024 * (g + 1)], pT[:]), inc="sv")
        mark_tr = B.mark("sv")
        B.wait("tensor", "sv", mark_tr)
        mark_ctx = None
        for blk in range(16):
            mark_ctx = B.on("tensor", lambda t, blk=blk: t.matmul(
                pAux[:, 0:128], Etm[:, 128 * blk:128 * (blk + 1)],
                vtm[:, 128 * blk:128 * (blk + 1)],
                start=(blk == 0), stop=(blk == 15)),
                inc="st" if blk == 15 else None) or mark_ctx
        B.on("vector", lambda v: v.reciprocal(rspr[:], ssum[:]), inc="sv")
        B.wait("vector", "st", mark_ctx)
        mark_cb = B.on("vector", lambda v: v.scalar_tensor_tensor(
            cb_sb[:], pAux[:, 0:128], rspr[:], maskb[:], OP.mult, OP.mult),
            inc="sv")

        B.wait("tensor", "dW", mark_wffn)
        last_bcast = {"evac": None}

        def bcastp(src_row, dst_tile, negate=False):
            if last_bcast["evac"] is not None:
                B.wait("tensor", *last_bcast["evac"])
            mkv = bcast(src_row, dst_tile, negate)
            last_bcast["evac"] = ("sv", mkv)
            return mkv

        att_all = {}
        outh_evac = {}
        stat_free = ["ss", stA_evac[3]]
        x2_marks = {}
        w_marks = {}
        mix_marks = {}
        xk_marks = {}
        o_all = {}
        out_dma_marks = {}
        m3_marks = {}
        last_fin = {"sg": 0}

        def CC1a(j):
            sl = slice(NT * j, NT * (j + 1))
            if j == 0:
                B.wait("tensor", "sv", mark_cb)
            if j - 1 in outh_evac:
                B.wait("tensor", "sv", outh_evac[j - 1])
            if last_bcast["evac"] is not None:
                B.wait("tensor", *last_bcast["evac"])
            if (j - 2, NCH - 1) in o_all:
                B.wait("tensor", "sv", o_all[(j - 2, NCH - 1)])
            mko = B.on("tensor", lambda t, sl=sl: t.matmul(
                pB[:], cb_sb[:], q_sb[:, sl], start=True, stop=True), inc="st")
            B.wait("vector", "st", mko)
            outh_evac[j] = B.on("vector", lambda v, sl=sl: v.tensor_copy(
                outh[:, sl], pB[:]), inc="sv")
            B.wait("tensor", "sv", outh_evac[j])
            B.wait("vector", "sv", last_fin["sg"])
            bcastp(rowRw[0:1, sl], trt(0))
            if j - 1 in ffn_m and 7 in ffn_m[j - 1][0]:
                B.wait("tensor", "ss", ffn_m[j - 1][0][7])
            for c in range(NCH):
                psum = (pQ, pK, pV)[c % 3]
                prev_key = (j, c - 3) if c >= 3 else (j - 1, (6, 7, 5)[c])
                if prev_key in att_all:
                    B.wait("tensor", "sv", att_all[prev_key])
                mkp = B.on("tensor", lambda t, c=c, sl=sl, psum=psum: t.matmul(
                    psum[:], woutb[:, 128 * c:128 * (c + 1)], outh[:, sl],
                    start=True, stop=True), inc="st")
                tsl = NT * (c % 2)
                B.on("vector", lambda v, c=c, sl=sl, tsl=tsl: v.tensor_tensor(
                    t5r[:, tsl:tsl + NT], x_sb[c][:, sl], trt(0), OP.mult),
                    inc="sv")
                B.wait("vector", "st", mkp)
                att_all[(j, c)] = B.on(
                    "vector", lambda v, c=c, psum=psum, tsl=tsl:
                    v.scalar_tensor_tensor(
                        attr[:, NT * c:NT * (c + 1)], psum[:],
                        bocol[:, c:c + 1], t5r[:, tsl:tsl + NT],
                        OP.add, OP.add), inc="sv")
            B.wait("tensor", *stat_free)
            m3_marks[j] = B.on("tensor", lambda t, sl=sl: t.matmul(
                pSm[:], rwcol[:], outh[:, sl], start=True, stop=True),
                inc="st")

        def CC1c(j):
            sl = slice(NT * j, NT * (j + 1))
            B.wait("tensor", *stat_free)
            for c in range(NCH):
                if (c % 3) in sq_used:
                    B.wait("scalar", *sq_used[c % 3])
                B.wait("scalar", "sv", att_all[(j, c)])
                mkaq = B.on("scalar", lambda s, c=c: s.activation(
                    sq_slot(c), attr[:, NT * c:NT * (c + 1)], AF.Square),
                    inc="ss")
                B.wait("tensor", "ss", mkaq)
                sq_used[c % 3] = ("st", B.on("tensor", lambda t, c=c: t.matmul(
                    pSq[:], invccol[:], sq_slot(c),
                    start=(c == 0), stop=(c == NCH - 1)), inc="st"))
            B.wait("vector", "st", m3_marks[j])
            B.wait("vector", "ss", B.mark("ss"))
            B.on("vector", lambda v, sl=sl: v.scalar_tensor_tensor(
                rmj(0), pSm[:], bbar[:], mrs1row[0:1, sl], OP.add, OP.add),
                inc="sv")
            stat_free[:] = ["sv", B.on("vector", lambda v: v.tensor_copy(
                rqj(0), pSq[:]), inc="sv")]
            chain_core(0)
            B.on("vector", lambda v: v.tensor_tensor(rowM4[0:1, NT:2 * NT], rmj(0), rowRw[0:1, 0:NT],
                                                     OP.mult), inc="sv")
            B.on("vector", lambda v, sl=sl: v.tensor_copy(p2row[0:1, sl],
                                                          rowM4[0:1, NT:2 * NT]), inc="sv")

        def CC1b(j):
            sl = slice(NT * j, NT * (j + 1))
            mk_r3 = bcastp(rowRw[0:1, 0:NT], trt(0))
            B.wait("vector", "sv", mk_r3)
            for c in range(NCH):
                tsl = NT * (c % 2)
                B.on("vector", lambda v, c=c, tsl=tsl: v.tensor_tensor(
                    t3s[:, tsl:tsl + NT], attr[:, NT * c:NT * (c + 1)],
                    trt(0), OP.mult), inc="sv")
                x2_marks[(j, c)] = B.on(
                    "vector", lambda v, c=c, sl=sl, tsl=tsl, j=j: v.tensor_tensor(
                        x2_(c, j), t3s[:, tsl:tsl + NT],
                        x_sb[c][:, sl], OP.add), inc="sv")

        def CC2a(j):
            sl = slice(NT * j, NT * (j + 1))
            B.wait("tensor", *stat_free)
            for c in range(NCH):
                B.wait("tensor", "sv", x2_marks[(j, c)])
                B.on("tensor", lambda t, c=c, j=j: t.matmul(
                    pSm[:], invccol[:], x2_(c, j),
                    start=(c == 0), stop=(c == NCH - 1)))
            mks4 = None
            for c in range(NCH):
                if (c % 3) in sq_used:
                    B.wait("scalar", *sq_used[c % 3])
                B.wait("scalar", "sv", x2_marks[(j, c)])
                mkv = B.on("scalar", lambda s, c=c, j=j: s.activation(
                    sq_slot(c), x2_(c, j), AF.Square), inc="ss")
                B.wait("tensor", "ss", mkv)
                mks4 = B.on("tensor", lambda t, c=c: t.matmul(
                    pSq[:], invccol[:], sq_slot(c),
                    start=(c == 0), stop=(c == NCH - 1)), inc="st")
                sq_used[c % 3] = ("st", mks4)
            B.wait("scalar", "st", mks4)
            stat_free[:] = ["ss", stats_evac_s(0)]
            chain_core(0)
            B.on("vector", lambda v: v.tensor_tensor(rowM4[0:1, NT:2 * NT], rmj(0), rowRw[0:1, 0:NT],
                                                     OP.mult), inc="sv")

        def CC2b(j):
            sl = slice(NT * j, NT * (j + 1))
            B.wait("vector", "sv", last_fin["sg"])
            bcastp(rowRw[0:1, 0:NT], trt(1))
            mk_m4 = bcastp(rowM4[0:1, NT:2 * NT], trtb4[:], negate=True)
            B.wait("vector", "sv", mk_m4)
            for c in range(NCH):
                tsl = NT * (c % 2)
                B.on("vector", lambda v, c=c, tsl=tsl, j=j: v.tensor_tensor(
                    t3s[:, tsl:tsl + NT], x2_(c, j), trt(1), OP.mult),
                    inc="sv")
                w_marks[(j, c)] = B.on(
                    "vector", lambda v, c=c, sl=sl, tsl=tsl: v.tensor_tensor(
                        w_sb[c][:, sl], t3s[:, tsl:tsl + NT], trtb4[:],
                        OP.subtract), inc="sv")
            if j - 1 in rk_last:
                B.wait("vector", "st", rk_last[j - 1])
            d_marks = {}
            for c in range(NCH):
                if j == 0:
                    B.on("vector", lambda v, c=c: v.tensor_copy(
                        attr[:, NT * c:NT * c + 1], w_sb[c][:, 0:1]))
                    d_marks[c] = B.on("vector", lambda v, c=c: v.tensor_tensor(
                        attr[:, NT * c + 1:NT * (c + 1)], w_sb[c][:, 1:NT],
                        w_sb[c][:, 0:NT - 1], OP.subtract), inc="sv")
                else:
                    d_marks[c] = B.on("vector", lambda v, c=c, j=j:
                                      v.tensor_tensor(
                                          attr[:, NT * c:NT * (c + 1)],
                                          w_sb[c][:, NT * j:NT * (j + 1)],
                                          w_sb[c][:, NT * j - 1:NT * (j + 1) - 1],
                                          OP.subtract), inc="sv")
            for c in range(NCH):
                if j == 0:
                    B.on("vector", lambda v, c=c: v.tensor_scalar(
                        xr_(c)[:, 0:1], attr[:, NT * c:NT * c + 1],
                        murcol[:, c:c + 1], None, OP.mult))
                    mix_marks[(j, c)] = B.on(
                        "vector", lambda v, c=c: v.scalar_tensor_tensor(
                            xr_(c)[:, 1:NT], attr[:, NT * c + 1:NT * (c + 1)],
                            murcol[:, c:c + 1], w_sb[c][:, 0:NT - 1],
                            OP.mult, OP.add), inc="sv")
                else:
                    mix_marks[(j, c)] = B.on(
                        "vector", lambda v, c=c, j=j: v.scalar_tensor_tensor(
                            xr_(c), attr[:, NT * c:NT * (c + 1)],
                            murcol[:, c:c + 1],
                            w_sb[c][:, NT * j - 1:NT * (j + 1) - 1],
                            OP.mult, OP.add), inc="sv")
            for c in range(NCH):
                if j == 0:
                    B.on("vector", lambda v, c=c: v.tensor_scalar(
                        xk_(c)[:, 0:1], attr[:, NT * c:NT * c + 1],
                        mukcol[:, c:c + 1], None, OP.mult))
                    xk_marks[(j, c)] = B.on(
                        "vector", lambda v, c=c: v.scalar_tensor_tensor(
                            xk_(c)[:, 1:NT], attr[:, NT * c + 1:NT * (c + 1)],
                            mukcol[:, c:c + 1], w_sb[c][:, 0:NT - 1],
                            OP.mult, OP.add), inc="sv")
                else:
                    xk_marks[(j, c)] = B.on(
                        "vector", lambda v, c=c, j=j: v.scalar_tensor_tensor(
                            xk_(c), attr[:, NT * c:NT * (c + 1)],
                            mukcol[:, c:c + 1],
                            w_sb[c][:, NT * j - 1:NT * (j + 1) - 1],
                            OP.mult, OP.add), inc="sv")

        rk_last = {}
        rk_last = {}
        ffn_m = {}

        def _rk_groups(j, g0, g1, sig_m, ka_m):
            groups = ([("R", c) for c in range(NCH)]
                      + [("K", c) for c in range(NCH)])
            for g in range(g0, g1):
                kind, c = groups[g]
                bank = (pAux, pQ, pK)[g % 3]
                if j == 0:
                    if g == 1 and (0, 6) in att_all:
                        B.wait("tensor", "sv", att_all[(0, 6)])
                    if g == 2 and (0, 7) in att_all:
                        B.wait("tensor", "sv", att_all[(0, 7)])
                if g == 8 and (j + 1, 7) in att_all:
                    B.wait("tensor", "sv", att_all[(j + 1, 7)])
                if g == 10 and (j + 1, 6) in att_all:
                    B.wait("tensor", "sv", att_all[(j + 1, 6)])
                if g >= 3:
                    pk, pc = groups[g - 3]
                    if pk == "R":
                        B.wait("tensor", "ss", sig_m[pc])
                    else:
                        B.wait("tensor", "ss", ka_m[pc])
                W = Wr_sb if kind == "R" else Wk_sb
                xin = xr_ if kind == "R" else xk_
                mk = None
                for cc in range(NCH):
                    if g == 0:
                        B.wait("tensor", "sv", mix_marks[(j, cc)])
                    elif g == 1 and cc == 0:
                        B.wait("tensor", "sv", mix_marks[(j, NCH - 1)])
                    elif g == 8:
                        B.wait("tensor", "sv", xk_marks[(j, cc)])
                    elif g == 9 and cc == 0:
                        B.wait("tensor", "sv", xk_marks[(j, NCH - 1)])
                    mk = B.on("tensor", lambda t, c=c, cc=cc, bank=bank, W=W,
                              xin=xin: t.matmul(
                                  bank[:],
                                  W[:, C * cc + 128 * c:C * cc + 128 * (c + 1)],
                                  xin(cc), start=(cc == 0),
                                  stop=(cc == NCH - 1)),
                              inc="st" if cc == NCH - 1 else None) or mk
                if kind == "R":
                    B.wait("scalar", "st", mk)
                    sig_m[c] = B.on("scalar", lambda s, c=c, bank=bank:
                                    s.activation(rr_(c), bank[:], AF.Sigmoid),
                                    inc="ss")
                else:
                    B.wait("scalar", "st", mk)
                    if c == 4:
                        B.wait("scalar", "sv",
                               att_all[(min(j + 1, NJ - 1), NCH - 1)])
                    if c == 6:
                        B.wait("scalar", "sv", last_fin["sg"])
                    if (c % 3) in sq_used:
                        B.wait("scalar", *sq_used[c % 3])
                    B.on("scalar", lambda s, c=c, bank=bank:
                         s.activation(sq_slot(c), bank[:], AF.Relu),
                         inc="ss")
                    ka_m[c] = B.on("scalar", lambda s, c=c: s.activation(
                        ka_(c), sq_slot(c), AF.Square), inc="ss")
                    sq_used[c % 3] = ("ss", ka_m[c])
                if g == 2 * NCH - 1:
                    rk_last[j] = mk

        def EEa1A(j):
            sig_m = {}
            ka_m = {}
            ffn_m[j] = (sig_m, ka_m)
            _rk_groups(j, 0, 8, sig_m, ka_m)

        def EEa1B(j):
            sig_m, ka_m = ffn_m[j]
            _rk_groups(j, 8, 16, sig_m, ka_m)

        def EEa2a(j):
            sl = slice(NT * j, NT * (j + 1))
            sig_m, ka_m = ffn_m[j]
            B.wait("tensor", "ss", ka_m[NCH - 1])
            B.wait("tensor", "ss", sig_m[NCH - 1])
            B.wait("vector", "ss", B.mark("ss"))
            for c in range(NCH):
                bV = (pV, pQ, pK)[c % 3]
                if c == 0 and (j + 1, 5) in att_all:
                    B.wait("tensor", "sv", att_all[(j + 1, 5)])
                if c >= 3:
                    B.wait("tensor", "sv", o_all[(j, c - 3)])
                mkV = None
                for cc in range(NCH):
                    mkV = B.on("tensor", lambda t, c=c, cc=cc, bV=bV: t.matmul(
                        bV[:], Wv_sb[:, C * cc + 128 * c:C * cc + 128 * (c + 1)],
                        ka_(cc), start=(cc == 0), stop=(cc == NCH - 1)),
                        inc="st" if cc == NCH - 1 else None) or mkV
                B.wait("vector", "st", mkV)
                o_all[(j, c)] = B.on("vector", lambda v, c=c, bV=bV:
                                     v.tensor_tensor(
                                         oring[:, NT * c:NT * (c + 1)], bV[:],
                                         rr_(c), OP.mult), inc="sv")
        def EEa2b(j):
            sl = slice(NT * j, NT * (j + 1))
            B.wait("tensor", *stat_free)
            for c in range(NCH):
                B.wait("tensor", "sv", o_all[(j, c)])
                B.on("tensor", lambda t, c=c: t.matmul(
                    pSm[:], invccol[:], oring[:, NT * c:NT * (c + 1)],
                    start=(c == 0), stop=(c == NCH - 1)))
            mks5 = None
            for c in range(NCH):
                if (c % 3) in sq_used:
                    B.wait("scalar", *sq_used[c % 3])
                B.wait("scalar", "sv", o_all[(j, c)])
                mkv = B.on("scalar", lambda s, c=c: s.activation(
                    sq_slot(c), oring[:, NT * c:NT * (c + 1)], AF.Square),
                    inc="ss")
                B.wait("tensor", "ss", mkv)
                mks5 = B.on("tensor", lambda t, c=c: t.matmul(
                    pSq[:], invccol[:], sq_slot(c),
                    start=(c == 0), stop=(c == NCH - 1)), inc="st")
                sq_used[c % 3] = ("st", mks5)
            B.wait("scalar", "st", mks5)
            stat_free[:] = ["ss", stats_evac_s(0)]
            chain_core(0)
            B.on("vector", lambda v: v.tensor_tensor(rowM4[0:1, NT:2 * NT], rmj(0), rowRw[0:1, 0:NT],
                                                     OP.mult), inc="sv")
            B.on("vector", lambda v, sl=sl: v.tensor_tensor(
                rowM4[0:1, NT:2 * NT], rowM4[0:1, NT:2 * NT], p2row[0:1, sl], OP.add), inc="sv")

        def EEa2c(j):
            bcastp(rowRw[0:1, 0:NT], trt(1))
            return bcastp(rowM4[0:1, NT:2 * NT], trtb4[:], negate=True)

        def EEb(j, mk_cl):
            sl = slice(NT * j, NT * (j + 1))
            B.wait("vector", "sv", mk_cl)
            B.wait("gpsimd", "sv", mk_cl)
            last = None
            fin_slot = {}
            for c in range(NCH):
                slot = c % 2
                tA = NT * (2 + c % 2)
                if c % 2 == 0:
                    if slot in fin_slot:
                        B.wait("gpsimd", "sv", fin_slot[slot])
                    B.on("gpsimd", lambda g, c=c, tA=tA: g.tensor_tensor(
                        t5r[:, tA:tA + NT], oring[:, NT * c:NT * (c + 1)],
                        trt(1), OP.mult), inc="sg")
                    mg = B.on("gpsimd", lambda g, tA=tA: g.tensor_tensor(
                        t5r[:, tA:tA + NT], t5r[:, tA:tA + NT],
                        trtb4[:], OP.subtract), inc="sg")
                    B.wait("vector", "sg", mg)
                else:
                    B.on("vector", lambda v, c=c, tA=tA: v.tensor_tensor(
                        t5r[:, tA:tA + NT], oring[:, NT * c:NT * (c + 1)],
                        trt(1), OP.mult), inc="sv")
                    B.on("vector", lambda v, tA=tA: v.tensor_tensor(
                        t5r[:, tA:tA + NT], t5r[:, tA:tA + NT],
                        trtb4[:], OP.subtract), inc="sv")
                if slot in out_dma_marks:
                    B.wait("vector", "dO", out_dma_marks[slot])
                last = B.on("vector", lambda v, c=c, slot=slot, tA=tA, j=j:
                            v.tensor_tensor(
                                outr[:, NT * slot:NT * (slot + 1)],
                                t5r[:, tA:tA + NT],
                                x2_(c, j), OP.add), inc="sv")
                fin_slot[slot] = last
                B.wait("sync", "sv", last)
                out_dma_marks[slot] = B.dma(
                    "sync", yT[128 * c:128 * (c + 1), sl],
                    outr[:, NT * slot:NT * (slot + 1)], "dO")
            last_fin["sg"] = last

        CC1a(0)
        CC1c(0)
        CC1b(0)
        CC2a(0)
        CC2b(0)
        EEa1A(0)
        CC1a(1)
        CC1c(1)
        EEa1B(0)
        CC1b(1)
        CC2a(1)
        EEa2a(0)
        CC2b(1)
        EEa2b(0)
        EEa1A(1)
        CC1a(2)
        cl0 = EEa2c(0)
        CC1c(2)
        EEb(0, cl0)
        EEa1B(1)
        CC1b(2)
        CC2a(2)
        EEa2a(1)
        CC2b(2)
        EEa2b(1)
        EEa1A(2)
        CC1a(3)
        cl1 = EEa2c(1)
        CC1c(3)
        EEb(1, cl1)
        EEa1B(2)
        CC1b(3)
        CC2a(3)
        EEa2a(2)
        CC2b(3)
        EEa2b(2)
        EEa1A(3)
        cl2 = EEa2c(2)
        EEb(2, cl2)
        EEa1B(3)
        EEa2a(3)
        EEa2b(3)
        cl3 = EEa2c(3)
        EEb(3, cl3)

        if debug:
            B.wait("sync", "sv", B.mark("sv"))
            for c in range(NCH):
                B.dma("sync", dbg["d_x2"][128 * c:128 * (c + 1), 0:2 * NT],
                      x2_sb[c][:], "dO")
                B.dma("sync", dbg["d_w"][128 * c:128 * (c + 1), :], w_sb[c][:],
                      "dO")
            B.dma("sync", dbg["d_or"], oring[:], "dO")
            B.dma("sync", dbg["d_xr"], k_sb[:], "dO")
            B.dma("sync", dbg["d_xk"], v_sb[:], "dO")
            B.dma("sync", dbg["d_xk2"], vtm[:], "dO")
            B.dma("sync", dbg["d_rr"], Etm[:], "dO")
            B.dma("sync", dbg["d_ka"], outh[:], "dO")
        B.wait("sync", "dO", B.mark("dO"))

        block = E_(nc.Block())

        @block.sync
        def _(e):
            for fn in B.ops["sync"]:
                fn(e)

        @block.gpsimd
        def _(e):
            for fn in B.ops["gpsimd"]:
                fn(e)

        @block.vector
        def _(e):
            for fn in B.ops["vector"]:
                fn(e)

        @block.scalar
        def _(e):
            for fn in B.ops["scalar"]:
                fn(e)

        @block.tensor
        def _(e):
            for fn in B.ops["tensor"]:
                fn(e)

    return nc


_NC_CACHE = {}


def _get_nc(debug=False):
    if debug not in _NC_CACHE:
        _NC_CACHE[debug] = build_nc(debug)
    return _NC_CACHE[debug]


def kernel(L0_attn_x, L0_attn_gamma, L0_attn_beta, L0_attn_w_qkv, L0_attn_w_out,
           L0_attn_b_out, L0_ffn_W_r, L0_ffn_W_k, L0_ffn_W_v, L0_ffn_mu_r,
           L0_ffn_mu_k, _debug=False):
    x = np.ascontiguousarray(L0_attn_x, dtype=np.float32)
    shared = {
        "w_qkv": np.ascontiguousarray(L0_attn_w_qkv, np.float32),
        "w_out": np.ascontiguousarray(L0_attn_w_out, np.float32),
        "WrT": np.ascontiguousarray(np.asarray(L0_ffn_W_r, np.float32).T),
        "WkT": np.ascontiguousarray(np.asarray(L0_ffn_W_k, np.float32).T),
        "WvT": np.ascontiguousarray(np.asarray(L0_ffn_W_v, np.float32).T),
        "vgamma": np.ascontiguousarray(np.asarray(L0_attn_gamma, np.float32).ravel()),
        "vbeta": np.ascontiguousarray(np.asarray(L0_attn_beta, np.float32).ravel()),
        "vbout": np.ascontiguousarray(np.asarray(L0_attn_b_out, np.float32).ravel()),
        "vmur": np.ascontiguousarray(L0_ffn_mu_r, np.float32),
        "vmuk": np.ascontiguousarray(L0_ffn_mu_k, np.float32),
    }
    in_maps = []
    for b in range(8):
        m = dict(shared)
        m["xT"] = np.ascontiguousarray(x[b].T)
        in_maps.append(m)
    nc = _get_nc(_debug)
    res = run_bass_kernel_spmd(nc, in_maps, core_ids=list(range(8)))
    out = np.stack([np.asarray(r["yT"]).T for r in res.results])
    if _debug:
        return out, res.results
    return out


# version 40
# speedup vs baseline: 1.0554x; 1.0210x over previous
import contextlib
import numpy as np
import concourse.bass as bass
import concourse.mybir as mybir
from concourse.bass_utils import run_bass_kernel_spmd
from concourse.alu_op_type import AluOpType as OP

F32 = mybir.dt.float32
BF16 = mybir.dt.bfloat16
AF = mybir.ActivationFunctionType
AX = mybir.AxisListType

T = 2048
C = 1024
NCH = 8
NJ = 4
NT = 512
EPS = 1e-5
INV_C = 1.0 / C

ENGINES = ("sync", "gpsimd", "vector", "scalar", "tensor")


class Builder:
    def __init__(self, nc, sems):
        self.nc = nc
        self.sems = sems
        self.cnt = {k: 0 for k in sems}
        self.ops = {e: [] for e in ENGINES}

    def on(self, eng, fn, inc=None):
        sem = self.sems[inc] if inc else None
        if inc:
            self.cnt[inc] += 1

        def run(e, fn=fn, sem=sem):
            inst = fn(e)
            if sem is not None:
                inst.then_inc(sem, 1)
        self.ops[eng].append(run)
        return self.cnt[inc] if inc else None

    def dma(self, eng, out, in_, inc, slow=False):
        self.cnt[inc] += 16
        sem = self.sems[inc]

        def run(e, out=out, in_=in_, sem=sem, slow=slow):
            kw = dict(allow_slow_non_contiguous=True) if slow else {}
            e.dma_start(out=out, in_=in_, **kw).then_inc(sem, 16)
        self.ops[eng].append(run)
        return self.cnt[inc]

    def mark(self, sem):
        return self.cnt[sem]

    def wait(self, eng, sem, val=None):
        if val is None:
            val = self.cnt[sem]
        s = self.sems[sem]

        def run(e, s=s, val=val):
            e.wait_ge(s, val)
        self.ops[eng].append(run)


def build_nc(debug=False):
    nc = bass.Bass("TRN2", target_bir_lowering=False)

    xT = nc.dram_tensor("xT", [C, T], F32, kind="ExternalInput").ap()
    w_qkv = nc.dram_tensor("w_qkv", [C, 384], F32, kind="ExternalInput").ap()
    w_out = nc.dram_tensor("w_out", [128, C], F32, kind="ExternalInput").ap()
    WrT = nc.dram_tensor("WrT", [C, C], F32, kind="ExternalInput").ap()
    WkT = nc.dram_tensor("WkT", [C, C], F32, kind="ExternalInput").ap()
    WvT = nc.dram_tensor("WvT", [C, C], F32, kind="ExternalInput").ap()
    vgamma = nc.dram_tensor("vgamma", [C], F32, kind="ExternalInput").ap()
    vbeta = nc.dram_tensor("vbeta", [C], F32, kind="ExternalInput").ap()
    vbout = nc.dram_tensor("vbout", [C], F32, kind="ExternalInput").ap()
    vmur = nc.dram_tensor("vmur", [C], F32, kind="ExternalInput").ap()
    vmuk = nc.dram_tensor("vmuk", [C], F32, kind="ExternalInput").ap()
    yT = nc.dram_tensor("yT", [C, T], F32, kind="ExternalOutput").ap()

    dbg = {}
    if debug:
        for nm, shp, dt in [("d_x2", [C, T], BF16), ("d_w", [C, T], BF16),
                            ("d_or", [128, NCH * NT], BF16),
                            ("d_xr", [128, T], BF16), ("d_xk", [128, T], BF16),
                            ("d_xk2", [128, T], BF16), ("d_rr", [128, T], BF16),
                            ("d_ka", [128, T], BF16)]:
            dbg[nm] = nc.dram_tensor(nm, shp, dt, kind="ExternalOutput").ap()

    with contextlib.ExitStack() as ST:
        E_ = ST.enter_context

        x_sb = [E_(nc.sbuf_tensor(f"x{c}", [128, T], BF16)) for c in range(NCH)]
        x2_sb = [E_(nc.sbuf_tensor(f"x2_{c}", [128, 2 * NT], BF16)) for c in range(NCH)]
        q_sb = E_(nc.sbuf_tensor("q", [128, T], BF16))
        k_sb = E_(nc.sbuf_tensor("k", [128, T], BF16))
        v_sb = E_(nc.sbuf_tensor("v", [128, T], BF16))
        Eb = E_(nc.sbuf_tensor("Eb", [128, T], BF16))
        Etm = E_(nc.sbuf_tensor("Etm", [128, T], BF16))
        vtm = E_(nc.sbuf_tensor("vtm", [128, T], BF16))
        outh = E_(nc.sbuf_tensor("outh", [128, T], BF16))
        t5r = E_(nc.sbuf_tensor("t5r", [128, T], BF16))
        Wr_sb = E_(nc.sbuf_tensor("Wr", [128, NCH * C], BF16))
        Wk_sb = E_(nc.sbuf_tensor("Wk", [128, NCH * C], BF16))
        Wv_sb = E_(nc.sbuf_tensor("Wv", [128, NCH * C], BF16))
        woutb = E_(nc.sbuf_tensor("woutb", [128, C], BF16))
        attr = E_(nc.sbuf_tensor("attr", [128, NCH * NT], BF16))
        oring = E_(nc.sbuf_tensor("oring", [128, NCH * NT], BF16))
        sqr = E_(nc.sbuf_tensor("sqr", [128, 3 * NT], BF16))
        t3s = E_(nc.sbuf_tensor("t3s", [128, 2 * NT], BF16))
        rrB = E_(nc.sbuf_tensor("rrB", [128, 4 * NT], BF16))

        def _alias(name, backing):
            t_ = nc.virtual_sbuf_tensor(name, [128, T], BF16)
            nc.back_virtual_tensor_with(t_, backing)
            return t_
        w_sb = [_alias(f"w{c}", x_sb[c]) for c in range(NCH)]
        xrA = _alias("xrA", k_sb)
        xrB = _alias("xrB", Eb)
        xkA = _alias("xkA", v_sb)
        xkB = _alias("xkB", vtm)
        rrA = _alias("rrA", Etm)
        kaA = _alias("kaA", outh)
        kaB = _alias("kaB", t5r)

        def _seg(tA, tB, c):
            t_ = tA if c < 4 else tB
            return t_[:, NT * (c % 4):NT * (c % 4 + 1)]

        def x2_(c, j):
            return x2_sb[c][:, NT * (j % 2):NT * (j % 2 + 1)]

        def xr_(c):
            return _seg(xrA, xrB, c)

        def xk_(c):
            return _seg(xkA, xkB, c)

        def rr_(c):
            if c < 4:
                return rrA[:, NT * c:NT * (c + 1)]
            return rrB[:, NT * (c - 4):NT * (c - 3)]

        def ka_(c):
            return _seg(kaA, kaB, c)

        rsrow = E_(nc.sbuf_tensor("rsrow", [1, T], BF16))
        rowM4 = E_(nc.sbuf_tensor("rowM4", [1, T], F32))
        rowQ4 = E_(nc.sbuf_tensor("rowQ4", [1, T], F32))
        wT0 = E_(nc.sbuf_tensor("wT0", [1, T], F32))
        rowRw = E_(nc.sbuf_tensor("rowRw", [1, T], F32))
        mrs1row = E_(nc.sbuf_tensor("mrs1row", [1, T], BF16))
        p2row = E_(nc.sbuf_tensor("p2row", [1, T], BF16))
        s4hi = p2row

        outr = E_(nc.sbuf_tensor("outr", [128, 2 * NT], F32))
        TRT = E_(nc.sbuf_tensor("TRT", [128, 2 * NT], BF16))
        trtb4 = E_(nc.sbuf_tensor("trtb4", [128, NT], BF16))

        def trt(i):
            return TRT[:, NT * i:NT * (i + 1)]

        gcol = E_(nc.sbuf_tensor("gcol", [128, 8], F32))
        bcol = E_(nc.sbuf_tensor("bcol", [128, 8], F32))
        bocol = E_(nc.sbuf_tensor("bocol", [128, 8], F32))
        murcol = E_(nc.sbuf_tensor("murcol", [128, 8], F32))
        mukcol = E_(nc.sbuf_tensor("mukcol", [128, 8], F32))
        bcolb2 = E_(nc.sbuf_tensor("bcolb2", [128, 16], BF16, align_bytes=4))
        onescol = E_(nc.sbuf_tensor("onescol", [128, 1], BF16, align_bytes=4))
        invccol = E_(nc.sbuf_tensor("invccol", [128, 1], BF16, align_bytes=4))
        ones1 = E_(nc.sbuf_tensor("ones1", [1, 128], BF16, align_bytes=4))
        bocolb = E_(nc.sbuf_tensor("bocolb", [128, 8], BF16, align_bytes=4))
        cb_sb = E_(nc.sbuf_tensor("cb_sb", [128, 128], BF16))
        idb = nc.virtual_sbuf_tensor("idb", [128, 128], BF16)
        nc.back_virtual_tensor_with(idb, cb_sb)
        io_c = E_(nc.sbuf_tensor("io_c", [128, 128], BF16))
        maskb = nc.virtual_sbuf_tensor("maskb", [128, 128], BF16)
        nc.back_virtual_tensor_with(maskb, io_c)
        io_r = E_(nc.sbuf_tensor("io_r", [128, 1], F32, align_bytes=4))
        rspr = E_(nc.sbuf_tensor("rspr", [128, 1], F32, align_bytes=4))
        kmax = E_(nc.sbuf_tensor("kmax", [128, 1], F32, align_bytes=4))
        ssum = E_(nc.sbuf_tensor("ssum", [128, 1], F32, align_bytes=4))
        rwcol = E_(nc.sbuf_tensor("rwcol", [128, 1], BF16, align_bytes=4))
        rwf = E_(nc.sbuf_tensor("rwf", [128, 1], F32, align_bytes=4))
        bbar = E_(nc.sbuf_tensor("bbar", [1, 1], F32, align_bytes=4))
        epsc = E_(nc.sbuf_tensor("epsc", [1, 1], F32, align_bytes=4))
        pb8 = E_(nc.sbuf_tensor("pb8", [1, 8], F32, align_bytes=4))

        pQ = E_(nc.psum_tensor("pQ", [128, NT], F32))
        pK = E_(nc.psum_tensor("pK", [128, NT], F32))
        pV = E_(nc.psum_tensor("pV", [128, NT], F32))
        pSm = E_(nc.psum_tensor("pSm", [1, NT], F32))
        pSq = E_(nc.psum_tensor("pSq", [1, NT], F32))
        pB = E_(nc.psum_tensor("pB", [128, NT], F32))
        pAux = E_(nc.psum_tensor("pAux", [128, NT], F32))
        pT = E_(nc.psum_tensor("pT", [128, 1024], BF16))

        sems = {}
        for s in ("dX", "dW", "dM", "dO", "sv", "ss", "st", "sg"):
            sems[s] = E_(nc.semaphore(s))
        B = Builder(nc, sems)

        mark_xj = {}
        for c in range(NCH):
            B.dma("gpsimd", x_sb[c][:, 0:NT],
                  xT[128 * c:128 * c + 128, 0:NT], "dX")
        mark_xj[0] = B.mark("dX")
        for dst, src in ((gcol, vgamma), (bcol, vbeta), (bocol, vbout),
                         (murcol, vmur), (mukcol, vmuk)):
            B.dma("sync", dst[:], src.rearrange("(a p) -> p a", p=128), "dM",
                  slow=True)
        mark_vecs = B.mark("dM")
        B.dma("gpsimd", Wr_sb[:, 0:NCH * 384].rearrange("p (a k) -> p a k", k=384),
              w_qkv.rearrange("(a p) k -> p a k", p=128), "dW")
        B.dma("gpsimd", woutb[:], w_out, "dW")
        mark_wqkv = B.mark("dW")
        for j in range(1, NJ):
            for c in range(NCH):
                B.dma("gpsimd", x_sb[c][:, NT * j:NT * (j + 1)],
                      xT[128 * c:128 * c + 128, NT * j:NT * (j + 1)], "dX")
            mark_xj[j] = B.mark("dX")
        mark_x = B.mark("dX")
        B.dma("gpsimd", Wv_sb[:].rearrange("p (a k) -> p a k", k=C),
              WvT.rearrange("(a p) k -> p a k", p=128), "dW")

        B.on("gpsimd", lambda g: g.iota(io_c[:], pattern=[[1, 128]], base=0,
                                        channel_multiplier=0,
                                        allow_small_or_imprecise_dtypes=True))
        mark_iota = B.on("gpsimd", lambda g: g.iota(
            io_r[:], pattern=[[0, 1]], base=0, channel_multiplier=1,
            allow_small_or_imprecise_dtypes=True), inc="sg")

        B.on("vector", lambda v: v.memset(onescol[:], 1.0))
        B.on("vector", lambda v: v.memset(invccol[:], INV_C))
        B.on("vector", lambda v: v.memset(ones1[:], 1.0))
        B.on("vector", lambda v: v.memset(epsc[:], EPS))
        B.on("vector", lambda v: v.memset(bcolb2[:], 0.0))
        B.wait("vector", "sg", mark_iota)
        mark_const = B.on("vector", lambda v: v.tensor_scalar(
            idb[:], io_c[:], io_r[:], None, OP.is_equal), inc="sv")
        B.on("vector", lambda v: v.memset(maskb[:], 0.0))
        for h in range(4):
            B.on("vector", lambda v, h=h: v.memset(
                maskb[32 * h:32 * h + 32, 32 * h:32 * h + 32], 32.0 ** -0.5))

        B.wait("vector", "dW", mark_wqkv)
        B.wait("vector", "dM", mark_vecs)
        for c in range(NCH):
            B.on("vector", lambda v, c=c: v.tensor_scalar(
                Wk_sb[:, 384 * c:384 * (c + 1)], Wr_sb[:, 384 * c:384 * (c + 1)],
                gcol[:, c:c + 1], None, OP.mult), inc="sv")
        B.on("vector", lambda v: v.tensor_copy(
            bcolb2[:].rearrange("p (a two) -> p a two", two=2)[:, :, 0:1],
            bcol[:].rearrange("p (a one) -> p a one", one=1)), inc="sv")
        mk_rwf = B.on("vector", lambda v: v.reduce_sum(rwf[:], woutb[:],
                                                       axis=AX.X), inc="sv")
        B.on("vector", lambda v: v.tensor_copy(bocolb[:], bocol[:]), inc="sv")
        B.wait("scalar", "sv", mk_rwf)
        B.on("scalar", lambda s: s.mul(rwcol[:], rwf[:], INV_C), inc="ss")
        mark_prep = B.mark("sv")
        mark_prep_s = B.mark("ss")

        B.wait("tensor", "sv", mark_prep)
        mark_bqcw = None
        for c in range(NCH):
            B.on("tensor", lambda t, c=c: t.matmul(
                pSm[0:1, 0:384], bcolb2[:, 2 * c:2 * c + 1],
                Wr_sb[:, 384 * c:384 * (c + 1)], start=(c == 0),
                stop=(c == NCH - 1)))
            mark_bqcw = B.on("tensor", lambda t, c=c: t.matmul(
                pSq[0:1, 0:384], onescol[:], Wk_sb[:, 384 * c:384 * (c + 1)],
                start=(c == 0), stop=(c == NCH - 1)),
                inc="st" if c == NCH - 1 else None) or mark_bqcw
        B.wait("vector", "st", mark_bqcw)
        B.on("vector", lambda v: v.tensor_copy(attr[0:1, 0:384], pSm[0:1, 0:384]),
             inc="sv")
        B.on("vector", lambda v: v.tensor_copy(attr[0:1, 384:768], pSq[0:1, 0:384]),
             inc="sv")
        mark_cwbq = B.mark("sv")
        B.wait("tensor", "sv", mark_cwbq)
        mark_bo = B.on("tensor", lambda t: t.matmul(
            pSm[0:1, 0:8], onescol[:], bocolb[:], start=True, stop=True),
            inc="st")
        B.wait("vector", "st", mark_bo)
        mk_pb8 = B.on("vector", lambda v: v.reduce_sum(pb8[:, 0:1],
                                                       pSm[0:1, 0:8],
                                                       axis=AX.X), inc="sv")
        B.wait("scalar", "sv", mk_pb8)
        B.on("scalar", lambda s: s.mul(bbar[:], pb8[:, 0:1], INV_C), inc="ss")
        B.wait("gpsimd", "st", mark_bo)
        B.wait("gpsimd", "sv", mark_prep)
        B.dma("gpsimd", Wr_sb[:].rearrange("p (a k) -> p a k", k=C),
              WrT.rearrange("(a p) k -> p a k", p=128), "dW")

        sq_used = {}
        ob_used = sq_used

        def sq_slot(c):
            return sqr[:, NT * (c % 3):NT * (c % 3 + 1)]

        def ob_slot(c):
            return sq_slot(c)

        def rmj(j):
            return rowM4[0:1, 0:NT]

        def rqj(j):
            return rowQ4[0:1, 0:NT]

        def chain_core(j=0):
            B.wait("scalar", "sv", B.mark("sv"))
            B.on("scalar", lambda s, j=j: s.activation(wT0[0:1, 0:NT], rmj(j),
                                                       AF.Square), inc="ss")
            B.wait("vector", "ss", B.mark("ss"))
            mkv = B.on("vector", lambda v, j=j: v.tensor_tensor(
                rowM4[0:1, NT:2 * NT], rqj(j), wT0[0:1, 0:NT], OP.subtract), inc="sv")
            B.wait("scalar", "sv", mkv)
            B.on("scalar", lambda s: s.activation(
                wT0[0:1, 0:NT], rowM4[0:1, NT:2 * NT], AF.Ln, bias=epsc[:], scale=1.0), inc="ss")
            mk = B.on("scalar", lambda s: s.activation(
                rowRw[0:1, 0:NT], wT0[0:1, 0:NT], AF.Exp, scale=-0.5), inc="ss")
            B.wait("vector", "ss", mk)
            return mk

        def stats_evac_s(j=0):
            B.wait("scalar", "sv", B.mark("sv"))
            B.on("scalar", lambda s, j=j: s.copy(rmj(j), pSm[:]))
            return B.on("scalar", lambda s, j=j: s.copy(rqj(j), pSq[:]),
                        inc="ss")

        def stats_evac(j=0):
            B.wait("vector", "ss", B.mark("ss"))
            B.on("vector", lambda v, j=j: v.tensor_copy(rmj(j), pSm[:]),
                 inc="sv")
            return B.on("vector", lambda v, j=j: v.tensor_copy(rqj(j), pSq[:]),
                        inc="sv")

        def bcast(src_row, dst_tile, negate=False):
            mk = B.on("vector", lambda v: v.tensor_copy(rsrow[0:1, 0:NT], src_row),
                      inc="sv")
            B.wait("tensor", "sv", mk)
            mk2 = B.on("tensor", lambda t: t.matmul(pB[:], ones1[:],
                                                    rsrow[0:1, 0:NT],
                                                    start=True, stop=True),
                       inc="st")
            B.wait("vector", "st", mk2)
            return B.on("vector", lambda v, dst=dst_tile: v.tensor_copy(
                dst, pB[:]), inc="sv")

        mark_setup_ss = B.mark("ss")
        B.wait("tensor", "sv", mark_cwbq)
        B.wait("tensor", "ss", mark_setup_ss)
        B.wait("vector", "ss", mark_setup_ss)
        qkv_out = (q_sb, k_sb, v_sb)
        evac_marks = {}
        stA_evac = {}

        def stats_j(j):
            sl = slice(NT * j, NT * (j + 1))
            B.wait("tensor", "dX", mark_xj[j])
            B.wait("scalar", "dX", mark_xj[j])
            if j == 0:
                B.wait("tensor", "sv", mk_pb8)
            else:
                B.wait("tensor", "ss", stA_evac[j - 1])
            mkm = None
            for c in range(NCH):
                mkm = B.on("tensor", lambda t, c=c, sl=sl: t.matmul(
                    pSm[:], invccol[:], x_sb[c][:, sl],
                    start=(c == 0), stop=(c == NCH - 1)),
                    inc="st" if c == NCH - 1 else None) or mkm
            mks = None
            for c in range(NCH):
                if (c % 3) in sq_used:
                    B.wait("scalar", *sq_used[c % 3])
                mkv = B.on("scalar", lambda s, c=c, sl=sl: s.activation(
                    sq_slot(c), x_sb[c][:, sl], AF.Square), inc="ss")
                B.wait("tensor", "ss", mkv)
                mks = B.on("tensor", lambda t, c=c: t.matmul(
                    pSq[:], invccol[:], sq_slot(c),
                    start=(c == 0), stop=(c == NCH - 1)), inc="st")
                sq_used[c % 3] = ("st", mks)
            B.wait("scalar", "st", mks)
            B.on("scalar", lambda s, sl=sl: s.copy(rowM4[0:1, sl], pSm[:]))
            stA_evac[j] = B.on("scalar", lambda s, sl=sl: s.copy(
                rowQ4[0:1, sl], pSq[:]), inc="ss")

        def chain_wide():
            mk0 = B.on("scalar", lambda s: s.activation(
                wT0[:], rowM4[:], AF.Square), inc="ss")
            B.wait("vector", "ss", mk0)
            mkv = B.on("vector", lambda v: v.tensor_tensor(
                rowQ4[:], rowQ4[:], wT0[:], OP.subtract), inc="sv")
            B.wait("scalar", "sv", mkv)
            B.on("scalar", lambda s: s.activation(
                wT0[:], rowQ4[:], AF.Ln, bias=epsc[:], scale=1.0), inc="ss")
            mke = B.on("scalar", lambda s: s.activation(
                rowRw[:], wT0[:], AF.Exp, scale=-0.5), inc="ss")
            B.wait("vector", "ss", mke)
            B.on("vector", lambda v: v.tensor_tensor(
                mrs1row[0:1, :], rowM4[:], rowRw[:], OP.mult), inc="sv")
            B.on("vector", lambda v: v.tensor_scalar(
                oring[0:1, 0:T], rowM4[:], -1.0, None, OP.mult), inc="sv")
            mku = B.on("vector", lambda v: v.tensor_scalar(
                rowM4[:], rowQ4[:], 1.0 + EPS, EPS * EPS, OP.mult, OP.add),
                inc="sv")
            B.wait("scalar", "sv", mku)
            B.on("scalar", lambda s: s.activation(
                wT0[:], rowM4[:], AF.Ln), inc="ss")
            mk_s4 = B.on("scalar", lambda s: s.activation(
                s4hi[0:1, :], wT0[:], AF.Exp, scale=-0.5), inc="ss")
            mk_rows = B.on("scalar", lambda s: s.activation(
                rsrow[0:1, :], wT0[:], AF.Exp, scale=0.5), inc="ss")
            return mk_s4, mk_rows

        def Gbig(j):
            sl = slice(NT * j, NT * (j + 1))
            B.wait("tensor", "dX", mark_xj[j])
            for oi, psum in enumerate((pQ, pK, pV)):
                osl = slice(128 * oi, 128 * (oi + 1))
                if (j - 1, oi) in evac_marks:
                    B.wait("tensor", "sv", evac_marks[(j - 1, oi)])
                for c in range(NCH):
                    B.on("tensor", lambda t, c=c, osl=osl, sl=sl, psum=psum:
                         t.matmul(psum[:],
                                  Wk_sb[:, 384 * c + osl.start:384 * c + osl.stop],
                                  x_sb[c][:, sl], start=(c == 0), stop=False))

        def sbcast(j, mk_s4=None):
            if mk_s4 is not None:
                B.wait("tensor", "ss", mk_s4)
            if last_bcastA[0] is not None:
                B.wait("tensor", "sv", last_bcastA[0])
            mkb = B.on("tensor", lambda t, j=j: t.matmul(
                pB[:], ones1[:], s4hi[0:1, NT * j:NT * (j + 1)],
                start=True, stop=True), inc="st")
            B.wait("vector", "st", mkb)
            last_bcastA[0] = B.on("vector", lambda v, j=j: v.tensor_copy(
                trt(j % 2), pB[:]), inc="sv")

        def Gtail(j, mk_rows=None):
            sl = slice(NT * j, NT * (j + 1))
            jj4 = NT * j
            if mk_rows is not None:
                B.wait("tensor", "ss", mk_rows)
            for oi, (psum, sbout) in enumerate(zip((pQ, pK, pV), qkv_out)):
                osl = slice(128 * oi, 128 * (oi + 1))
                B.on("tensor", lambda t, osl=osl, jj4=jj4, psum=psum: t.matmul(
                    psum[:], attr[0:1, 384 + osl.start:384 + osl.stop], oring[0:1, jj4:jj4 + NT],
                    start=False, stop=False))
                mkq = B.on("tensor", lambda t, osl=osl, jj4=jj4, psum=psum:
                           t.matmul(psum[:], attr[0:1, osl.start:osl.stop],
                                    rsrow[0:1, jj4:jj4 + NT],
                                    start=False, stop=True), inc="st")
                B.wait("vector", "st", mkq)
                evac_marks[(j, oi)] = B.on(
                    "vector", lambda v, sbout=sbout, sl=sl, psum=psum, j=j:
                    v.tensor_tensor(sbout[:, sl], psum[:], trt(j % 2), OP.mult),
                    inc="sv")

        last_bcastA = [None]
        stats_j(0)
        Gbig(0)
        stats_j(1)
        stats_j(2)
        stats_j(3)
        mk_s4, mk_rows = chain_wide()
        sbcast(0, mk_s4)
        Gtail(0, mk_rows)
        sbcast(1)
        Gbig(1)
        Gtail(1)
        sbcast(2)
        Gbig(2)
        Gtail(2)
        sbcast(3)
        Gbig(3)
        Gtail(3)
        mark_qkv = B.mark("sv")
        B.wait("gpsimd", "sv", mark_qkv)
        B.dma("gpsimd", Wk_sb[:].rearrange("p (a k) -> p a k", k=C),
              WkT.rearrange("(a p) k -> p a k", p=128), "dW")
        mark_wffn = B.mark("dW")

        B.wait("vector", "sv", mark_qkv)
        B.on("vector", lambda v: v.reduce_max(kmax[:], k_sb[:], axis=AX.X,
                                              negate=True), inc="sv")
        B.wait("scalar", "sv", B.mark("sv"))
        mark_E = B.on("scalar", lambda s: s.activation(
            Eb[:], k_sb[:], AF.Exp, bias=kmax[:], scale=1.0),
            inc="ss")
        B.wait("vector", "ss", mark_E)
        B.on("vector", lambda v: v.reduce_sum(ssum[:], Eb[:], axis=AX.X),
             inc="sv")
        B.wait("tensor", "ss", mark_E)
        B.wait("tensor", "sv", mark_const)
        for src, dst in ((Eb, Etm), (v_sb, vtm)):
            for g in range(2):
                B.wait("tensor", "sv", B.mark("sv"))
                mkt = None
                for b_ in range(8):
                    blk = 8 * g + b_
                    mkt = B.on("tensor", lambda t, src=src, b_=b_, blk=blk:
                               t.transpose(pT[:, 128 * b_:128 * (b_ + 1)],
                                           src[:, 128 * blk:128 * (blk + 1)],
                                           idb[:]),
                               inc="st" if b_ == 7 else None) or mkt
                B.wait("vector", "st", mkt)
                B.on("vector", lambda v, dst=dst, g=g: v.tensor_copy(
                    dst[:, 1024 * g:1024 * (g + 1)], pT[:]), inc="sv")
        mark_tr = B.mark("sv")
        B.wait("tensor", "sv", mark_tr)
        mark_ctx = None
        for blk in range(16):
            mark_ctx = B.on("tensor", lambda t, blk=blk: t.matmul(
                pAux[:, 0:128], Etm[:, 128 * blk:128 * (blk + 1)],
                vtm[:, 128 * blk:128 * (blk + 1)],
                start=(blk == 0), stop=(blk == 15)),
                inc="st" if blk == 15 else None) or mark_ctx
        B.on("vector", lambda v: v.reciprocal(rspr[:], ssum[:]), inc="sv")
        B.wait("vector", "st", mark_ctx)
        mark_cb = B.on("vector", lambda v: v.scalar_tensor_tensor(
            cb_sb[:], pAux[:, 0:128], rspr[:], maskb[:], OP.mult, OP.mult),
            inc="sv")

        B.wait("tensor", "dW", mark_wffn)
        last_bcast = {"evac": None}

        def bcastp(src_row, dst_tile, negate=False):
            if last_bcast["evac"] is not None:
                B.wait("tensor", *last_bcast["evac"])
            mkv = bcast(src_row, dst_tile, negate)
            last_bcast["evac"] = ("sv", mkv)
            return mkv

        att_all = {}
        outh_evac = {}
        stat_free = ["ss", stA_evac[3]]
        x2_marks = {}
        w_marks = {}
        mix_marks = {}
        xk_marks = {}
        o_all = {}
        out_dma_marks = {}
        m3_marks = {}
        last_fin = {"sg": 0}

        def CC1a(j):
            sl = slice(NT * j, NT * (j + 1))
            if j == 0:
                B.wait("tensor", "sv", mark_cb)
                B.wait("vector", "dX", mark_x)
            if j - 1 in outh_evac:
                B.wait("tensor", "sv", outh_evac[j - 1])
            if last_bcast["evac"] is not None:
                B.wait("tensor", *last_bcast["evac"])
            if (j - 2, NCH - 1) in o_all:
                B.wait("tensor", "sv", o_all[(j - 2, NCH - 1)])
            mko = B.on("tensor", lambda t, sl=sl: t.matmul(
                pB[:], cb_sb[:], q_sb[:, sl], start=True, stop=True), inc="st")
            B.wait("vector", "st", mko)
            outh_evac[j] = B.on("vector", lambda v, sl=sl: v.tensor_copy(
                outh[:, sl], pB[:]), inc="sv")
            B.wait("tensor", "sv", outh_evac[j])
            B.wait("vector", "sv", last_fin["sg"])
            bcastp(rowRw[0:1, sl], trt(0))
            if j - 1 in ffn_m and 7 in ffn_m[j - 1][0]:
                B.wait("tensor", "ss", ffn_m[j - 1][0][7])
            for c in range(NCH):
                psum = (pQ, pK, pV)[c % 3]
                prev_key = (j, c - 3) if c >= 3 else (j - 1, (6, 7, 5)[c])
                if prev_key in att_all:
                    B.wait("tensor", "sv", att_all[prev_key])
                mkp = B.on("tensor", lambda t, c=c, sl=sl, psum=psum: t.matmul(
                    psum[:], woutb[:, 128 * c:128 * (c + 1)], outh[:, sl],
                    start=True, stop=True), inc="st")
                tsl = NT * (c % 2)
                B.on("vector", lambda v, c=c, sl=sl, tsl=tsl: v.tensor_tensor(
                    t5r[:, tsl:tsl + NT], x_sb[c][:, sl], trt(0), OP.mult),
                    inc="sv")
                B.wait("vector", "st", mkp)
                att_all[(j, c)] = B.on(
                    "vector", lambda v, c=c, psum=psum, tsl=tsl:
                    v.scalar_tensor_tensor(
                        attr[:, NT * c:NT * (c + 1)], psum[:],
                        bocol[:, c:c + 1], t5r[:, tsl:tsl + NT],
                        OP.add, OP.add), inc="sv")
            B.wait("tensor", *stat_free)
            m3_marks[j] = B.on("tensor", lambda t, sl=sl: t.matmul(
                pSm[:], rwcol[:], outh[:, sl], start=True, stop=True),
                inc="st")

        def CC1c(j):
            sl = slice(NT * j, NT * (j + 1))
            B.wait("tensor", *stat_free)
            for c in range(NCH):
                if (c % 3) in sq_used:
                    B.wait("scalar", *sq_used[c % 3])
                B.wait("scalar", "sv", att_all[(j, c)])
                mkaq = B.on("scalar", lambda s, c=c: s.activation(
                    sq_slot(c), attr[:, NT * c:NT * (c + 1)], AF.Square),
                    inc="ss")
                B.wait("tensor", "ss", mkaq)
                sq_used[c % 3] = ("st", B.on("tensor", lambda t, c=c: t.matmul(
                    pSq[:], invccol[:], sq_slot(c),
                    start=(c == 0), stop=(c == NCH - 1)), inc="st"))
            B.wait("vector", "st", m3_marks[j])
            B.wait("vector", "ss", B.mark("ss"))
            B.on("vector", lambda v, sl=sl: v.scalar_tensor_tensor(
                rmj(0), pSm[:], bbar[:], mrs1row[0:1, sl], OP.add, OP.add),
                inc="sv")
            stat_free[:] = ["sv", B.on("vector", lambda v: v.tensor_copy(
                rqj(0), pSq[:]), inc="sv")]
            chain_core(0)
            B.on("vector", lambda v: v.tensor_tensor(rowM4[0:1, NT:2 * NT], rmj(0), rowRw[0:1, 0:NT],
                                                     OP.mult), inc="sv")
            B.on("vector", lambda v, sl=sl: v.tensor_copy(p2row[0:1, sl],
                                                          rowM4[0:1, NT:2 * NT]), inc="sv")

        def CC1b(j):
            sl = slice(NT * j, NT * (j + 1))
            mk_r3 = bcastp(rowRw[0:1, 0:NT], trt(0))
            B.wait("vector", "sv", mk_r3)
            for c in range(NCH):
                tsl = NT * (c % 2)
                B.on("vector", lambda v, c=c, tsl=tsl: v.tensor_tensor(
                    t3s[:, tsl:tsl + NT], attr[:, NT * c:NT * (c + 1)],
                    trt(0), OP.mult), inc="sv")
                x2_marks[(j, c)] = B.on(
                    "vector", lambda v, c=c, sl=sl, tsl=tsl, j=j: v.tensor_tensor(
                        x2_(c, j), t3s[:, tsl:tsl + NT],
                        x_sb[c][:, sl], OP.add), inc="sv")

        def CC2a(j):
            sl = slice(NT * j, NT * (j + 1))
            B.wait("tensor", *stat_free)
            for c in range(NCH):
                B.wait("tensor", "sv", x2_marks[(j, c)])
                B.on("tensor", lambda t, c=c, j=j: t.matmul(
                    pSm[:], invccol[:], x2_(c, j),
                    start=(c == 0), stop=(c == NCH - 1)))
            mks4 = None
            for c in range(NCH):
                if (c % 3) in sq_used:
                    B.wait("scalar", *sq_used[c % 3])
                B.wait("scalar", "sv", x2_marks[(j, c)])
                mkv = B.on("scalar", lambda s, c=c, j=j: s.activation(
                    sq_slot(c), x2_(c, j), AF.Square), inc="ss")
                B.wait("tensor", "ss", mkv)
                mks4 = B.on("tensor", lambda t, c=c: t.matmul(
                    pSq[:], invccol[:], sq_slot(c),
                    start=(c == 0), stop=(c == NCH - 1)), inc="st")
                sq_used[c % 3] = ("st", mks4)
            B.wait("scalar", "st", mks4)
            stat_free[:] = ["ss", stats_evac_s(0)]
            chain_core(0)
            B.on("vector", lambda v: v.tensor_tensor(rowM4[0:1, NT:2 * NT], rmj(0), rowRw[0:1, 0:NT],
                                                     OP.mult), inc="sv")

        def CC2b(j):
            sl = slice(NT * j, NT * (j + 1))
            B.wait("vector", "sv", last_fin["sg"])
            bcastp(rowRw[0:1, 0:NT], trt(1))
            mk_m4 = bcastp(rowM4[0:1, NT:2 * NT], trtb4[:], negate=True)
            B.wait("vector", "sv", mk_m4)
            for c in range(NCH):
                tsl = NT * (c % 2)
                B.on("vector", lambda v, c=c, tsl=tsl, j=j: v.tensor_tensor(
                    t3s[:, tsl:tsl + NT], x2_(c, j), trt(1), OP.mult),
                    inc="sv")
                w_marks[(j, c)] = B.on(
                    "vector", lambda v, c=c, sl=sl, tsl=tsl: v.tensor_tensor(
                        w_sb[c][:, sl], t3s[:, tsl:tsl + NT], trtb4[:],
                        OP.subtract), inc="sv")
            if j - 1 in rk_last:
                B.wait("vector", "st", rk_last[j - 1])
            d_marks = {}
            for c in range(NCH):
                if j == 0:
                    B.on("vector", lambda v, c=c: v.tensor_copy(
                        attr[:, NT * c:NT * c + 1], w_sb[c][:, 0:1]))
                    d_marks[c] = B.on("vector", lambda v, c=c: v.tensor_tensor(
                        attr[:, NT * c + 1:NT * (c + 1)], w_sb[c][:, 1:NT],
                        w_sb[c][:, 0:NT - 1], OP.subtract), inc="sv")
                else:
                    d_marks[c] = B.on("vector", lambda v, c=c, j=j:
                                      v.tensor_tensor(
                                          attr[:, NT * c:NT * (c + 1)],
                                          w_sb[c][:, NT * j:NT * (j + 1)],
                                          w_sb[c][:, NT * j - 1:NT * (j + 1) - 1],
                                          OP.subtract), inc="sv")
            for c in range(NCH):
                if j == 0:
                    B.on("vector", lambda v, c=c: v.tensor_scalar(
                        xr_(c)[:, 0:1], attr[:, NT * c:NT * c + 1],
                        murcol[:, c:c + 1], None, OP.mult))
                    mix_marks[(j, c)] = B.on(
                        "vector", lambda v, c=c: v.scalar_tensor_tensor(
                            xr_(c)[:, 1:NT], attr[:, NT * c + 1:NT * (c + 1)],
                            murcol[:, c:c + 1], w_sb[c][:, 0:NT - 1],
                            OP.mult, OP.add), inc="sv")
                else:
                    mix_marks[(j, c)] = B.on(
                        "vector", lambda v, c=c, j=j: v.scalar_tensor_tensor(
                            xr_(c), attr[:, NT * c:NT * (c + 1)],
                            murcol[:, c:c + 1],
                            w_sb[c][:, NT * j - 1:NT * (j + 1) - 1],
                            OP.mult, OP.add), inc="sv")
            for c in range(NCH):
                if j == 0:
                    B.on("vector", lambda v, c=c: v.tensor_scalar(
                        xk_(c)[:, 0:1], attr[:, NT * c:NT * c + 1],
                        mukcol[:, c:c + 1], None, OP.mult))
                    xk_marks[(j, c)] = B.on(
                        "vector", lambda v, c=c: v.scalar_tensor_tensor(
                            xk_(c)[:, 1:NT], attr[:, NT * c + 1:NT * (c + 1)],
                            mukcol[:, c:c + 1], w_sb[c][:, 0:NT - 1],
                            OP.mult, OP.add), inc="sv")
                else:
                    xk_marks[(j, c)] = B.on(
                        "vector", lambda v, c=c, j=j: v.scalar_tensor_tensor(
                            xk_(c), attr[:, NT * c:NT * (c + 1)],
                            mukcol[:, c:c + 1],
                            w_sb[c][:, NT * j - 1:NT * (j + 1) - 1],
                            OP.mult, OP.add), inc="sv")

        rk_last = {}
        rk_last = {}
        ffn_m = {}

        def _rk_groups(j, g0, g1, sig_m, ka_m):
            groups = ([("R", c) for c in range(NCH)]
                      + [("K", c) for c in range(NCH)])
            for g in range(g0, g1):
                kind, c = groups[g]
                bank = (pAux, pQ, pK)[g % 3]
                if j == 0:
                    if g == 1 and (0, 6) in att_all:
                        B.wait("tensor", "sv", att_all[(0, 6)])
                    if g == 2 and (0, 7) in att_all:
                        B.wait("tensor", "sv", att_all[(0, 7)])
                if g == 8 and (j + 1, 7) in att_all:
                    B.wait("tensor", "sv", att_all[(j + 1, 7)])
                if g == 10 and (j + 1, 6) in att_all:
                    B.wait("tensor", "sv", att_all[(j + 1, 6)])
                if g >= 3:
                    pk, pc = groups[g - 3]
                    if pk == "R":
                        B.wait("tensor", "ss", sig_m[pc])
                    else:
                        B.wait("tensor", "ss", ka_m[pc])
                W = Wr_sb if kind == "R" else Wk_sb
                xin = xr_ if kind == "R" else xk_
                mk = None
                for cc in range(NCH):
                    if g == 0:
                        B.wait("tensor", "sv", mix_marks[(j, cc)])
                    elif g == 1 and cc == 0:
                        B.wait("tensor", "sv", mix_marks[(j, NCH - 1)])
                    elif g == 8:
                        B.wait("tensor", "sv", xk_marks[(j, cc)])
                    elif g == 9 and cc == 0:
                        B.wait("tensor", "sv", xk_marks[(j, NCH - 1)])
                    mk = B.on("tensor", lambda t, c=c, cc=cc, bank=bank, W=W,
                              xin=xin: t.matmul(
                                  bank[:],
                                  W[:, C * cc + 128 * c:C * cc + 128 * (c + 1)],
                                  xin(cc), start=(cc == 0),
                                  stop=(cc == NCH - 1)),
                              inc="st" if cc == NCH - 1 else None) or mk
                if kind == "R":
                    B.wait("scalar", "st", mk)
                    sig_m[c] = B.on("scalar", lambda s, c=c, bank=bank:
                                    s.activation(rr_(c), bank[:], AF.Sigmoid),
                                    inc="ss")
                else:
                    B.wait("scalar", "st", mk)
                    if c == 4:
                        B.wait("scalar", "sv",
                               att_all[(min(j + 1, NJ - 1), NCH - 1)])
                    if c == 6:
                        B.wait("scalar", "sv", last_fin["sg"])
                    if (c % 3) in sq_used:
                        B.wait("scalar", *sq_used[c % 3])
                    B.on("scalar", lambda s, c=c, bank=bank:
                         s.activation(sq_slot(c), bank[:], AF.Relu),
                         inc="ss")
                    ka_m[c] = B.on("scalar", lambda s, c=c: s.activation(
                        ka_(c), sq_slot(c), AF.Square), inc="ss")
                    sq_used[c % 3] = ("ss", ka_m[c])
                if g == 2 * NCH - 1:
                    rk_last[j] = mk

        def EEa1A(j):
            sig_m = {}
            ka_m = {}
            ffn_m[j] = (sig_m, ka_m)
            _rk_groups(j, 0, 8, sig_m, ka_m)

        def EEa1B(j):
            sig_m, ka_m = ffn_m[j]
            _rk_groups(j, 8, 16, sig_m, ka_m)

        def EEa2a(j):
            sl = slice(NT * j, NT * (j + 1))
            sig_m, ka_m = ffn_m[j]
            B.wait("tensor", "ss", ka_m[NCH - 1])
            B.wait("tensor", "ss", sig_m[NCH - 1])
            B.wait("vector", "ss", B.mark("ss"))
            for c in range(NCH):
                bV = (pV, pQ, pK)[c % 3]
                if c == 0 and (j + 1, 5) in att_all:
                    B.wait("tensor", "sv", att_all[(j + 1, 5)])
                if c >= 3:
                    B.wait("tensor", "sv", o_all[(j, c - 3)])
                mkV = None
                for cc in range(NCH):
                    mkV = B.on("tensor", lambda t, c=c, cc=cc, bV=bV: t.matmul(
                        bV[:], Wv_sb[:, C * cc + 128 * c:C * cc + 128 * (c + 1)],
                        ka_(cc), start=(cc == 0), stop=(cc == NCH - 1)),
                        inc="st" if cc == NCH - 1 else None) or mkV
                B.wait("vector", "st", mkV)
                o_all[(j, c)] = B.on("vector", lambda v, c=c, bV=bV:
                                     v.tensor_tensor(
                                         oring[:, NT * c:NT * (c + 1)], bV[:],
                                         rr_(c), OP.mult), inc="sv")
        def EEa2b(j):
            sl = slice(NT * j, NT * (j + 1))
            B.wait("tensor", *stat_free)
            for c in range(NCH):
                B.wait("tensor", "sv", o_all[(j, c)])
                B.on("tensor", lambda t, c=c: t.matmul(
                    pSm[:], invccol[:], oring[:, NT * c:NT * (c + 1)],
                    start=(c == 0), stop=(c == NCH - 1)))
            mks5 = None
            for c in range(NCH):
                if (c % 3) in sq_used:
                    B.wait("scalar", *sq_used[c % 3])
                B.wait("scalar", "sv", o_all[(j, c)])
                mkv = B.on("scalar", lambda s, c=c: s.activation(
                    sq_slot(c), oring[:, NT * c:NT * (c + 1)], AF.Square),
                    inc="ss")
                B.wait("tensor", "ss", mkv)
                mks5 = B.on("tensor", lambda t, c=c: t.matmul(
                    pSq[:], invccol[:], sq_slot(c),
                    start=(c == 0), stop=(c == NCH - 1)), inc="st")
                sq_used[c % 3] = ("st", mks5)
            B.wait("scalar", "st", mks5)
            stat_free[:] = ["ss", stats_evac_s(0)]
            chain_core(0)
            B.on("vector", lambda v: v.tensor_tensor(rowM4[0:1, NT:2 * NT], rmj(0), rowRw[0:1, 0:NT],
                                                     OP.mult), inc="sv")
            B.on("vector", lambda v, sl=sl: v.tensor_tensor(
                rowM4[0:1, NT:2 * NT], rowM4[0:1, NT:2 * NT], p2row[0:1, sl], OP.add), inc="sv")

        def EEa2c(j):
            bcastp(rowRw[0:1, 0:NT], trt(1))
            return bcastp(rowM4[0:1, NT:2 * NT], trtb4[:], negate=True)

        def EEb(j, mk_cl):
            sl = slice(NT * j, NT * (j + 1))
            B.wait("vector", "sv", mk_cl)
            B.wait("gpsimd", "sv", mk_cl)
            last = None
            fin_slot = {}
            for c in range(NCH):
                slot = c % 2
                tA = NT * (2 + c % 2)
                if c % 2 == 0:
                    if slot in fin_slot:
                        B.wait("gpsimd", "sv", fin_slot[slot])
                    B.on("gpsimd", lambda g, c=c, tA=tA: g.tensor_tensor(
                        t5r[:, tA:tA + NT], oring[:, NT * c:NT * (c + 1)],
                        trt(1), OP.mult), inc="sg")
                    mg = B.on("gpsimd", lambda g, tA=tA: g.tensor_tensor(
                        t5r[:, tA:tA + NT], t5r[:, tA:tA + NT],
                        trtb4[:], OP.subtract), inc="sg")
                    B.wait("vector", "sg", mg)
                else:
                    B.on("vector", lambda v, c=c, tA=tA: v.tensor_tensor(
                        t5r[:, tA:tA + NT], oring[:, NT * c:NT * (c + 1)],
                        trt(1), OP.mult), inc="sv")
                    B.on("vector", lambda v, tA=tA: v.tensor_tensor(
                        t5r[:, tA:tA + NT], t5r[:, tA:tA + NT],
                        trtb4[:], OP.subtract), inc="sv")
                if slot in out_dma_marks:
                    B.wait("vector", "dO", out_dma_marks[slot])
                last = B.on("vector", lambda v, c=c, slot=slot, tA=tA, j=j:
                            v.tensor_tensor(
                                outr[:, NT * slot:NT * (slot + 1)],
                                t5r[:, tA:tA + NT],
                                x2_(c, j), OP.add), inc="sv")
                fin_slot[slot] = last
                B.wait("sync", "sv", last)
                out_dma_marks[slot] = B.dma(
                    "sync", yT[128 * c:128 * (c + 1), sl],
                    outr[:, NT * slot:NT * (slot + 1)], "dO")
            last_fin["sg"] = last

        CC1a(0)
        CC1c(0)
        CC1b(0)
        CC2a(0)
        CC2b(0)
        EEa1A(0)
        CC1a(1)
        CC1c(1)
        EEa1B(0)
        CC1b(1)
        CC2a(1)
        EEa2a(0)
        CC2b(1)
        EEa2b(0)
        EEa1A(1)
        CC1a(2)
        cl0 = EEa2c(0)
        CC1c(2)
        EEb(0, cl0)
        EEa1B(1)
        CC1b(2)
        CC2a(2)
        EEa2a(1)
        CC2b(2)
        EEa2b(1)
        EEa1A(2)
        CC1a(3)
        cl1 = EEa2c(1)
        CC1c(3)
        EEb(1, cl1)
        EEa1B(2)
        CC1b(3)
        CC2a(3)
        EEa2a(2)
        CC2b(3)
        EEa2b(2)
        EEa1A(3)
        cl2 = EEa2c(2)
        EEb(2, cl2)
        EEa1B(3)
        EEa2a(3)
        EEa2b(3)
        cl3 = EEa2c(3)
        EEb(3, cl3)

        if debug:
            B.wait("sync", "sv", B.mark("sv"))
            for c in range(NCH):
                B.dma("sync", dbg["d_x2"][128 * c:128 * (c + 1), 0:2 * NT],
                      x2_sb[c][:], "dO")
                B.dma("sync", dbg["d_w"][128 * c:128 * (c + 1), :], w_sb[c][:],
                      "dO")
            B.dma("sync", dbg["d_or"], oring[:], "dO")
            B.dma("sync", dbg["d_xr"], k_sb[:], "dO")
            B.dma("sync", dbg["d_xk"], v_sb[:], "dO")
            B.dma("sync", dbg["d_xk2"], vtm[:], "dO")
            B.dma("sync", dbg["d_rr"], Etm[:], "dO")
            B.dma("sync", dbg["d_ka"], outh[:], "dO")
        B.wait("sync", "dO", B.mark("dO"))

        block = E_(nc.Block())

        @block.sync
        def _(e):
            for fn in B.ops["sync"]:
                fn(e)

        @block.gpsimd
        def _(e):
            for fn in B.ops["gpsimd"]:
                fn(e)

        @block.vector
        def _(e):
            for fn in B.ops["vector"]:
                fn(e)

        @block.scalar
        def _(e):
            for fn in B.ops["scalar"]:
                fn(e)

        @block.tensor
        def _(e):
            for fn in B.ops["tensor"]:
                fn(e)

    return nc


_NC_CACHE = {}


def _get_nc(debug=False):
    if debug not in _NC_CACHE:
        _NC_CACHE[debug] = build_nc(debug)
    return _NC_CACHE[debug]


def kernel(L0_attn_x, L0_attn_gamma, L0_attn_beta, L0_attn_w_qkv, L0_attn_w_out,
           L0_attn_b_out, L0_ffn_W_r, L0_ffn_W_k, L0_ffn_W_v, L0_ffn_mu_r,
           L0_ffn_mu_k, _debug=False):
    x = np.ascontiguousarray(L0_attn_x, dtype=np.float32)
    shared = {
        "w_qkv": np.ascontiguousarray(L0_attn_w_qkv, np.float32),
        "w_out": np.ascontiguousarray(L0_attn_w_out, np.float32),
        "WrT": np.ascontiguousarray(np.asarray(L0_ffn_W_r, np.float32).T),
        "WkT": np.ascontiguousarray(np.asarray(L0_ffn_W_k, np.float32).T),
        "WvT": np.ascontiguousarray(np.asarray(L0_ffn_W_v, np.float32).T),
        "vgamma": np.ascontiguousarray(np.asarray(L0_attn_gamma, np.float32).ravel()),
        "vbeta": np.ascontiguousarray(np.asarray(L0_attn_beta, np.float32).ravel()),
        "vbout": np.ascontiguousarray(np.asarray(L0_attn_b_out, np.float32).ravel()),
        "vmur": np.ascontiguousarray(L0_ffn_mu_r, np.float32),
        "vmuk": np.ascontiguousarray(L0_ffn_mu_k, np.float32),
    }
    in_maps = []
    for b in range(8):
        m = dict(shared)
        m["xT"] = np.ascontiguousarray(x[b].T)
        in_maps.append(m)
    nc = _get_nc(_debug)
    res = run_bass_kernel_spmd(nc, in_maps, core_ids=list(range(8)))
    out = np.stack([np.asarray(r["yT"]).T for r in res.results])
    if _debug:
        return out, res.results
    return out


# version 42
# speedup vs baseline: 1.0696x; 1.0134x over previous
import contextlib
import numpy as np
import concourse.bass as bass
import concourse.mybir as mybir
from concourse.bass_utils import run_bass_kernel_spmd
from concourse.alu_op_type import AluOpType as OP

F32 = mybir.dt.float32
BF16 = mybir.dt.bfloat16
AF = mybir.ActivationFunctionType
AX = mybir.AxisListType

T = 2048
C = 1024
NCH = 8
NJ = 4
NT = 512
EPS = 1e-5
INV_C = 1.0 / C

ENGINES = ("sync", "gpsimd", "vector", "scalar", "tensor")


class Builder:
    def __init__(self, nc, sems):
        self.nc = nc
        self.sems = sems
        self.cnt = {k: 0 for k in sems}
        self.ops = {e: [] for e in ENGINES}

    def on(self, eng, fn, inc=None):
        sem = self.sems[inc] if inc else None
        if inc:
            self.cnt[inc] += 1

        def run(e, fn=fn, sem=sem):
            inst = fn(e)
            if sem is not None:
                inst.then_inc(sem, 1)
        self.ops[eng].append(run)
        return self.cnt[inc] if inc else None

    def dma(self, eng, out, in_, inc, slow=False):
        self.cnt[inc] += 16
        sem = self.sems[inc]

        def run(e, out=out, in_=in_, sem=sem, slow=slow):
            kw = dict(allow_slow_non_contiguous=True) if slow else {}
            e.dma_start(out=out, in_=in_, **kw).then_inc(sem, 16)
        self.ops[eng].append(run)
        return self.cnt[inc]

    def mark(self, sem):
        return self.cnt[sem]

    def wait(self, eng, sem, val=None):
        if val is None:
            val = self.cnt[sem]
        s = self.sems[sem]

        def run(e, s=s, val=val):
            e.wait_ge(s, val)
        self.ops[eng].append(run)


def build_nc(debug=False):
    nc = bass.Bass("TRN2", target_bir_lowering=False)

    xT = nc.dram_tensor("xT", [C, T], F32, kind="ExternalInput").ap()
    w_qkv = nc.dram_tensor("w_qkv", [C, 384], F32, kind="ExternalInput").ap()
    w_out = nc.dram_tensor("w_out", [128, C], F32, kind="ExternalInput").ap()
    WrT = nc.dram_tensor("WrT", [C, C], F32, kind="ExternalInput").ap()
    WkT = nc.dram_tensor("WkT", [C, C], F32, kind="ExternalInput").ap()
    WvT = nc.dram_tensor("WvT", [C, C], F32, kind="ExternalInput").ap()
    vgamma = nc.dram_tensor("vgamma", [128, NCH], F32, kind="ExternalInput").ap()
    vbeta = nc.dram_tensor("vbeta", [128, NCH], F32, kind="ExternalInput").ap()
    vbout = nc.dram_tensor("vbout", [128, NCH], F32, kind="ExternalInput").ap()
    vmur = nc.dram_tensor("vmur", [128, NCH], F32, kind="ExternalInput").ap()
    vmuk = nc.dram_tensor("vmuk", [128, NCH], F32, kind="ExternalInput").ap()
    yT = nc.dram_tensor("yT", [C, T], F32, kind="ExternalOutput").ap()

    dbg = {}
    if debug:
        for nm, shp, dt in [("d_x2", [C, T], BF16), ("d_w", [C, T], BF16),
                            ("d_or", [128, NCH * NT], BF16),
                            ("d_xr", [128, T], BF16), ("d_xk", [128, T], BF16),
                            ("d_xk2", [128, T], BF16), ("d_rr", [128, T], BF16),
                            ("d_ka", [128, T], BF16)]:
            dbg[nm] = nc.dram_tensor(nm, shp, dt, kind="ExternalOutput").ap()

    with contextlib.ExitStack() as ST:
        E_ = ST.enter_context

        x_sb = [E_(nc.sbuf_tensor(f"x{c}", [128, T], BF16)) for c in range(NCH)]
        x2_sb = [E_(nc.sbuf_tensor(f"x2_{c}", [128, 2 * NT], BF16)) for c in range(NCH)]
        q_sb = E_(nc.sbuf_tensor("q", [128, T], BF16))
        k_sb = E_(nc.sbuf_tensor("k", [128, T], BF16))
        v_sb = E_(nc.sbuf_tensor("v", [128, T], BF16))
        Eb = E_(nc.sbuf_tensor("Eb", [128, T], BF16))
        Etm = E_(nc.sbuf_tensor("Etm", [128, T], BF16))
        vtm = E_(nc.sbuf_tensor("vtm", [128, T], BF16))
        outh = E_(nc.sbuf_tensor("outh", [128, T], BF16))
        t5r = E_(nc.sbuf_tensor("t5r", [128, T], BF16))
        Wr_sb = E_(nc.sbuf_tensor("Wr", [128, NCH * C], BF16))
        Wk_sb = E_(nc.sbuf_tensor("Wk", [128, NCH * C], BF16))
        Wv_sb = E_(nc.sbuf_tensor("Wv", [128, NCH * C], BF16))
        woutb = E_(nc.sbuf_tensor("woutb", [128, C], BF16))
        attr = E_(nc.sbuf_tensor("attr", [128, NCH * NT], BF16))
        oring = E_(nc.sbuf_tensor("oring", [128, NCH * NT], BF16))
        sqr = E_(nc.sbuf_tensor("sqr", [128, 3 * NT], BF16))
        t3s = E_(nc.sbuf_tensor("t3s", [128, 2 * NT], BF16))
        rrB = E_(nc.sbuf_tensor("rrB", [128, 4 * NT], BF16))

        def _alias(name, backing):
            t_ = nc.virtual_sbuf_tensor(name, [128, T], BF16)
            nc.back_virtual_tensor_with(t_, backing)
            return t_
        w_sb = [_alias(f"w{c}", x_sb[c]) for c in range(NCH)]
        xrA = _alias("xrA", k_sb)
        xrB = _alias("xrB", Eb)
        xkA = _alias("xkA", v_sb)
        xkB = _alias("xkB", vtm)
        rrA = _alias("rrA", Etm)
        kaA = _alias("kaA", outh)
        kaB = _alias("kaB", t5r)

        def _seg(tA, tB, c):
            t_ = tA if c < 4 else tB
            return t_[:, NT * (c % 4):NT * (c % 4 + 1)]

        def x2_(c, j):
            return x2_sb[c][:, NT * (j % 2):NT * (j % 2 + 1)]

        def xr_(c):
            return _seg(xrA, xrB, c)

        def xk_(c):
            return _seg(xkA, xkB, c)

        def rr_(c):
            if c < 4:
                return rrA[:, NT * c:NT * (c + 1)]
            return rrB[:, NT * (c - 4):NT * (c - 3)]

        def ka_(c):
            return _seg(kaA, kaB, c)

        rsrow = E_(nc.sbuf_tensor("rsrow", [1, T], BF16))
        rowM4 = E_(nc.sbuf_tensor("rowM4", [1, T], F32))
        rowQ4 = E_(nc.sbuf_tensor("rowQ4", [1, T], F32))
        wT0 = E_(nc.sbuf_tensor("wT0", [1, T], F32))
        rowRw = E_(nc.sbuf_tensor("rowRw", [1, T], F32))
        mrs1row = E_(nc.sbuf_tensor("mrs1row", [1, T], BF16))
        p2row = E_(nc.sbuf_tensor("p2row", [1, T], BF16))
        s4hi = p2row

        outr = E_(nc.sbuf_tensor("outr", [128, 2 * NT], F32))
        TRT = E_(nc.sbuf_tensor("TRT", [128, 2 * NT], BF16))
        trtb4 = E_(nc.sbuf_tensor("trtb4", [128, NT], BF16))

        def trt(i):
            return TRT[:, NT * i:NT * (i + 1)]

        gcol = E_(nc.sbuf_tensor("gcol", [128, 8], F32))
        bcol = E_(nc.sbuf_tensor("bcol", [128, 8], F32))
        bocol = E_(nc.sbuf_tensor("bocol", [128, 8], F32))
        murcol = E_(nc.sbuf_tensor("murcol", [128, 8], F32))
        mukcol = E_(nc.sbuf_tensor("mukcol", [128, 8], F32))
        bcolb2 = E_(nc.sbuf_tensor("bcolb2", [128, 16], BF16, align_bytes=4))
        onescol = E_(nc.sbuf_tensor("onescol", [128, 1], BF16, align_bytes=4))
        invccol = E_(nc.sbuf_tensor("invccol", [128, 1], BF16, align_bytes=4))
        ones1 = E_(nc.sbuf_tensor("ones1", [1, 128], BF16, align_bytes=4))
        bocolb = E_(nc.sbuf_tensor("bocolb", [128, 8], BF16, align_bytes=4))
        cb_sb = E_(nc.sbuf_tensor("cb_sb", [128, 128], BF16))
        idb = nc.virtual_sbuf_tensor("idb", [128, 128], BF16)
        nc.back_virtual_tensor_with(idb, cb_sb)
        io_c = E_(nc.sbuf_tensor("io_c", [128, 128], BF16))
        maskb = nc.virtual_sbuf_tensor("maskb", [128, 128], BF16)
        nc.back_virtual_tensor_with(maskb, io_c)
        io_r = E_(nc.sbuf_tensor("io_r", [128, 1], F32, align_bytes=4))
        rspr = E_(nc.sbuf_tensor("rspr", [128, 1], F32, align_bytes=4))
        kmax = E_(nc.sbuf_tensor("kmax", [128, 1], F32, align_bytes=4))
        ssum = E_(nc.sbuf_tensor("ssum", [128, 1], F32, align_bytes=4))
        rwcol = E_(nc.sbuf_tensor("rwcol", [128, 1], BF16, align_bytes=4))
        rwf = E_(nc.sbuf_tensor("rwf", [128, 1], F32, align_bytes=4))
        bbar = E_(nc.sbuf_tensor("bbar", [1, 1], F32, align_bytes=4))
        epsc = E_(nc.sbuf_tensor("epsc", [1, 1], F32, align_bytes=4))
        pb8 = E_(nc.sbuf_tensor("pb8", [1, 8], F32, align_bytes=4))

        pQ = E_(nc.psum_tensor("pQ", [128, NT], F32))
        pK = E_(nc.psum_tensor("pK", [128, NT], F32))
        pV = E_(nc.psum_tensor("pV", [128, NT], F32))
        pSm = E_(nc.psum_tensor("pSm", [1, NT], F32))
        pSq = E_(nc.psum_tensor("pSq", [1, NT], F32))
        pB = E_(nc.psum_tensor("pB", [128, NT], F32))
        pAux = E_(nc.psum_tensor("pAux", [128, NT], F32))
        pT = E_(nc.psum_tensor("pT", [128, 1024], BF16))

        sems = {}
        for s in ("dX", "dW", "dM", "dO", "sv", "ss", "st", "sg"):
            sems[s] = E_(nc.semaphore(s))
        B = Builder(nc, sems)

        mark_xj = {}
        for c in range(NCH):
            B.dma("gpsimd", x_sb[c][:, 0:NT],
                  xT[128 * c:128 * c + 128, 0:NT], "dX")
        mark_xj[0] = B.mark("dX")
        for dst, src in ((gcol, vgamma), (bcol, vbeta), (bocol, vbout),
                         (murcol, vmur), (mukcol, vmuk)):
            B.dma("sync", dst[:], src, "dM")
        mark_vecs = B.mark("dM")
        B.dma("gpsimd", Wr_sb[:, 0:NCH * 384].rearrange("p (a k) -> p a k", k=384),
              w_qkv.rearrange("(a p) k -> p a k", p=128), "dW")
        B.dma("gpsimd", woutb[:], w_out, "dW")
        mark_wqkv = B.mark("dW")
        for j in range(1, NJ):
            for c in range(NCH):
                B.dma("gpsimd", x_sb[c][:, NT * j:NT * (j + 1)],
                      xT[128 * c:128 * c + 128, NT * j:NT * (j + 1)], "dX")
            mark_xj[j] = B.mark("dX")
        mark_x = B.mark("dX")
        B.dma("gpsimd", Wv_sb[:].rearrange("p (a k) -> p a k", k=C),
              WvT.rearrange("(a p) k -> p a k", p=128), "dW")

        B.on("gpsimd", lambda g: g.iota(io_c[:], pattern=[[1, 128]], base=0,
                                        channel_multiplier=0,
                                        allow_small_or_imprecise_dtypes=True))
        mark_iota = B.on("gpsimd", lambda g: g.iota(
            io_r[:], pattern=[[0, 1]], base=0, channel_multiplier=1,
            allow_small_or_imprecise_dtypes=True), inc="sg")

        B.on("vector", lambda v: v.memset(onescol[:], 1.0))
        B.on("vector", lambda v: v.memset(invccol[:], INV_C))
        B.on("vector", lambda v: v.memset(ones1[:], 1.0))
        B.on("vector", lambda v: v.memset(epsc[:], EPS))
        B.on("vector", lambda v: v.memset(bcolb2[:], 0.0))
        B.wait("vector", "sg", mark_iota)
        mark_const = B.on("vector", lambda v: v.tensor_scalar(
            idb[:], io_c[:], io_r[:], None, OP.is_equal), inc="sv")
        B.on("vector", lambda v: v.memset(maskb[:], 0.0))
        for h in range(4):
            B.on("vector", lambda v, h=h: v.memset(
                maskb[32 * h:32 * h + 32, 32 * h:32 * h + 32], 32.0 ** -0.5))

        B.wait("vector", "dW", mark_wqkv)
        B.wait("vector", "dM", mark_vecs)
        for c in range(NCH):
            B.on("vector", lambda v, c=c: v.tensor_scalar(
                Wk_sb[:, 384 * c:384 * (c + 1)], Wr_sb[:, 384 * c:384 * (c + 1)],
                gcol[:, c:c + 1], None, OP.mult), inc="sv")
        B.on("vector", lambda v: v.tensor_copy(
            bcolb2[:].rearrange("p (a two) -> p a two", two=2)[:, :, 0:1],
            bcol[:].rearrange("p (a one) -> p a one", one=1)), inc="sv")
        mk_rwf = B.on("vector", lambda v: v.reduce_sum(rwf[:], woutb[:],
                                                       axis=AX.X), inc="sv")
        B.on("vector", lambda v: v.tensor_copy(bocolb[:], bocol[:]), inc="sv")
        B.wait("scalar", "sv", mk_rwf)
        B.on("scalar", lambda s: s.mul(rwcol[:], rwf[:], INV_C), inc="ss")
        mark_prep = B.mark("sv")
        mark_prep_s = B.mark("ss")

        B.wait("tensor", "sv", mark_prep)
        mark_bqcw = None
        for c in range(NCH):
            B.on("tensor", lambda t, c=c: t.matmul(
                pB[0:1, 0:384], bcolb2[:, 2 * c:2 * c + 1],
                Wr_sb[:, 384 * c:384 * (c + 1)], start=(c == 0),
                stop=(c == NCH - 1)))
            mark_bqcw = B.on("tensor", lambda t, c=c: t.matmul(
                pAux[0:1, 0:384], onescol[:], Wk_sb[:, 384 * c:384 * (c + 1)],
                start=(c == 0), stop=(c == NCH - 1)),
                inc="st" if c == NCH - 1 else None) or mark_bqcw
        mark_bo = B.on("tensor", lambda t: t.matmul(
            pB[0:1, 384:392], onescol[:], bocolb[:], start=True, stop=True),
            inc="st")
        B.wait("vector", "st", mark_bqcw)
        B.on("vector", lambda v: v.tensor_copy(attr[0:1, 0:384], pB[0:1, 0:384]),
             inc="sv")
        B.on("vector", lambda v: v.tensor_copy(attr[0:1, 384:768],
                                               pAux[0:1, 0:384]), inc="sv")
        mark_cwbq = B.mark("sv")
        B.wait("vector", "st", mark_bo)
        mk_pb8 = B.on("vector", lambda v: v.reduce_sum(pb8[:, 0:1],
                                                       pB[0:1, 384:392],
                                                       axis=AX.X), inc="sv")
        B.wait("scalar", "sv", mk_pb8)
        B.on("scalar", lambda s: s.mul(bbar[:], pb8[:, 0:1], INV_C), inc="ss")
        B.wait("gpsimd", "st", mark_bo)
        B.wait("gpsimd", "sv", mark_prep)
        B.dma("gpsimd", Wr_sb[:].rearrange("p (a k) -> p a k", k=C),
              WrT.rearrange("(a p) k -> p a k", p=128), "dW")

        sq_used = {}
        ob_used = sq_used

        def sq_slot(c):
            return sqr[:, NT * (c % 3):NT * (c % 3 + 1)]

        def ob_slot(c):
            return sq_slot(c)

        def rmj(j):
            return rowM4[0:1, 0:NT]

        def rqj(j):
            return rowQ4[0:1, 0:NT]

        def chain_core(j=0):
            B.wait("scalar", "sv", B.mark("sv"))
            B.on("scalar", lambda s, j=j: s.activation(wT0[0:1, 0:NT], rmj(j),
                                                       AF.Square), inc="ss")
            B.wait("vector", "ss", B.mark("ss"))
            mkv = B.on("vector", lambda v, j=j: v.tensor_tensor(
                rowM4[0:1, NT:2 * NT], rqj(j), wT0[0:1, 0:NT], OP.subtract), inc="sv")
            B.wait("scalar", "sv", mkv)
            B.on("scalar", lambda s: s.activation(
                wT0[0:1, 0:NT], rowM4[0:1, NT:2 * NT], AF.Ln, bias=epsc[:], scale=1.0), inc="ss")
            mk = B.on("scalar", lambda s: s.activation(
                rowRw[0:1, 0:NT], wT0[0:1, 0:NT], AF.Exp, scale=-0.5), inc="ss")
            B.wait("vector", "ss", mk)
            return mk

        def stats_evac_s(j=0):
            B.wait("scalar", "sv", B.mark("sv"))
            B.on("scalar", lambda s, j=j: s.copy(rmj(j), pSm[:]))
            return B.on("scalar", lambda s, j=j: s.copy(rqj(j), pSq[:]),
                        inc="ss")

        def stats_evac(j=0):
            B.wait("vector", "ss", B.mark("ss"))
            B.on("vector", lambda v, j=j: v.tensor_copy(rmj(j), pSm[:]),
                 inc="sv")
            return B.on("vector", lambda v, j=j: v.tensor_copy(rqj(j), pSq[:]),
                        inc="sv")

        def bcast(src_row, dst_tile, negate=False):
            mk = B.on("vector", lambda v: v.tensor_copy(rsrow[0:1, 0:NT], src_row),
                      inc="sv")
            B.wait("tensor", "sv", mk)
            mk2 = B.on("tensor", lambda t: t.matmul(pB[:], ones1[:],
                                                    rsrow[0:1, 0:NT],
                                                    start=True, stop=True),
                       inc="st")
            B.wait("vector", "st", mk2)
            return B.on("vector", lambda v, dst=dst_tile: v.tensor_copy(
                dst, pB[:]), inc="sv")

        mark_setup_ss = B.mark("ss")
        B.wait("tensor", "sv", mark_cwbq)
        B.wait("tensor", "ss", mark_setup_ss)
        B.wait("vector", "ss", mark_setup_ss)
        qkv_out = (q_sb, k_sb, v_sb)
        evac_marks = {}
        stA_evac = {}

        def stats_j(j):
            sl = slice(NT * j, NT * (j + 1))
            B.wait("tensor", "dX", mark_xj[j])
            B.wait("scalar", "dX", mark_xj[j])
            if j > 0:
                B.wait("tensor", "ss", stA_evac[j - 1])
            mkm = None
            for c in range(NCH):
                mkm = B.on("tensor", lambda t, c=c, sl=sl: t.matmul(
                    pSm[:], invccol[:], x_sb[c][:, sl],
                    start=(c == 0), stop=(c == NCH - 1)),
                    inc="st" if c == NCH - 1 else None) or mkm
            mks = None
            for c in range(NCH):
                if (c % 3) in sq_used:
                    B.wait("scalar", *sq_used[c % 3])
                mkv = B.on("scalar", lambda s, c=c, sl=sl: s.activation(
                    sq_slot(c), x_sb[c][:, sl], AF.Square), inc="ss")
                B.wait("tensor", "ss", mkv)
                mks = B.on("tensor", lambda t, c=c: t.matmul(
                    pSq[:], invccol[:], sq_slot(c),
                    start=(c == 0), stop=(c == NCH - 1)), inc="st")
                sq_used[c % 3] = ("st", mks)
            B.wait("scalar", "st", mks)
            B.on("scalar", lambda s, sl=sl: s.copy(rowM4[0:1, sl], pSm[:]))
            stA_evac[j] = B.on("scalar", lambda s, sl=sl: s.copy(
                rowQ4[0:1, sl], pSq[:]), inc="ss")

        def chain_wide():
            mk0 = B.on("scalar", lambda s: s.activation(
                wT0[:], rowM4[:], AF.Square), inc="ss")
            B.wait("vector", "ss", mk0)
            mkv = B.on("vector", lambda v: v.tensor_tensor(
                rowQ4[:], rowQ4[:], wT0[:], OP.subtract), inc="sv")
            B.wait("scalar", "sv", mkv)
            B.on("scalar", lambda s: s.activation(
                wT0[:], rowQ4[:], AF.Ln, bias=epsc[:], scale=1.0), inc="ss")
            mke = B.on("scalar", lambda s: s.activation(
                rowRw[:], wT0[:], AF.Exp, scale=-0.5), inc="ss")
            B.wait("vector", "ss", mke)
            B.on("vector", lambda v: v.tensor_tensor(
                mrs1row[0:1, :], rowM4[:], rowRw[:], OP.mult), inc="sv")
            B.on("vector", lambda v: v.tensor_scalar(
                oring[0:1, 0:T], rowM4[:], -1.0, None, OP.mult), inc="sv")
            mku = B.on("vector", lambda v: v.tensor_scalar(
                rowM4[:], rowQ4[:], 1.0 + EPS, EPS * EPS, OP.mult, OP.add),
                inc="sv")
            B.wait("scalar", "sv", mku)
            B.on("scalar", lambda s: s.activation(
                wT0[:], rowM4[:], AF.Ln), inc="ss")
            mk_s4 = B.on("scalar", lambda s: s.activation(
                s4hi[0:1, :], wT0[:], AF.Exp, scale=-0.5), inc="ss")
            mk_rows = B.on("scalar", lambda s: s.activation(
                rsrow[0:1, :], wT0[:], AF.Exp, scale=0.5), inc="ss")
            return mk_s4, mk_rows

        def Gbig(j):
            sl = slice(NT * j, NT * (j + 1))
            B.wait("tensor", "dX", mark_xj[j])
            for oi, psum in enumerate((pQ, pK, pV)):
                osl = slice(128 * oi, 128 * (oi + 1))
                if (j - 1, oi) in evac_marks:
                    B.wait("tensor", "sv", evac_marks[(j - 1, oi)])
                for c in range(NCH):
                    B.on("tensor", lambda t, c=c, osl=osl, sl=sl, psum=psum:
                         t.matmul(psum[:],
                                  Wk_sb[:, 384 * c + osl.start:384 * c + osl.stop],
                                  x_sb[c][:, sl], start=(c == 0), stop=False))

        def sbcast(j, mk_s4=None):
            if mk_s4 is not None:
                B.wait("tensor", "ss", mk_s4)
            if last_bcastA[0] is not None:
                B.wait("tensor", "sv", last_bcastA[0])
            mkb = B.on("tensor", lambda t, j=j: t.matmul(
                pB[:], ones1[:], s4hi[0:1, NT * j:NT * (j + 1)],
                start=True, stop=True), inc="st")
            B.wait("vector", "st", mkb)
            last_bcastA[0] = B.on("vector", lambda v, j=j: v.tensor_copy(
                trt(j % 2), pB[:]), inc="sv")

        def Gtail(j, mk_rows=None):
            sl = slice(NT * j, NT * (j + 1))
            jj4 = NT * j
            if mk_rows is not None:
                B.wait("tensor", "ss", mk_rows)
            for oi, (psum, sbout) in enumerate(zip((pQ, pK, pV), qkv_out)):
                osl = slice(128 * oi, 128 * (oi + 1))
                B.on("tensor", lambda t, osl=osl, jj4=jj4, psum=psum: t.matmul(
                    psum[:], attr[0:1, 384 + osl.start:384 + osl.stop], oring[0:1, jj4:jj4 + NT],
                    start=False, stop=False))
                mkq = B.on("tensor", lambda t, osl=osl, jj4=jj4, psum=psum:
                           t.matmul(psum[:], attr[0:1, osl.start:osl.stop],
                                    rsrow[0:1, jj4:jj4 + NT],
                                    start=False, stop=True), inc="st")
                B.wait("vector", "st", mkq)
                evac_marks[(j, oi)] = B.on(
                    "vector", lambda v, sbout=sbout, sl=sl, psum=psum, j=j:
                    v.tensor_tensor(sbout[:, sl], psum[:], trt(j % 2), OP.mult),
                    inc="sv")

        last_bcastA = [None]
        stats_j(0)
        Gbig(0)
        stats_j(1)
        stats_j(2)
        stats_j(3)
        mk_s4, mk_rows = chain_wide()
        sbcast(0, mk_s4)
        Gtail(0, mk_rows)
        sbcast(1)
        Gbig(1)
        Gtail(1)
        sbcast(2)
        Gbig(2)
        Gtail(2)
        sbcast(3)
        Gbig(3)
        Gtail(3)
        mark_qkv = B.mark("sv")
        B.wait("gpsimd", "sv", mark_qkv)
        B.dma("gpsimd", Wk_sb[:].rearrange("p (a k) -> p a k", k=C),
              WkT.rearrange("(a p) k -> p a k", p=128), "dW")
        mark_wffn = B.mark("dW")

        B.wait("vector", "sv", mark_qkv)
        B.on("vector", lambda v: v.reduce_max(kmax[:], k_sb[:], axis=AX.X,
                                              negate=True), inc="sv")
        B.wait("scalar", "sv", B.mark("sv"))
        mark_E = B.on("scalar", lambda s: s.activation(
            Eb[:], k_sb[:], AF.Exp, bias=kmax[:], scale=1.0),
            inc="ss")
        B.wait("vector", "ss", mark_E)
        B.on("vector", lambda v: v.reduce_sum(ssum[:], Eb[:], axis=AX.X),
             inc="sv")
        B.wait("tensor", "ss", mark_E)
        B.wait("tensor", "sv", mark_const)
        for src, dst in ((Eb, Etm), (v_sb, vtm)):
            for g in range(2):
                B.wait("tensor", "sv", B.mark("sv"))
                mkt = None
                for b_ in range(8):
                    blk = 8 * g + b_
                    mkt = B.on("tensor", lambda t, src=src, b_=b_, blk=blk:
                               t.transpose(pT[:, 128 * b_:128 * (b_ + 1)],
                                           src[:, 128 * blk:128 * (blk + 1)],
                                           idb[:]),
                               inc="st" if b_ == 7 else None) or mkt
                B.wait("vector", "st", mkt)
                B.on("vector", lambda v, dst=dst, g=g: v.tensor_copy(
                    dst[:, 1024 * g:1024 * (g + 1)], pT[:]), inc="sv")
        mark_tr = B.mark("sv")
        B.wait("tensor", "sv", mark_tr)
        mark_ctx = None
        for blk in range(16):
            mark_ctx = B.on("tensor", lambda t, blk=blk: t.matmul(
                pAux[:, 0:128], Etm[:, 128 * blk:128 * (blk + 1)],
                vtm[:, 128 * blk:128 * (blk + 1)],
                start=(blk == 0), stop=(blk == 15)),
                inc="st" if blk == 15 else None) or mark_ctx
        B.on("vector", lambda v: v.reciprocal(rspr[:], ssum[:]), inc="sv")
        B.wait("vector", "st", mark_ctx)
        mark_cb = B.on("vector", lambda v: v.scalar_tensor_tensor(
            cb_sb[:], pAux[:, 0:128], rspr[:], maskb[:], OP.mult, OP.mult),
            inc="sv")

        B.wait("tensor", "dW", mark_wffn)
        last_bcast = {"evac": None}

        def bcastp(src_row, dst_tile, negate=False):
            if last_bcast["evac"] is not None:
                B.wait("tensor", *last_bcast["evac"])
            mkv = bcast(src_row, dst_tile, negate)
            last_bcast["evac"] = ("sv", mkv)
            return mkv

        att_all = {}
        outh_evac = {}
        stat_free = ["ss", stA_evac[3]]
        x2_marks = {}
        w_marks = {}
        mix_marks = {}
        xk_marks = {}
        o_all = {}
        out_dma_marks = {}
        m3_marks = {}
        last_fin = {"sg": 0}

        def CC1a(j):
            sl = slice(NT * j, NT * (j + 1))
            if j == 0:
                B.wait("tensor", "sv", mark_cb)
                B.wait("vector", "dX", mark_x)
            if j - 1 in outh_evac:
                B.wait("tensor", "sv", outh_evac[j - 1])
            if last_bcast["evac"] is not None:
                B.wait("tensor", *last_bcast["evac"])
            if (j - 2, NCH - 1) in o_all:
                B.wait("tensor", "sv", o_all[(j - 2, NCH - 1)])
            mko = B.on("tensor", lambda t, sl=sl: t.matmul(
                pB[:], cb_sb[:], q_sb[:, sl], start=True, stop=True), inc="st")
            B.wait("vector", "st", mko)
            outh_evac[j] = B.on("vector", lambda v, sl=sl: v.tensor_copy(
                outh[:, sl], pB[:]), inc="sv")
            B.wait("tensor", "sv", outh_evac[j])
            B.wait("vector", "sv", last_fin["sg"])
            bcastp(rowRw[0:1, sl], trt(0))
            if j - 1 in ffn_m and 7 in ffn_m[j - 1][0]:
                B.wait("tensor", "ss", ffn_m[j - 1][0][7])
            for c in range(NCH):
                psum = (pQ, pK, pV)[c % 3]
                prev_key = (j, c - 3) if c >= 3 else (j - 1, (6, 7, 5)[c])
                if prev_key in att_all:
                    B.wait("tensor", "sv", att_all[prev_key])
                mkp = B.on("tensor", lambda t, c=c, sl=sl, psum=psum: t.matmul(
                    psum[:], woutb[:, 128 * c:128 * (c + 1)], outh[:, sl],
                    start=True, stop=True), inc="st")
                tsl = NT * (c % 2)
                B.on("vector", lambda v, c=c, sl=sl, tsl=tsl: v.tensor_tensor(
                    t5r[:, tsl:tsl + NT], x_sb[c][:, sl], trt(0), OP.mult),
                    inc="sv")
                B.wait("vector", "st", mkp)
                att_all[(j, c)] = B.on(
                    "vector", lambda v, c=c, psum=psum, tsl=tsl:
                    v.scalar_tensor_tensor(
                        attr[:, NT * c:NT * (c + 1)], psum[:],
                        bocol[:, c:c + 1], t5r[:, tsl:tsl + NT],
                        OP.add, OP.add), inc="sv")
            B.wait("tensor", *stat_free)
            m3_marks[j] = B.on("tensor", lambda t, sl=sl: t.matmul(
                pSm[:], rwcol[:], outh[:, sl], start=True, stop=True),
                inc="st")

        def CC1c(j):
            sl = slice(NT * j, NT * (j + 1))
            B.wait("tensor", *stat_free)
            for c in range(NCH):
                if (c % 3) in sq_used:
                    B.wait("scalar", *sq_used[c % 3])
                B.wait("scalar", "sv", att_all[(j, c)])
                mkaq = B.on("scalar", lambda s, c=c: s.activation(
                    sq_slot(c), attr[:, NT * c:NT * (c + 1)], AF.Square),
                    inc="ss")
                B.wait("tensor", "ss", mkaq)
                sq_used[c % 3] = ("st", B.on("tensor", lambda t, c=c: t.matmul(
                    pSq[:], invccol[:], sq_slot(c),
                    start=(c == 0), stop=(c == NCH - 1)), inc="st"))
            B.wait("vector", "st", m3_marks[j])
            B.wait("vector", "ss", B.mark("ss"))
            B.on("vector", lambda v, sl=sl: v.scalar_tensor_tensor(
                rmj(0), pSm[:], bbar[:], mrs1row[0:1, sl], OP.add, OP.add),
                inc="sv")
            stat_free[:] = ["sv", B.on("vector", lambda v: v.tensor_copy(
                rqj(0), pSq[:]), inc="sv")]
            chain_core(0)
            B.on("vector", lambda v: v.tensor_tensor(rowM4[0:1, NT:2 * NT], rmj(0), rowRw[0:1, 0:NT],
                                                     OP.mult), inc="sv")
            B.on("vector", lambda v, sl=sl: v.tensor_copy(p2row[0:1, sl],
                                                          rowM4[0:1, NT:2 * NT]), inc="sv")

        def CC1b(j):
            sl = slice(NT * j, NT * (j + 1))
            mk_r3 = bcastp(rowRw[0:1, 0:NT], trt(0))
            B.wait("vector", "sv", mk_r3)
            for c in range(NCH):
                tsl = NT * (c % 2)
                B.on("vector", lambda v, c=c, tsl=tsl: v.tensor_tensor(
                    t3s[:, tsl:tsl + NT], attr[:, NT * c:NT * (c + 1)],
                    trt(0), OP.mult), inc="sv")
                x2_marks[(j, c)] = B.on(
                    "vector", lambda v, c=c, sl=sl, tsl=tsl, j=j: v.tensor_tensor(
                        x2_(c, j), t3s[:, tsl:tsl + NT],
                        x_sb[c][:, sl], OP.add), inc="sv")

        def CC2a(j):
            sl = slice(NT * j, NT * (j + 1))
            B.wait("tensor", *stat_free)
            for c in range(NCH):
                B.wait("tensor", "sv", x2_marks[(j, c)])
                B.on("tensor", lambda t, c=c, j=j: t.matmul(
                    pSm[:], invccol[:], x2_(c, j),
                    start=(c == 0), stop=(c == NCH - 1)))
            mks4 = None
            for c in range(NCH):
                if (c % 3) in sq_used:
                    B.wait("scalar", *sq_used[c % 3])
                B.wait("scalar", "sv", x2_marks[(j, c)])
                mkv = B.on("scalar", lambda s, c=c, j=j: s.activation(
                    sq_slot(c), x2_(c, j), AF.Square), inc="ss")
                B.wait("tensor", "ss", mkv)
                mks4 = B.on("tensor", lambda t, c=c: t.matmul(
                    pSq[:], invccol[:], sq_slot(c),
                    start=(c == 0), stop=(c == NCH - 1)), inc="st")
                sq_used[c % 3] = ("st", mks4)
            B.wait("scalar", "st", mks4)
            stat_free[:] = ["ss", stats_evac_s(0)]
            chain_core(0)
            B.on("vector", lambda v: v.tensor_tensor(rowM4[0:1, NT:2 * NT], rmj(0), rowRw[0:1, 0:NT],
                                                     OP.mult), inc="sv")

        def CC2b(j):
            sl = slice(NT * j, NT * (j + 1))
            B.wait("vector", "sv", last_fin["sg"])
            bcastp(rowRw[0:1, 0:NT], trt(1))
            mk_m4 = bcastp(rowM4[0:1, NT:2 * NT], trtb4[:], negate=True)
            B.wait("vector", "sv", mk_m4)
            for c in range(NCH):
                tsl = NT * (c % 2)
                B.on("vector", lambda v, c=c, tsl=tsl, j=j: v.tensor_tensor(
                    t3s[:, tsl:tsl + NT], x2_(c, j), trt(1), OP.mult),
                    inc="sv")
                w_marks[(j, c)] = B.on(
                    "vector", lambda v, c=c, sl=sl, tsl=tsl: v.tensor_tensor(
                        w_sb[c][:, sl], t3s[:, tsl:tsl + NT], trtb4[:],
                        OP.subtract), inc="sv")
            if j - 1 in rk_last:
                B.wait("vector", "st", rk_last[j - 1])
            d_marks = {}
            for c in range(NCH):
                if j == 0:
                    B.on("vector", lambda v, c=c: v.tensor_copy(
                        attr[:, NT * c:NT * c + 1], w_sb[c][:, 0:1]))
                    d_marks[c] = B.on("vector", lambda v, c=c: v.tensor_tensor(
                        attr[:, NT * c + 1:NT * (c + 1)], w_sb[c][:, 1:NT],
                        w_sb[c][:, 0:NT - 1], OP.subtract), inc="sv")
                else:
                    d_marks[c] = B.on("vector", lambda v, c=c, j=j:
                                      v.tensor_tensor(
                                          attr[:, NT * c:NT * (c + 1)],
                                          w_sb[c][:, NT * j:NT * (j + 1)],
                                          w_sb[c][:, NT * j - 1:NT * (j + 1) - 1],
                                          OP.subtract), inc="sv")
            for c in range(NCH):
                if j == 0:
                    B.on("vector", lambda v, c=c: v.tensor_scalar(
                        xr_(c)[:, 0:1], attr[:, NT * c:NT * c + 1],
                        murcol[:, c:c + 1], None, OP.mult))
                    mix_marks[(j, c)] = B.on(
                        "vector", lambda v, c=c: v.scalar_tensor_tensor(
                            xr_(c)[:, 1:NT], attr[:, NT * c + 1:NT * (c + 1)],
                            murcol[:, c:c + 1], w_sb[c][:, 0:NT - 1],
                            OP.mult, OP.add), inc="sv")
                else:
                    mix_marks[(j, c)] = B.on(
                        "vector", lambda v, c=c, j=j: v.scalar_tensor_tensor(
                            xr_(c), attr[:, NT * c:NT * (c + 1)],
                            murcol[:, c:c + 1],
                            w_sb[c][:, NT * j - 1:NT * (j + 1) - 1],
                            OP.mult, OP.add), inc="sv")
            for c in range(NCH):
                if j == 0:
                    B.on("vector", lambda v, c=c: v.tensor_scalar(
                        xk_(c)[:, 0:1], attr[:, NT * c:NT * c + 1],
                        mukcol[:, c:c + 1], None, OP.mult))
                    xk_marks[(j, c)] = B.on(
                        "vector", lambda v, c=c: v.scalar_tensor_tensor(
                            xk_(c)[:, 1:NT], attr[:, NT * c + 1:NT * (c + 1)],
                            mukcol[:, c:c + 1], w_sb[c][:, 0:NT - 1],
                            OP.mult, OP.add), inc="sv")
                else:
                    xk_marks[(j, c)] = B.on(
                        "vector", lambda v, c=c, j=j: v.scalar_tensor_tensor(
                            xk_(c), attr[:, NT * c:NT * (c + 1)],
                            mukcol[:, c:c + 1],
                            w_sb[c][:, NT * j - 1:NT * (j + 1) - 1],
                            OP.mult, OP.add), inc="sv")

        rk_last = {}
        rk_last = {}
        ffn_m = {}

        def _rk_groups(j, g0, g1, sig_m, ka_m):
            groups = ([("R", c) for c in range(NCH)]
                      + [("K", c) for c in range(NCH)])
            for g in range(g0, g1):
                kind, c = groups[g]
                bank = (pAux, pQ, pK)[g % 3]
                if j == 0:
                    if g == 1 and (0, 6) in att_all:
                        B.wait("tensor", "sv", att_all[(0, 6)])
                    if g == 2 and (0, 7) in att_all:
                        B.wait("tensor", "sv", att_all[(0, 7)])
                if g == 8 and (j + 1, 7) in att_all:
                    B.wait("tensor", "sv", att_all[(j + 1, 7)])
                if g == 10 and (j + 1, 6) in att_all:
                    B.wait("tensor", "sv", att_all[(j + 1, 6)])
                if g >= 3:
                    pk, pc = groups[g - 3]
                    if pk == "R":
                        B.wait("tensor", "ss", sig_m[pc])
                    else:
                        B.wait("tensor", "ss", ka_m[pc])
                W = Wr_sb if kind == "R" else Wk_sb
                xin = xr_ if kind == "R" else xk_
                mk = None
                for cc in range(NCH):
                    if g == 0:
                        B.wait("tensor", "sv", mix_marks[(j, cc)])
                    elif g == 1 and cc == 0:
                        B.wait("tensor", "sv", mix_marks[(j, NCH - 1)])
                    elif g == 8:
                        B.wait("tensor", "sv", xk_marks[(j, cc)])
                    elif g == 9 and cc == 0:
                        B.wait("tensor", "sv", xk_marks[(j, NCH - 1)])
                    mk = B.on("tensor", lambda t, c=c, cc=cc, bank=bank, W=W,
                              xin=xin: t.matmul(
                                  bank[:],
                                  W[:, C * cc + 128 * c:C * cc + 128 * (c + 1)],
                                  xin(cc), start=(cc == 0),
                                  stop=(cc == NCH - 1)),
                              inc="st" if cc == NCH - 1 else None) or mk
                if kind == "R":
                    B.wait("scalar", "st", mk)
                    sig_m[c] = B.on("scalar", lambda s, c=c, bank=bank:
                                    s.activation(rr_(c), bank[:], AF.Sigmoid),
                                    inc="ss")
                else:
                    B.wait("scalar", "st", mk)
                    if c == 4:
                        B.wait("scalar", "sv",
                               att_all[(min(j + 1, NJ - 1), NCH - 1)])
                    if c == 6:
                        B.wait("scalar", "sv", last_fin["sg"])
                    if (c % 3) in sq_used:
                        B.wait("scalar", *sq_used[c % 3])
                    B.on("scalar", lambda s, c=c, bank=bank:
                         s.activation(sq_slot(c), bank[:], AF.Relu),
                         inc="ss")
                    ka_m[c] = B.on("scalar", lambda s, c=c: s.activation(
                        ka_(c), sq_slot(c), AF.Square), inc="ss")
                    sq_used[c % 3] = ("ss", ka_m[c])
                if g == 2 * NCH - 1:
                    rk_last[j] = mk

        def EEa1A(j):
            sig_m = {}
            ka_m = {}
            ffn_m[j] = (sig_m, ka_m)
            _rk_groups(j, 0, 8, sig_m, ka_m)

        def EEa1B(j):
            sig_m, ka_m = ffn_m[j]
            _rk_groups(j, 8, 16, sig_m, ka_m)

        def EEa2a(j):
            sl = slice(NT * j, NT * (j + 1))
            sig_m, ka_m = ffn_m[j]
            B.wait("tensor", "ss", ka_m[NCH - 1])
            B.wait("tensor", "ss", sig_m[NCH - 1])
            B.wait("vector", "ss", B.mark("ss"))
            for c in range(NCH):
                bV = (pV, pQ, pK)[c % 3]
                if c == 0 and (j + 1, 5) in att_all:
                    B.wait("tensor", "sv", att_all[(j + 1, 5)])
                if c >= 3:
                    B.wait("tensor", "sv", o_all[(j, c - 3)])
                mkV = None
                for cc in range(NCH):
                    mkV = B.on("tensor", lambda t, c=c, cc=cc, bV=bV: t.matmul(
                        bV[:], Wv_sb[:, C * cc + 128 * c:C * cc + 128 * (c + 1)],
                        ka_(cc), start=(cc == 0), stop=(cc == NCH - 1)),
                        inc="st" if cc == NCH - 1 else None) or mkV
                B.wait("vector", "st", mkV)
                o_all[(j, c)] = B.on("vector", lambda v, c=c, bV=bV:
                                     v.tensor_tensor(
                                         oring[:, NT * c:NT * (c + 1)], bV[:],
                                         rr_(c), OP.mult), inc="sv")
        def EEa2b(j):
            sl = slice(NT * j, NT * (j + 1))
            B.wait("tensor", *stat_free)
            for c in range(NCH):
                B.wait("tensor", "sv", o_all[(j, c)])
                B.on("tensor", lambda t, c=c: t.matmul(
                    pSm[:], invccol[:], oring[:, NT * c:NT * (c + 1)],
                    start=(c == 0), stop=(c == NCH - 1)))
            mks5 = None
            for c in range(NCH):
                if (c % 3) in sq_used:
                    B.wait("scalar", *sq_used[c % 3])
                B.wait("scalar", "sv", o_all[(j, c)])
                mkv = B.on("scalar", lambda s, c=c: s.activation(
                    sq_slot(c), oring[:, NT * c:NT * (c + 1)], AF.Square),
                    inc="ss")
                B.wait("tensor", "ss", mkv)
                mks5 = B.on("tensor", lambda t, c=c: t.matmul(
                    pSq[:], invccol[:], sq_slot(c),
                    start=(c == 0), stop=(c == NCH - 1)), inc="st")
                sq_used[c % 3] = ("st", mks5)
            B.wait("scalar", "st", mks5)
            stat_free[:] = ["ss", stats_evac_s(0)]
            chain_core(0)
            B.on("vector", lambda v: v.tensor_tensor(rowM4[0:1, NT:2 * NT], rmj(0), rowRw[0:1, 0:NT],
                                                     OP.mult), inc="sv")
            B.on("vector", lambda v, sl=sl: v.tensor_tensor(
                rowM4[0:1, NT:2 * NT], rowM4[0:1, NT:2 * NT], p2row[0:1, sl], OP.add), inc="sv")

        def EEa2c(j):
            bcastp(rowRw[0:1, 0:NT], trt(1))
            return bcastp(rowM4[0:1, NT:2 * NT], trtb4[:], negate=True)

        def EEb(j, mk_cl):
            sl = slice(NT * j, NT * (j + 1))
            B.wait("vector", "sv", mk_cl)
            B.wait("gpsimd", "sv", mk_cl)
            last = None
            fin_slot = {}
            for c in range(NCH):
                slot = c % 2
                tA = NT * (2 + c % 2)
                if c % 2 == 0:
                    if slot in fin_slot:
                        B.wait("gpsimd", "sv", fin_slot[slot])
                    B.on("gpsimd", lambda g, c=c, tA=tA: g.tensor_tensor(
                        t5r[:, tA:tA + NT], oring[:, NT * c:NT * (c + 1)],
                        trt(1), OP.mult), inc="sg")
                    mg = B.on("gpsimd", lambda g, tA=tA: g.tensor_tensor(
                        t5r[:, tA:tA + NT], t5r[:, tA:tA + NT],
                        trtb4[:], OP.subtract), inc="sg")
                    B.wait("vector", "sg", mg)
                else:
                    B.on("vector", lambda v, c=c, tA=tA: v.tensor_tensor(
                        t5r[:, tA:tA + NT], oring[:, NT * c:NT * (c + 1)],
                        trt(1), OP.mult), inc="sv")
                    B.on("vector", lambda v, tA=tA: v.tensor_tensor(
                        t5r[:, tA:tA + NT], t5r[:, tA:tA + NT],
                        trtb4[:], OP.subtract), inc="sv")
                if slot in out_dma_marks:
                    B.wait("vector", "dO", out_dma_marks[slot])
                last = B.on("vector", lambda v, c=c, slot=slot, tA=tA, j=j:
                            v.tensor_tensor(
                                outr[:, NT * slot:NT * (slot + 1)],
                                t5r[:, tA:tA + NT],
                                x2_(c, j), OP.add), inc="sv")
                fin_slot[slot] = last
                B.wait("sync", "sv", last)
                out_dma_marks[slot] = B.dma(
                    "sync", yT[128 * c:128 * (c + 1), sl],
                    outr[:, NT * slot:NT * (slot + 1)], "dO")
            last_fin["sg"] = last

        CC1a(0)
        CC1c(0)
        CC1b(0)
        CC2a(0)
        CC2b(0)
        EEa1A(0)
        CC1a(1)
        CC1c(1)
        EEa1B(0)
        CC1b(1)
        CC2a(1)
        EEa2a(0)
        CC2b(1)
        EEa2b(0)
        EEa1A(1)
        CC1a(2)
        cl0 = EEa2c(0)
        CC1c(2)
        EEb(0, cl0)
        EEa1B(1)
        CC1b(2)
        CC2a(2)
        EEa2a(1)
        CC2b(2)
        EEa2b(1)
        EEa1A(2)
        CC1a(3)
        cl1 = EEa2c(1)
        CC1c(3)
        EEb(1, cl1)
        EEa1B(2)
        CC1b(3)
        CC2a(3)
        EEa2a(2)
        CC2b(3)
        EEa2b(2)
        EEa1A(3)
        cl2 = EEa2c(2)
        EEb(2, cl2)
        EEa1B(3)
        EEa2a(3)
        EEa2b(3)
        cl3 = EEa2c(3)
        EEb(3, cl3)

        if debug:
            B.wait("sync", "sv", B.mark("sv"))
            for c in range(NCH):
                B.dma("sync", dbg["d_x2"][128 * c:128 * (c + 1), 0:2 * NT],
                      x2_sb[c][:], "dO")
                B.dma("sync", dbg["d_w"][128 * c:128 * (c + 1), :], w_sb[c][:],
                      "dO")
            B.dma("sync", dbg["d_or"], oring[:], "dO")
            B.dma("sync", dbg["d_xr"], k_sb[:], "dO")
            B.dma("sync", dbg["d_xk"], v_sb[:], "dO")
            B.dma("sync", dbg["d_xk2"], vtm[:], "dO")
            B.dma("sync", dbg["d_rr"], Etm[:], "dO")
            B.dma("sync", dbg["d_ka"], outh[:], "dO")
        B.wait("sync", "dO", B.mark("dO"))

        block = E_(nc.Block())

        @block.sync
        def _(e):
            for fn in B.ops["sync"]:
                fn(e)

        @block.gpsimd
        def _(e):
            for fn in B.ops["gpsimd"]:
                fn(e)

        @block.vector
        def _(e):
            for fn in B.ops["vector"]:
                fn(e)

        @block.scalar
        def _(e):
            for fn in B.ops["scalar"]:
                fn(e)

        @block.tensor
        def _(e):
            for fn in B.ops["tensor"]:
                fn(e)

    return nc


_NC_CACHE = {}


def _get_nc(debug=False):
    if debug not in _NC_CACHE:
        _NC_CACHE[debug] = build_nc(debug)
    return _NC_CACHE[debug]


def _pcol(v):
    return np.ascontiguousarray(
        np.asarray(v, np.float32).ravel().reshape(8, 128).T)


def kernel(L0_attn_x, L0_attn_gamma, L0_attn_beta, L0_attn_w_qkv, L0_attn_w_out,
           L0_attn_b_out, L0_ffn_W_r, L0_ffn_W_k, L0_ffn_W_v, L0_ffn_mu_r,
           L0_ffn_mu_k, _debug=False):
    x = np.ascontiguousarray(L0_attn_x, dtype=np.float32)
    shared = {
        "w_qkv": np.ascontiguousarray(L0_attn_w_qkv, np.float32),
        "w_out": np.ascontiguousarray(L0_attn_w_out, np.float32),
        "WrT": np.ascontiguousarray(np.asarray(L0_ffn_W_r, np.float32).T),
        "WkT": np.ascontiguousarray(np.asarray(L0_ffn_W_k, np.float32).T),
        "WvT": np.ascontiguousarray(np.asarray(L0_ffn_W_v, np.float32).T),
        "vgamma": _pcol(L0_attn_gamma),
        "vbeta": _pcol(L0_attn_beta),
        "vbout": _pcol(L0_attn_b_out),
        "vmur": _pcol(L0_ffn_mu_r),
        "vmuk": _pcol(L0_ffn_mu_k),
    }
    in_maps = []
    for b in range(8):
        m = dict(shared)
        m["xT"] = np.ascontiguousarray(x[b].T)
        in_maps.append(m)
    nc = _get_nc(_debug)
    res = run_bass_kernel_spmd(nc, in_maps, core_ids=list(range(8)))
    out = np.stack([np.asarray(r["yT"]).T for r in res.results])
    if _debug:
        return out, res.results
    return out


# version 45 (final)
# speedup vs baseline: 1.0904x; 1.0195x over previous
import contextlib
import numpy as np
import concourse.bass as bass
import concourse.mybir as mybir
from concourse.bass_utils import run_bass_kernel_spmd
from concourse.alu_op_type import AluOpType as OP

F32 = mybir.dt.float32
BF16 = mybir.dt.bfloat16
AF = mybir.ActivationFunctionType
AX = mybir.AxisListType

T = 2048
C = 1024
NCH = 8
NJ = 4
NT = 512
EPS = 1e-5
INV_C = 1.0 / C

ENGINES = ("sync", "gpsimd", "vector", "scalar", "tensor")


class Builder:
    def __init__(self, nc, sems):
        self.nc = nc
        self.sems = sems
        self.cnt = {k: 0 for k in sems}
        self.ops = {e: [] for e in ENGINES}

    def on(self, eng, fn, inc=None):
        sem = self.sems[inc] if inc else None
        if inc:
            self.cnt[inc] += 1

        def run(e, fn=fn, sem=sem):
            inst = fn(e)
            if sem is not None:
                inst.then_inc(sem, 1)
        self.ops[eng].append(run)
        return self.cnt[inc] if inc else None

    def dma(self, eng, out, in_, inc, slow=False):
        self.cnt[inc] += 16
        sem = self.sems[inc]

        def run(e, out=out, in_=in_, sem=sem, slow=slow):
            kw = dict(allow_slow_non_contiguous=True) if slow else {}
            e.dma_start(out=out, in_=in_, **kw).then_inc(sem, 16)
        self.ops[eng].append(run)
        return self.cnt[inc]

    def mark(self, sem):
        return self.cnt[sem]

    def wait(self, eng, sem, val=None):
        if val is None:
            val = self.cnt[sem]
        s = self.sems[sem]

        def run(e, s=s, val=val):
            e.wait_ge(s, val)
        self.ops[eng].append(run)


def build_nc(debug=False):
    nc = bass.Bass("TRN2", target_bir_lowering=False)

    xT = nc.dram_tensor("xT", [C, T], F32, kind="ExternalInput").ap()
    w_qkv = nc.dram_tensor("w_qkv", [C, 384], F32, kind="ExternalInput").ap()
    w_out = nc.dram_tensor("w_out", [128, C], F32, kind="ExternalInput").ap()
    WrT = nc.dram_tensor("WrT", [C, C], F32, kind="ExternalInput").ap()
    WkT = nc.dram_tensor("WkT", [C, C], F32, kind="ExternalInput").ap()
    WvT = nc.dram_tensor("WvT", [C, C], F32, kind="ExternalInput").ap()
    vgamma = nc.dram_tensor("vgamma", [128, NCH], F32, kind="ExternalInput").ap()
    vbeta = nc.dram_tensor("vbeta", [128, NCH], F32, kind="ExternalInput").ap()
    vbout = nc.dram_tensor("vbout", [128, NCH], F32, kind="ExternalInput").ap()
    vmur = nc.dram_tensor("vmur", [128, NCH], F32, kind="ExternalInput").ap()
    vmuk = nc.dram_tensor("vmuk", [128, NCH], F32, kind="ExternalInput").ap()
    yT = nc.dram_tensor("yT", [C, T], F32, kind="ExternalOutput").ap()

    dbg = {}
    if debug:
        for nm, shp, dt in [("d_x2", [C, T], BF16), ("d_w", [C, T], BF16),
                            ("d_or", [128, NCH * NT], BF16),
                            ("d_xr", [128, T], BF16), ("d_xk", [128, T], BF16),
                            ("d_xk2", [128, T], BF16), ("d_rr", [128, T], BF16),
                            ("d_ka", [128, T], BF16)]:
            dbg[nm] = nc.dram_tensor(nm, shp, dt, kind="ExternalOutput").ap()

    with contextlib.ExitStack() as ST:
        E_ = ST.enter_context

        x_sb = [E_(nc.sbuf_tensor(f"x{c}", [128, T], BF16)) for c in range(NCH)]
        x2_sb = [E_(nc.sbuf_tensor(f"x2_{c}", [128, 2 * NT], BF16)) for c in range(NCH)]
        q_sb = E_(nc.sbuf_tensor("q", [128, T], BF16))
        k_sb = E_(nc.sbuf_tensor("k", [128, T], BF16))
        v_sb = E_(nc.sbuf_tensor("v", [128, T], BF16))
        Eb = E_(nc.sbuf_tensor("Eb", [128, T], BF16))
        Etm = E_(nc.sbuf_tensor("Etm", [128, T], BF16))
        vtm = E_(nc.sbuf_tensor("vtm", [128, T], BF16))
        outh = E_(nc.sbuf_tensor("outh", [128, T], BF16))
        t5r = E_(nc.sbuf_tensor("t5r", [128, T], BF16))
        Wr_sb = E_(nc.sbuf_tensor("Wr", [128, NCH * C], BF16))
        Wk_sb = E_(nc.sbuf_tensor("Wk", [128, NCH * C], BF16))
        Wv_sb = E_(nc.sbuf_tensor("Wv", [128, NCH * C], BF16))
        woutb = E_(nc.sbuf_tensor("woutb", [128, C], BF16))
        attr = E_(nc.sbuf_tensor("attr", [128, NCH * NT], BF16))
        oring = E_(nc.sbuf_tensor("oring", [128, NCH * NT], BF16))
        sqr = E_(nc.sbuf_tensor("sqr", [128, 3 * NT], BF16))
        t3s = E_(nc.sbuf_tensor("t3s", [128, 2 * NT], BF16))
        rrB = E_(nc.sbuf_tensor("rrB", [128, 4 * NT], BF16))

        def _alias(name, backing):
            t_ = nc.virtual_sbuf_tensor(name, [128, T], BF16)
            nc.back_virtual_tensor_with(t_, backing)
            return t_
        w_sb = [_alias(f"w{c}", x_sb[c]) for c in range(NCH)]
        xrA = _alias("xrA", k_sb)
        xrB = _alias("xrB", Eb)
        xkA = _alias("xkA", v_sb)
        xkB = _alias("xkB", vtm)
        rrA = _alias("rrA", Etm)
        kaA = _alias("kaA", outh)
        kaB = _alias("kaB", t5r)

        def _seg(tA, tB, c):
            t_ = tA if c < 4 else tB
            return t_[:, NT * (c % 4):NT * (c % 4 + 1)]

        def x2_(c, j):
            return x2_sb[c][:, NT * (j % 2):NT * (j % 2 + 1)]

        def xr_(c):
            return _seg(xrA, xrB, c)

        def xk_(c):
            return _seg(xkA, xkB, c)

        def rr_(c):
            if c < 4:
                return rrA[:, NT * c:NT * (c + 1)]
            return rrB[:, NT * (c - 4):NT * (c - 3)]

        def ka_(c):
            return _seg(kaA, kaB, c)

        rsrow = E_(nc.sbuf_tensor("rsrow", [1, T], BF16))
        rowM4 = E_(nc.sbuf_tensor("rowM4", [1, T], F32))
        rowQ4 = E_(nc.sbuf_tensor("rowQ4", [1, T], F32))
        wT0 = E_(nc.sbuf_tensor("wT0", [1, T], F32))
        rowRw = E_(nc.sbuf_tensor("rowRw", [1, T], F32))
        mrs1row = E_(nc.sbuf_tensor("mrs1row", [1, T], BF16))
        p2row = E_(nc.sbuf_tensor("p2row", [1, T], BF16))
        s4hi = p2row

        outr = E_(nc.sbuf_tensor("outr", [128, 2 * NT], F32))
        TRT = E_(nc.sbuf_tensor("TRT", [128, 2 * NT], BF16))
        trtb4 = E_(nc.sbuf_tensor("trtb4", [128, NT], BF16))

        def trt(i):
            return TRT[:, NT * i:NT * (i + 1)]

        gcol = E_(nc.sbuf_tensor("gcol", [128, 8], F32))
        bcol = E_(nc.sbuf_tensor("bcol", [128, 8], F32))
        bocol = E_(nc.sbuf_tensor("bocol", [128, 8], F32))
        murcol = E_(nc.sbuf_tensor("murcol", [128, 8], F32))
        mukcol = E_(nc.sbuf_tensor("mukcol", [128, 8], F32))
        bcolb2 = E_(nc.sbuf_tensor("bcolb2", [128, 16], BF16, align_bytes=4))
        onescol = E_(nc.sbuf_tensor("onescol", [128, 1], BF16, align_bytes=4))
        invccol = E_(nc.sbuf_tensor("invccol", [128, 1], BF16, align_bytes=4))
        ones1 = E_(nc.sbuf_tensor("ones1", [1, 128], BF16, align_bytes=4))
        bocolb = E_(nc.sbuf_tensor("bocolb", [128, 8], BF16, align_bytes=4))
        cb_sb = E_(nc.sbuf_tensor("cb_sb", [128, 128], BF16))
        idb = nc.virtual_sbuf_tensor("idb", [128, 128], BF16)
        nc.back_virtual_tensor_with(idb, cb_sb)
        io_c = E_(nc.sbuf_tensor("io_c", [128, 128], BF16))
        maskb = nc.virtual_sbuf_tensor("maskb", [128, 128], BF16)
        nc.back_virtual_tensor_with(maskb, io_c)
        io_r = E_(nc.sbuf_tensor("io_r", [128, 1], F32, align_bytes=4))
        rspr = E_(nc.sbuf_tensor("rspr", [128, 1], F32, align_bytes=4))
        kmax = E_(nc.sbuf_tensor("kmax", [128, 1], F32, align_bytes=4))
        ssum = E_(nc.sbuf_tensor("ssum", [128, 1], F32, align_bytes=4))
        rwcol = E_(nc.sbuf_tensor("rwcol", [128, 1], BF16, align_bytes=4))
        rwf = E_(nc.sbuf_tensor("rwf", [128, 1], F32, align_bytes=4))
        bbar = E_(nc.sbuf_tensor("bbar", [1, 1], F32, align_bytes=4))
        epsc = E_(nc.sbuf_tensor("epsc", [1, 1], F32, align_bytes=4))
        pb8 = E_(nc.sbuf_tensor("pb8", [1, 8], F32, align_bytes=4))

        pQ = E_(nc.psum_tensor("pQ", [128, NT], F32))
        pK = E_(nc.psum_tensor("pK", [128, NT], F32))
        pV = E_(nc.psum_tensor("pV", [128, NT], F32))
        pSm = E_(nc.psum_tensor("pSm", [1, NT], F32))
        pSq = E_(nc.psum_tensor("pSq", [1, NT], F32))
        pB = E_(nc.psum_tensor("pB", [128, NT], F32))
        pAux = E_(nc.psum_tensor("pAux", [128, NT], F32))
        pT = E_(nc.psum_tensor("pT", [128, 1024], BF16))

        sems = {}
        for s in ("dX", "dW", "dM", "dO", "sv", "ss", "st", "sg"):
            sems[s] = E_(nc.semaphore(s))
        B = Builder(nc, sems)

        mark_xj = {}
        for c in range(NCH):
            B.dma("gpsimd", x_sb[c][:, 0:NT],
                  xT[128 * c:128 * c + 128, 0:NT], "dX")
        mark_xj[0] = B.mark("dX")
        for dst, src in ((gcol, vgamma), (bcol, vbeta), (bocol, vbout),
                         (murcol, vmur), (mukcol, vmuk)):
            B.dma("sync", dst[:], src, "dM")
        mark_vecs = B.mark("dM")
        B.dma("gpsimd", Wr_sb[:, 0:NCH * 384].rearrange("p (a k) -> p a k", k=384),
              w_qkv.rearrange("(a p) k -> p a k", p=128), "dW")
        B.dma("gpsimd", woutb[:], w_out, "dW")
        mark_wqkv = B.mark("dW")
        for j in range(1, NJ):
            for c in range(NCH):
                B.dma("gpsimd", x_sb[c][:, NT * j:NT * (j + 1)],
                      xT[128 * c:128 * c + 128, NT * j:NT * (j + 1)], "dX")
            mark_xj[j] = B.mark("dX")
        mark_x = B.mark("dX")
        B.dma("gpsimd", Wv_sb[:].rearrange("p (a k) -> p a k", k=C),
              WvT.rearrange("(a p) k -> p a k", p=128), "dW")

        B.on("gpsimd", lambda g: g.iota(io_c[:], pattern=[[1, 128]], base=0,
                                        channel_multiplier=0,
                                        allow_small_or_imprecise_dtypes=True))
        mark_iota = B.on("gpsimd", lambda g: g.iota(
            io_r[:], pattern=[[0, 1]], base=0, channel_multiplier=1,
            allow_small_or_imprecise_dtypes=True), inc="sg")

        B.on("vector", lambda v: v.memset(onescol[:], 1.0))
        B.on("vector", lambda v: v.memset(invccol[:], INV_C))
        B.on("vector", lambda v: v.memset(ones1[:], 1.0))
        B.on("vector", lambda v: v.memset(epsc[:], EPS))
        B.on("vector", lambda v: v.memset(bcolb2[:], 0.0))
        B.wait("vector", "sg", mark_iota)
        mark_const = B.on("vector", lambda v: v.tensor_scalar(
            idb[:], io_c[:], io_r[:], None, OP.is_equal), inc="sv")
        B.on("vector", lambda v: v.memset(maskb[:], 0.0))
        for h in range(4):
            B.on("vector", lambda v, h=h: v.memset(
                maskb[32 * h:32 * h + 32, 32 * h:32 * h + 32], 32.0 ** -0.5))

        B.wait("vector", "dW", mark_wqkv)
        B.wait("vector", "dM", mark_vecs)
        for c in range(NCH):
            B.on("vector", lambda v, c=c: v.tensor_scalar(
                Wk_sb[:, 384 * c:384 * (c + 1)], Wr_sb[:, 384 * c:384 * (c + 1)],
                gcol[:, c:c + 1], None, OP.mult), inc="sv")
        B.on("vector", lambda v: v.tensor_copy(
            bcolb2[:].rearrange("p (a two) -> p a two", two=2)[:, :, 0:1],
            bcol[:].rearrange("p (a one) -> p a one", one=1)), inc="sv")
        mk_rwf = B.on("vector", lambda v: v.reduce_sum(rwf[:], woutb[:],
                                                       axis=AX.X), inc="sv")
        B.on("vector", lambda v: v.tensor_copy(bocolb[:], bocol[:]), inc="sv")
        B.wait("scalar", "sv", mk_rwf)
        B.on("scalar", lambda s: s.mul(rwcol[:], rwf[:], INV_C), inc="ss")
        mark_prep = B.mark("sv")
        mark_prep_s = B.mark("ss")

        B.wait("tensor", "sv", mark_prep)
        mark_bqcw = None
        for c in range(NCH):
            B.on("tensor", lambda t, c=c: t.matmul(
                pB[0:1, 0:384], bcolb2[:, 2 * c:2 * c + 1],
                Wr_sb[:, 384 * c:384 * (c + 1)], start=(c == 0),
                stop=(c == NCH - 1)))
            mark_bqcw = B.on("tensor", lambda t, c=c: t.matmul(
                pAux[0:1, 0:384], onescol[:], Wk_sb[:, 384 * c:384 * (c + 1)],
                start=(c == 0), stop=(c == NCH - 1)),
                inc="st" if c == NCH - 1 else None) or mark_bqcw
        mark_bo = B.on("tensor", lambda t: t.matmul(
            pB[0:1, 384:392], onescol[:], bocolb[:], start=True, stop=True),
            inc="st")
        B.wait("vector", "st", mark_bqcw)
        B.on("vector", lambda v: v.tensor_copy(attr[0:1, 0:384], pB[0:1, 0:384]),
             inc="sv")
        B.on("vector", lambda v: v.tensor_copy(attr[0:1, 384:768],
                                               pAux[0:1, 0:384]), inc="sv")
        mark_cwbq = B.mark("sv")
        B.wait("vector", "st", mark_bo)
        mk_pb8 = B.on("vector", lambda v: v.reduce_sum(pb8[:, 0:1],
                                                       pB[0:1, 384:392],
                                                       axis=AX.X), inc="sv")
        B.wait("scalar", "sv", mk_pb8)
        B.on("scalar", lambda s: s.mul(bbar[:], pb8[:, 0:1], INV_C), inc="ss")
        B.wait("gpsimd", "st", mark_bo)
        B.wait("gpsimd", "sv", mark_prep)
        B.dma("gpsimd", Wr_sb[:].rearrange("p (a k) -> p a k", k=C),
              WrT.rearrange("(a p) k -> p a k", p=128), "dW")

        sq_used = {}
        ob_used = sq_used

        def sq_slot(c):
            return sqr[:, NT * (c % 3):NT * (c % 3 + 1)]

        def ob_slot(c):
            return sq_slot(c)

        def rmj(j):
            return rowM4[0:1, 0:NT]

        def rqj(j):
            return rowQ4[0:1, 0:NT]

        def chain_core(j=0):
            B.wait("scalar", "sv", B.mark("sv"))
            B.on("scalar", lambda s, j=j: s.activation(wT0[0:1, 0:NT], rmj(j),
                                                       AF.Square), inc="ss")
            B.wait("vector", "ss", B.mark("ss"))
            mkv = B.on("vector", lambda v, j=j: v.tensor_tensor(
                rowM4[0:1, NT:2 * NT], rqj(j), wT0[0:1, 0:NT], OP.subtract), inc="sv")
            B.wait("scalar", "sv", mkv)
            B.on("scalar", lambda s: s.activation(
                wT0[0:1, 0:NT], rowM4[0:1, NT:2 * NT], AF.Ln, bias=epsc[:], scale=1.0), inc="ss")
            mk = B.on("scalar", lambda s: s.activation(
                rowRw[0:1, 0:NT], wT0[0:1, 0:NT], AF.Exp, scale=-0.5), inc="ss")
            B.wait("vector", "ss", mk)
            return mk

        def stats_evac_s(j=0):
            B.wait("scalar", "sv", B.mark("sv"))
            B.on("scalar", lambda s, j=j: s.copy(rmj(j), pSm[:]))
            return B.on("scalar", lambda s, j=j: s.copy(rqj(j), pSq[:]),
                        inc="ss")

        def stats_evac(j=0):
            B.wait("vector", "ss", B.mark("ss"))
            B.on("vector", lambda v, j=j: v.tensor_copy(rmj(j), pSm[:]),
                 inc="sv")
            return B.on("vector", lambda v, j=j: v.tensor_copy(rqj(j), pSq[:]),
                        inc="sv")

        def bcast(src_row, dst_tile, negate=False):
            mk = B.on("vector", lambda v: v.tensor_copy(rsrow[0:1, 0:NT], src_row),
                      inc="sv")
            B.wait("tensor", "sv", mk)
            mk2 = B.on("tensor", lambda t: t.matmul(pB[:], ones1[:],
                                                    rsrow[0:1, 0:NT],
                                                    start=True, stop=True),
                       inc="st")
            B.wait("vector", "st", mk2)
            return B.on("vector", lambda v, dst=dst_tile: v.tensor_copy(
                dst, pB[:]), inc="sv")

        mark_setup_ss = B.mark("ss")
        B.wait("tensor", "sv", mark_cwbq)
        B.wait("tensor", "ss", mark_setup_ss)
        B.wait("vector", "ss", mark_setup_ss)
        qkv_out = (q_sb, k_sb, v_sb)
        evac_marks = {}
        stA_evac = {}

        def stats_j(j):
            sl = slice(NT * j, NT * (j + 1))
            B.wait("tensor", "dX", mark_xj[j])
            B.wait("scalar", "dX", mark_xj[j])
            if j > 0:
                B.wait("tensor", "ss", stA_evac[j - 1])
            mkm = None
            for c in range(NCH):
                mkm = B.on("tensor", lambda t, c=c, sl=sl: t.matmul(
                    pSm[:], invccol[:], x_sb[c][:, sl],
                    start=(c == 0), stop=(c == NCH - 1)),
                    inc="st" if c == NCH - 1 else None) or mkm
            mks = None
            for c in range(NCH):
                if (c % 3) in sq_used:
                    B.wait("scalar", *sq_used[c % 3])
                mkv = B.on("scalar", lambda s, c=c, sl=sl: s.activation(
                    sq_slot(c), x_sb[c][:, sl], AF.Square), inc="ss")
                B.wait("tensor", "ss", mkv)
                mks = B.on("tensor", lambda t, c=c: t.matmul(
                    pSq[:], invccol[:], sq_slot(c),
                    start=(c == 0), stop=(c == NCH - 1)), inc="st")
                sq_used[c % 3] = ("st", mks)
            B.wait("scalar", "st", mks)
            B.on("scalar", lambda s, sl=sl: s.copy(rowM4[0:1, sl], pSm[:]))
            stA_evac[j] = B.on("scalar", lambda s, sl=sl: s.copy(
                rowQ4[0:1, sl], pSq[:]), inc="ss")

        def chain_wide():
            mk0 = B.on("scalar", lambda s: s.activation(
                wT0[:], rowM4[:], AF.Square), inc="ss")
            B.wait("vector", "ss", mk0)
            mkv = B.on("vector", lambda v: v.tensor_tensor(
                rowQ4[:], rowQ4[:], wT0[:], OP.subtract), inc="sv")
            B.wait("scalar", "sv", mkv)
            B.on("scalar", lambda s: s.activation(
                wT0[:], rowQ4[:], AF.Ln, bias=epsc[:], scale=1.0), inc="ss")
            mke = B.on("scalar", lambda s: s.activation(
                rowRw[:], wT0[:], AF.Exp, scale=-0.5), inc="ss")
            B.wait("vector", "ss", mke)
            B.on("vector", lambda v: v.tensor_tensor(
                mrs1row[0:1, :], rowM4[:], rowRw[:], OP.mult), inc="sv")
            B.on("vector", lambda v: v.tensor_scalar(
                oring[0:1, 0:T], rowM4[:], -1.0, None, OP.mult), inc="sv")
            mku = B.on("vector", lambda v: v.tensor_scalar(
                rowM4[:], rowQ4[:], 1.0 + EPS, EPS * EPS, OP.mult, OP.add),
                inc="sv")
            B.wait("scalar", "sv", mku)
            B.on("scalar", lambda s: s.activation(
                wT0[:], rowM4[:], AF.Ln), inc="ss")
            mk_s4 = B.on("scalar", lambda s: s.activation(
                s4hi[0:1, :], wT0[:], AF.Exp, scale=-0.5), inc="ss")
            mk_rows = B.on("scalar", lambda s: s.activation(
                rsrow[0:1, :], wT0[:], AF.Exp, scale=0.5), inc="ss")
            return mk_s4, mk_rows

        def Gbig(j):
            sl = slice(NT * j, NT * (j + 1))
            B.wait("tensor", "dX", mark_xj[j])
            for oi, psum in enumerate((pQ, pK, pV)):
                osl = slice(128 * oi, 128 * (oi + 1))
                if (j - 1, oi) in evac_marks:
                    B.wait("tensor", "sv", evac_marks[(j - 1, oi)])
                for c in range(NCH):
                    B.on("tensor", lambda t, c=c, osl=osl, sl=sl, psum=psum:
                         t.matmul(psum[:],
                                  Wk_sb[:, 384 * c + osl.start:384 * c + osl.stop],
                                  x_sb[c][:, sl], start=(c == 0), stop=False))

        def sbcast(j, mk_s4=None):
            if mk_s4 is not None:
                B.wait("tensor", "ss", mk_s4)
            if last_bcastA[0] is not None:
                B.wait("tensor", "sv", last_bcastA[0])
            mkb = B.on("tensor", lambda t, j=j: t.matmul(
                pB[:], ones1[:], s4hi[0:1, NT * j:NT * (j + 1)],
                start=True, stop=True), inc="st")
            B.wait("vector", "st", mkb)
            last_bcastA[0] = B.on("vector", lambda v, j=j: v.tensor_copy(
                trt(j % 2), pB[:]), inc="sv")

        def Gtail(j, mk_rows=None):
            sl = slice(NT * j, NT * (j + 1))
            jj4 = NT * j
            if mk_rows is not None:
                B.wait("tensor", "ss", mk_rows)
            for oi, (psum, sbout) in enumerate(zip((pQ, pK, pV), qkv_out)):
                osl = slice(128 * oi, 128 * (oi + 1))
                B.on("tensor", lambda t, osl=osl, jj4=jj4, psum=psum: t.matmul(
                    psum[:], attr[0:1, 384 + osl.start:384 + osl.stop], oring[0:1, jj4:jj4 + NT],
                    start=False, stop=False))
                mkq = B.on("tensor", lambda t, osl=osl, jj4=jj4, psum=psum:
                           t.matmul(psum[:], attr[0:1, osl.start:osl.stop],
                                    rsrow[0:1, jj4:jj4 + NT],
                                    start=False, stop=True), inc="st")
                B.wait("vector", "st", mkq)
                evac_marks[(j, oi)] = B.on(
                    "vector", lambda v, sbout=sbout, sl=sl, psum=psum, j=j:
                    v.tensor_tensor(sbout[:, sl], psum[:], trt(j % 2), OP.mult),
                    inc="sv")

        last_bcastA = [None]
        stats_j(0)
        Gbig(0)
        stats_j(1)
        stats_j(2)
        stats_j(3)
        mk_s4, mk_rows = chain_wide()
        sbcast(0, mk_s4)
        Gtail(0, mk_rows)
        sbcast(1)
        Gbig(1)
        Gtail(1)
        sbcast(2)
        Gbig(2)
        Gtail(2)
        sbcast(3)
        Gbig(3)
        Gtail(3)
        mark_qkv = B.mark("sv")
        B.wait("gpsimd", "sv", mark_qkv)
        B.dma("gpsimd", Wk_sb[:].rearrange("p (a k) -> p a k", k=C),
              WkT.rearrange("(a p) k -> p a k", p=128), "dW")
        mark_wffn = B.mark("dW")

        B.wait("vector", "sv", mark_qkv)
        B.on("vector", lambda v: v.reduce_max(kmax[:], k_sb[:], axis=AX.X,
                                              negate=True), inc="sv")
        B.wait("scalar", "sv", B.mark("sv"))
        mark_E = B.on("scalar", lambda s: s.activation(
            Eb[:], k_sb[:], AF.Exp, bias=kmax[:], scale=1.0),
            inc="ss")
        B.wait("vector", "ss", mark_E)
        B.on("vector", lambda v: v.reduce_sum(ssum[:], Eb[:], axis=AX.X),
             inc="sv")
        B.wait("tensor", "ss", mark_E)
        B.wait("tensor", "sv", mark_const)
        for src, dst in ((Eb, Etm), (v_sb, vtm)):
            for g in range(2):
                B.wait("tensor", "sv", B.mark("sv"))
                mkt = None
                for b_ in range(8):
                    blk = 8 * g + b_
                    mkt = B.on("tensor", lambda t, src=src, b_=b_, blk=blk:
                               t.transpose(pT[:, 128 * b_:128 * (b_ + 1)],
                                           src[:, 128 * blk:128 * (blk + 1)],
                                           idb[:]),
                               inc="st" if b_ == 7 else None) or mkt
                B.wait("vector", "st", mkt)
                B.on("vector", lambda v, dst=dst, g=g: v.tensor_copy(
                    dst[:, 1024 * g:1024 * (g + 1)], pT[:]), inc="sv")
        mark_tr = B.mark("sv")
        B.wait("tensor", "sv", mark_tr)
        mark_ctx = None
        for blk in range(16):
            mark_ctx = B.on("tensor", lambda t, blk=blk: t.matmul(
                pAux[:, 0:128], Etm[:, 128 * blk:128 * (blk + 1)],
                vtm[:, 128 * blk:128 * (blk + 1)],
                start=(blk == 0), stop=(blk == 15)),
                inc="st" if blk == 15 else None) or mark_ctx
        B.on("vector", lambda v: v.reciprocal(rspr[:], ssum[:]), inc="sv")
        B.wait("vector", "st", mark_ctx)
        mark_cb = B.on("vector", lambda v: v.scalar_tensor_tensor(
            cb_sb[:], pAux[:, 0:128], rspr[:], maskb[:], OP.mult, OP.mult),
            inc="sv")

        B.wait("tensor", "dW", mark_wffn)
        last_bcast = {"evac": None}

        def bcastp(src_row, dst_tile, negate=False):
            if last_bcast["evac"] is not None:
                B.wait("tensor", *last_bcast["evac"])
            mkv = bcast(src_row, dst_tile, negate)
            last_bcast["evac"] = ("sv", mkv)
            return mkv

        att_all = {}
        outh_evac = {}
        stat_free = ["ss", stA_evac[3]]
        x2_marks = {}
        w_marks = {}
        mix_marks = {}
        xk_marks = {}
        o_all = {}
        out_dma_marks = {}
        m3_marks = {}
        last_fin = {"sg": 0}

        def CC1a(j):
            sl = slice(NT * j, NT * (j + 1))
            if j == 0:
                B.wait("tensor", "sv", mark_cb)
                B.wait("vector", "dX", mark_x)
            if j - 1 in outh_evac:
                B.wait("tensor", "sv", outh_evac[j - 1])
            if last_bcast["evac"] is not None:
                B.wait("tensor", *last_bcast["evac"])
            if (j - 2, NCH - 1) in o_all:
                B.wait("tensor", "sv", o_all[(j - 2, NCH - 1)])
            mko = B.on("tensor", lambda t, sl=sl: t.matmul(
                pB[:], cb_sb[:], q_sb[:, sl], start=True, stop=True), inc="st")
            B.wait("vector", "st", mko)
            outh_evac[j] = B.on("vector", lambda v, sl=sl: v.tensor_copy(
                outh[:, sl], pB[:]), inc="sv")
            B.wait("tensor", "sv", outh_evac[j])
            B.wait("vector", "sv", last_fin["sg"])
            bcastp(rowRw[0:1, sl], trt(0))
            if j - 1 in ffn_m and 7 in ffn_m[j - 1][0]:
                B.wait("tensor", "ss", ffn_m[j - 1][0][7])
            for c in range(NCH):
                psum = (pQ, pK, pV)[c % 3]
                prev_key = (j, c - 3) if c >= 3 else (j - 1, (6, 7, 5)[c])
                if prev_key in att_all:
                    B.wait("tensor", "sv", att_all[prev_key])
                mkp = B.on("tensor", lambda t, c=c, sl=sl, psum=psum: t.matmul(
                    psum[:], woutb[:, 128 * c:128 * (c + 1)], outh[:, sl],
                    start=True, stop=True), inc="st")
                tsl = NT * (c % 2)
                B.on("vector", lambda v, c=c, sl=sl, tsl=tsl: v.tensor_tensor(
                    t5r[:, tsl:tsl + NT], x_sb[c][:, sl], trt(0), OP.mult),
                    inc="sv")
                B.wait("vector", "st", mkp)
                att_all[(j, c)] = B.on(
                    "vector", lambda v, c=c, psum=psum, tsl=tsl:
                    v.scalar_tensor_tensor(
                        attr[:, NT * c:NT * (c + 1)], psum[:],
                        bocol[:, c:c + 1], t5r[:, tsl:tsl + NT],
                        OP.add, OP.add), inc="sv")
            B.wait("tensor", *stat_free)
            m3_marks[j] = B.on("tensor", lambda t, sl=sl: t.matmul(
                pSm[:], rwcol[:], outh[:, sl], start=True, stop=True),
                inc="st")

        def CC1c(j):
            sl = slice(NT * j, NT * (j + 1))
            B.wait("tensor", *stat_free)
            for c in range(NCH):
                if (c % 3) in sq_used:
                    B.wait("scalar", *sq_used[c % 3])
                B.wait("scalar", "sv", att_all[(j, c)])
                mkaq = B.on("scalar", lambda s, c=c: s.activation(
                    sq_slot(c), attr[:, NT * c:NT * (c + 1)], AF.Square),
                    inc="ss")
                B.wait("tensor", "ss", mkaq)
                sq_used[c % 3] = ("st", B.on("tensor", lambda t, c=c: t.matmul(
                    pSq[:], invccol[:], sq_slot(c),
                    start=(c == 0), stop=(c == NCH - 1)), inc="st"))
            B.wait("vector", "st", m3_marks[j])
            B.wait("vector", "ss", B.mark("ss"))
            B.on("vector", lambda v, sl=sl: v.scalar_tensor_tensor(
                rmj(0), pSm[:], bbar[:], mrs1row[0:1, sl], OP.add, OP.add),
                inc="sv")
            stat_free[:] = ["sv", B.on("vector", lambda v: v.tensor_copy(
                rqj(0), pSq[:]), inc="sv")]
            chain_core(0)
            B.on("vector", lambda v: v.tensor_tensor(rowM4[0:1, NT:2 * NT], rmj(0), rowRw[0:1, 0:NT],
                                                     OP.mult), inc="sv")
            B.on("vector", lambda v, sl=sl: v.tensor_copy(p2row[0:1, sl],
                                                          rowM4[0:1, NT:2 * NT]), inc="sv")

        def CC1b(j):
            sl = slice(NT * j, NT * (j + 1))
            mk_r3 = bcastp(rowRw[0:1, 0:NT], trt(0))
            B.wait("vector", "sv", mk_r3)
            for c in range(NCH):
                tsl = NT * (c % 2)
                B.on("vector", lambda v, c=c, tsl=tsl: v.tensor_tensor(
                    t3s[:, tsl:tsl + NT], attr[:, NT * c:NT * (c + 1)],
                    trt(0), OP.mult), inc="sv")
                x2_marks[(j, c)] = B.on(
                    "vector", lambda v, c=c, sl=sl, tsl=tsl, j=j: v.tensor_tensor(
                        x2_(c, j), t3s[:, tsl:tsl + NT],
                        x_sb[c][:, sl], OP.add), inc="sv")

        def CC2a(j):
            sl = slice(NT * j, NT * (j + 1))
            B.wait("tensor", *stat_free)
            for c in range(NCH):
                B.wait("tensor", "sv", x2_marks[(j, c)])
                B.on("tensor", lambda t, c=c, j=j: t.matmul(
                    pSm[:], invccol[:], x2_(c, j),
                    start=(c == 0), stop=(c == NCH - 1)))
            mks4 = None
            for c in range(NCH):
                if (c % 3) in sq_used:
                    B.wait("scalar", *sq_used[c % 3])
                B.wait("scalar", "sv", x2_marks[(j, c)])
                mkv = B.on("scalar", lambda s, c=c, j=j: s.activation(
                    sq_slot(c), x2_(c, j), AF.Square), inc="ss")
                B.wait("tensor", "ss", mkv)
                mks4 = B.on("tensor", lambda t, c=c: t.matmul(
                    pSq[:], invccol[:], sq_slot(c),
                    start=(c == 0), stop=(c == NCH - 1)), inc="st")
                sq_used[c % 3] = ("st", mks4)
            B.wait("scalar", "st", mks4)
            stat_free[:] = ["ss", stats_evac_s(0)]
            chain_core(0)
            B.on("vector", lambda v: v.tensor_tensor(rowM4[0:1, NT:2 * NT], rmj(0), rowRw[0:1, 0:NT],
                                                     OP.mult), inc="sv")

        def CC2b(j):
            sl = slice(NT * j, NT * (j + 1))
            B.wait("vector", "sv", last_fin["sg"])
            bcastp(rowRw[0:1, 0:NT], trt(1))
            mk_m4 = bcastp(rowM4[0:1, NT:2 * NT], trtb4[:], negate=True)
            B.wait("vector", "sv", mk_m4)
            for c in range(NCH):
                tsl = NT * (c % 2)
                B.on("vector", lambda v, c=c, tsl=tsl, j=j: v.tensor_tensor(
                    t3s[:, tsl:tsl + NT], x2_(c, j), trt(1), OP.mult),
                    inc="sv")
                w_marks[(j, c)] = B.on(
                    "vector", lambda v, c=c, sl=sl, tsl=tsl: v.tensor_tensor(
                        w_sb[c][:, sl], t3s[:, tsl:tsl + NT], trtb4[:],
                        OP.subtract), inc="sv")
            if j - 1 in rk_last:
                B.wait("vector", "st", rk_last[j - 1])
            d_marks = {}
            for c in range(NCH):
                if j == 0:
                    B.on("vector", lambda v, c=c: v.tensor_copy(
                        attr[:, NT * c:NT * c + 1], w_sb[c][:, 0:1]))
                    d_marks[c] = B.on("vector", lambda v, c=c: v.tensor_tensor(
                        attr[:, NT * c + 1:NT * (c + 1)], w_sb[c][:, 1:NT],
                        w_sb[c][:, 0:NT - 1], OP.subtract), inc="sv")
                else:
                    d_marks[c] = B.on("vector", lambda v, c=c, j=j:
                                      v.tensor_tensor(
                                          attr[:, NT * c:NT * (c + 1)],
                                          w_sb[c][:, NT * j:NT * (j + 1)],
                                          w_sb[c][:, NT * j - 1:NT * (j + 1) - 1],
                                          OP.subtract), inc="sv")
            for c in range(NCH):
                if j == 0:
                    B.on("vector", lambda v, c=c: v.tensor_scalar(
                        xr_(c)[:, 0:1], attr[:, NT * c:NT * c + 1],
                        murcol[:, c:c + 1], None, OP.mult))
                    mix_marks[(j, c)] = B.on(
                        "vector", lambda v, c=c: v.scalar_tensor_tensor(
                            xr_(c)[:, 1:NT], attr[:, NT * c + 1:NT * (c + 1)],
                            murcol[:, c:c + 1], w_sb[c][:, 0:NT - 1],
                            OP.mult, OP.add), inc="sv")
                else:
                    mix_marks[(j, c)] = B.on(
                        "vector", lambda v, c=c, j=j: v.scalar_tensor_tensor(
                            xr_(c), attr[:, NT * c:NT * (c + 1)],
                            murcol[:, c:c + 1],
                            w_sb[c][:, NT * j - 1:NT * (j + 1) - 1],
                            OP.mult, OP.add), inc="sv")
            for c in range(NCH):
                if j == 0:
                    B.on("vector", lambda v, c=c: v.tensor_scalar(
                        xk_(c)[:, 0:1], attr[:, NT * c:NT * c + 1],
                        mukcol[:, c:c + 1], None, OP.mult))
                    xk_marks[(j, c)] = B.on(
                        "vector", lambda v, c=c: v.scalar_tensor_tensor(
                            xk_(c)[:, 1:NT], attr[:, NT * c + 1:NT * (c + 1)],
                            mukcol[:, c:c + 1], w_sb[c][:, 0:NT - 1],
                            OP.mult, OP.add), inc="sv")
                else:
                    xk_marks[(j, c)] = B.on(
                        "vector", lambda v, c=c, j=j: v.scalar_tensor_tensor(
                            xk_(c), attr[:, NT * c:NT * (c + 1)],
                            mukcol[:, c:c + 1],
                            w_sb[c][:, NT * j - 1:NT * (j + 1) - 1],
                            OP.mult, OP.add), inc="sv")

        rk_last = {}
        rk_last = {}
        ffn_m = {}

        def _rk_groups(j, g0, g1, sig_m, ka_m):
            groups = ([("R", c) for c in range(NCH)]
                      + [("K", c) for c in range(NCH)])
            for g in range(g0, g1):
                kind, c = groups[g]
                bank = (pAux, pQ, pK)[g % 3]
                if j == 0:
                    if g == 1 and (0, 6) in att_all:
                        B.wait("tensor", "sv", att_all[(0, 6)])
                    if g == 2 and (0, 7) in att_all:
                        B.wait("tensor", "sv", att_all[(0, 7)])
                if g == 8 and (j + 1, 7) in att_all:
                    B.wait("tensor", "sv", att_all[(j + 1, 7)])
                if g == 10 and (j + 1, 6) in att_all:
                    B.wait("tensor", "sv", att_all[(j + 1, 6)])
                if g >= 3:
                    pk, pc = groups[g - 3]
                    if pk == "R":
                        B.wait("tensor", "ss", sig_m[pc])
                    else:
                        B.wait("tensor", "ss", ka_m[pc])
                W = Wr_sb if kind == "R" else Wk_sb
                xin = xr_ if kind == "R" else xk_
                mk = None
                for cc in range(NCH):
                    if g == 0:
                        B.wait("tensor", "sv", mix_marks[(j, cc)])
                    elif g == 1 and cc == 0:
                        B.wait("tensor", "sv", mix_marks[(j, NCH - 1)])
                    elif g == 8:
                        B.wait("tensor", "sv", xk_marks[(j, cc)])
                    elif g == 9 and cc == 0:
                        B.wait("tensor", "sv", xk_marks[(j, NCH - 1)])
                    mk = B.on("tensor", lambda t, c=c, cc=cc, bank=bank, W=W,
                              xin=xin: t.matmul(
                                  bank[:],
                                  W[:, C * cc + 128 * c:C * cc + 128 * (c + 1)],
                                  xin(cc), start=(cc == 0),
                                  stop=(cc == NCH - 1)),
                              inc="st" if cc == NCH - 1 else None) or mk
                if kind == "R":
                    B.wait("scalar", "st", mk)
                    sig_m[c] = B.on("scalar", lambda s, c=c, bank=bank:
                                    s.activation(rr_(c), bank[:], AF.Sigmoid),
                                    inc="ss")
                else:
                    B.wait("scalar", "st", mk)
                    if c == 4:
                        B.wait("scalar", "sv",
                               att_all[(min(j + 1, NJ - 1), NCH - 1)])
                    if c == 6:
                        B.wait("scalar", "sv", last_fin["sg"])
                    if (c % 3) in sq_used:
                        B.wait("scalar", *sq_used[c % 3])
                    B.on("scalar", lambda s, c=c, bank=bank:
                         s.activation(sq_slot(c), bank[:], AF.Relu),
                         inc="ss")
                    ka_m[c] = B.on("scalar", lambda s, c=c: s.activation(
                        ka_(c), sq_slot(c), AF.Square), inc="ss")
                    sq_used[c % 3] = ("ss", ka_m[c])
                if g == 2 * NCH - 1:
                    rk_last[j] = mk

        def EEa1A(j):
            sig_m = {}
            ka_m = {}
            ffn_m[j] = (sig_m, ka_m)
            _rk_groups(j, 0, 8, sig_m, ka_m)

        def EEa1B(j):
            sig_m, ka_m = ffn_m[j]
            _rk_groups(j, 8, 16, sig_m, ka_m)

        def EEa2a(j):
            sl = slice(NT * j, NT * (j + 1))
            sig_m, ka_m = ffn_m[j]
            B.wait("tensor", "ss", ka_m[NCH - 1])
            B.wait("tensor", "ss", sig_m[NCH - 1])
            B.wait("vector", "ss", B.mark("ss"))
            for c in range(NCH):
                bV = (pV, pQ, pK)[c % 3]
                if c == 0 and (j + 1, 5) in att_all:
                    B.wait("tensor", "sv", att_all[(j + 1, 5)])
                if c >= 3:
                    B.wait("tensor", "sv", o_all[(j, c - 3)])
                mkV = None
                for cc in range(NCH):
                    mkV = B.on("tensor", lambda t, c=c, cc=cc, bV=bV: t.matmul(
                        bV[:], Wv_sb[:, C * cc + 128 * c:C * cc + 128 * (c + 1)],
                        ka_(cc), start=(cc == 0), stop=(cc == NCH - 1)),
                        inc="st" if cc == NCH - 1 else None) or mkV
                B.wait("vector", "st", mkV)
                o_all[(j, c)] = B.on("vector", lambda v, c=c, bV=bV:
                                     v.tensor_tensor(
                                         oring[:, NT * c:NT * (c + 1)], bV[:],
                                         rr_(c), OP.mult), inc="sv")
        def EEa2b(j):
            sl = slice(NT * j, NT * (j + 1))
            B.wait("tensor", *stat_free)
            for c in range(NCH):
                B.wait("tensor", "sv", o_all[(j, c)])
                B.on("tensor", lambda t, c=c: t.matmul(
                    pSm[:], invccol[:], oring[:, NT * c:NT * (c + 1)],
                    start=(c == 0), stop=(c == NCH - 1)))
            mks5 = None
            for c in range(NCH):
                if (c % 3) in sq_used:
                    B.wait("scalar", *sq_used[c % 3])
                B.wait("scalar", "sv", o_all[(j, c)])
                mkv = B.on("scalar", lambda s, c=c: s.activation(
                    sq_slot(c), oring[:, NT * c:NT * (c + 1)], AF.Square),
                    inc="ss")
                B.wait("tensor", "ss", mkv)
                mks5 = B.on("tensor", lambda t, c=c: t.matmul(
                    pSq[:], invccol[:], sq_slot(c),
                    start=(c == 0), stop=(c == NCH - 1)), inc="st")
                sq_used[c % 3] = ("st", mks5)
            B.wait("scalar", "st", mks5)
            stat_free[:] = ["ss", stats_evac_s(0)]
            chain_core(0)
            B.on("vector", lambda v: v.tensor_tensor(rowM4[0:1, NT:2 * NT], rmj(0), rowRw[0:1, 0:NT],
                                                     OP.mult), inc="sv")
            B.on("vector", lambda v, sl=sl: v.tensor_tensor(
                rowM4[0:1, NT:2 * NT], rowM4[0:1, NT:2 * NT], p2row[0:1, sl], OP.add), inc="sv")

        def EEa2c(j):
            bcastp(rowRw[0:1, 0:NT], trt(1))
            return bcastp(rowM4[0:1, NT:2 * NT], trtb4[:], negate=True)

        def EEb(j, mk_cl):
            sl = slice(NT * j, NT * (j + 1))
            B.wait("vector", "sv", mk_cl)
            last = None
            for c in range(NCH):
                slot = c % 2
                tA = NT * (2 + c % 2)
                B.on("vector", lambda v, c=c, tA=tA: v.tensor_tensor(
                    t5r[:, tA:tA + NT], oring[:, NT * c:NT * (c + 1)],
                    trt(1), OP.mult), inc="sv")
                B.on("vector", lambda v, tA=tA: v.tensor_tensor(
                    t5r[:, tA:tA + NT], t5r[:, tA:tA + NT],
                    trtb4[:], OP.subtract), inc="sv")
                if slot in out_dma_marks:
                    B.wait("vector", "dO", out_dma_marks[slot])
                last = B.on("vector", lambda v, c=c, slot=slot, tA=tA, j=j:
                            v.tensor_tensor(
                                outr[:, NT * slot:NT * (slot + 1)],
                                t5r[:, tA:tA + NT],
                                x2_(c, j), OP.add), inc="sv")
                B.wait("sync", "sv", last)
                out_dma_marks[slot] = B.dma(
                    "sync", yT[128 * c:128 * (c + 1), sl],
                    outr[:, NT * slot:NT * (slot + 1)], "dO")
            last_fin["sg"] = last

        CC1a(0)
        CC1c(0)
        CC1b(0)
        CC2a(0)
        CC2b(0)
        EEa1A(0)
        CC1a(1)
        CC1c(1)
        EEa1B(0)
        CC1b(1)
        CC2a(1)
        EEa2a(0)
        CC2b(1)
        EEa2b(0)
        EEa1A(1)
        CC1a(2)
        cl0 = EEa2c(0)
        CC1c(2)
        EEb(0, cl0)
        EEa1B(1)
        CC1b(2)
        CC2a(2)
        EEa2a(1)
        CC2b(2)
        EEa2b(1)
        EEa1A(2)
        CC1a(3)
        cl1 = EEa2c(1)
        CC1c(3)
        EEb(1, cl1)
        EEa1B(2)
        CC1b(3)
        CC2a(3)
        EEa2a(2)
        CC2b(3)
        EEa2b(2)
        EEa1A(3)
        cl2 = EEa2c(2)
        EEb(2, cl2)
        EEa1B(3)
        EEa2a(3)
        EEa2b(3)
        cl3 = EEa2c(3)
        EEb(3, cl3)

        if debug:
            B.wait("sync", "sv", B.mark("sv"))
            for c in range(NCH):
                B.dma("sync", dbg["d_x2"][128 * c:128 * (c + 1), 0:2 * NT],
                      x2_sb[c][:], "dO")
                B.dma("sync", dbg["d_w"][128 * c:128 * (c + 1), :], w_sb[c][:],
                      "dO")
            B.dma("sync", dbg["d_or"], oring[:], "dO")
            B.dma("sync", dbg["d_xr"], k_sb[:], "dO")
            B.dma("sync", dbg["d_xk"], v_sb[:], "dO")
            B.dma("sync", dbg["d_xk2"], vtm[:], "dO")
            B.dma("sync", dbg["d_rr"], Etm[:], "dO")
            B.dma("sync", dbg["d_ka"], outh[:], "dO")
        B.wait("sync", "dO", B.mark("dO"))

        block = E_(nc.Block())

        @block.sync
        def _(e):
            for fn in B.ops["sync"]:
                fn(e)

        @block.gpsimd
        def _(e):
            for fn in B.ops["gpsimd"]:
                fn(e)

        @block.vector
        def _(e):
            for fn in B.ops["vector"]:
                fn(e)

        @block.scalar
        def _(e):
            for fn in B.ops["scalar"]:
                fn(e)

        @block.tensor
        def _(e):
            for fn in B.ops["tensor"]:
                fn(e)

    return nc


_NC_CACHE = {}


def _get_nc(debug=False):
    if debug not in _NC_CACHE:
        _NC_CACHE[debug] = build_nc(debug)
    return _NC_CACHE[debug]


def _pcol(v):
    return np.ascontiguousarray(
        np.asarray(v, np.float32).ravel().reshape(8, 128).T)


def kernel(L0_attn_x, L0_attn_gamma, L0_attn_beta, L0_attn_w_qkv, L0_attn_w_out,
           L0_attn_b_out, L0_ffn_W_r, L0_ffn_W_k, L0_ffn_W_v, L0_ffn_mu_r,
           L0_ffn_mu_k, _debug=False):
    x = np.ascontiguousarray(L0_attn_x, dtype=np.float32)
    shared = {
        "w_qkv": np.ascontiguousarray(L0_attn_w_qkv, np.float32),
        "w_out": np.ascontiguousarray(L0_attn_w_out, np.float32),
        "WrT": np.ascontiguousarray(np.asarray(L0_ffn_W_r, np.float32).T),
        "WkT": np.ascontiguousarray(np.asarray(L0_ffn_W_k, np.float32).T),
        "WvT": np.ascontiguousarray(np.asarray(L0_ffn_W_v, np.float32).T),
        "vgamma": _pcol(L0_attn_gamma),
        "vbeta": _pcol(L0_attn_beta),
        "vbout": _pcol(L0_attn_b_out),
        "vmur": _pcol(L0_ffn_mu_r),
        "vmuk": _pcol(L0_ffn_mu_k),
    }
    in_maps = []
    for b in range(8):
        m = dict(shared)
        m["xT"] = np.ascontiguousarray(x[b].T)
        in_maps.append(m)
    nc = _get_nc(_debug)
    res = run_bass_kernel_spmd(nc, in_maps, core_ids=list(range(8)))
    out = np.stack([np.asarray(r["yT"]).T for r in res.results])
    if _debug:
        return out, res.results
    return out
